# version 15
# baseline (speedup 1.0000x reference)
import numpy as np
import ml_dtypes
import concourse.bass as bass
import concourse.mybir as mybir
from concourse.bass_utils import run_bass_kernel_spmd

F32 = mybir.dt.float32
BF16 = mybir.dt.bfloat16
I32 = mybir.dt.int32
AF = mybir.ActivationFunctionType
ALU = mybir.AluOpType
AX = mybir.AxisListType

P = 128
D = 1024
DC = D // P
E3 = 3072
NH = 16
HD = 64
IFF = 2816
IC = IFF // P
TILE = 512
NT = 8
TOK = TILE * NT
NCH = TILE // P
QMAX = 127.0
LN_EPS = 1e-5
N_CORES = 8
ALLC = [list(range(N_CORES))]

ENGS = ("sync", "scalar", "vector", "tensor", "gpsimd")
DEBUG = False


class Prog:

    def __init__(self, nc, sems):
        self.nc = nc
        self.ops = {e: [] for e in ENGS}
        self.cnt = {s: 0 for s in sems}
        self.h = {s: nc.alloc_semaphore(s) for s in sems}

    def do(self, eng, fn, sem=None, n=1):
        if sem is None:
            self.ops[eng].append(fn)
            return None
        self.cnt[sem] += n
        h = self.h[sem]
        self.ops[eng].append(lambda E, fn=fn, h=h, n=n: fn(E).then_inc(h, n))
        return self.cnt[sem]

    def dma(self, eng, out, in_, sem, **kw):
        return self.do(
            eng,
            lambda E, o=out, i=in_, kw=kw: E.dma_start(out=o, in_=i, **kw),
            sem, 16)

    def wait(self, eng, sem, val):
        if val is None or val <= 0:
            return
        h = self.h[sem]
        self.ops[eng].append(lambda E, h=h, v=val: E.wait_ge(h, v))

    ENG_SEM = {"vector": "dve", "scalar": "act", "gpsimd": "pool",
               "tensor": "pe"}

    def fence(self, eng):
        sem = self.ENG_SEM[eng]
        self.wait(eng, sem, self.cnt[sem])

    def seq(self, eng, fn):
        return self.do(eng, fn, self.ENG_SEM[eng])

    def waitv(self, eng, sv):
        if sv is None:
            return
        if isinstance(sv, list):
            for e in sv:
                self.waitv(eng, e)
            return
        self.wait(eng, sv[0], sv[1])

    def emit(self):
        with self.nc.Block() as block:
            for eng in ENGS:
                ops = self.ops[eng]

                def runner(E, ops=ops):
                    for f in ops:
                        f(E)

                getattr(block, eng)(runner)


def build():
    nc = bass.Bass("TRN2", target_bir_lowering=False, detect_race_conditions=False)

    x_in = nc.dram_tensor("x", [TOK, D], F32, kind="ExternalInput").ap()
    wqkv_t = nc.dram_tensor("wqkv_t", [D, E3], F32, kind="ExternalInput").ap()
    wproj_t = nc.dram_tensor("wproj_t", [D, D], F32, kind="ExternalInput").ap()
    wg_nat = nc.dram_tensor("wg_nat", [IFF, D], F32, kind="ExternalInput").ap()
    wu_nat = nc.dram_tensor("wu_nat", [IFF, D], F32, kind="ExternalInput").ap()
    wd_nat = nc.dram_tensor("wd_nat", [D, IFF], F32, kind="ExternalInput").ap()
    bq_in = nc.dram_tensor("bq_in", [P, DC], F32, kind="ExternalInput").ap()
    bkv_in = nc.dram_tensor("bkv_in", [1, 2 * D], F32, kind="ExternalInput").ap()
    bp_in = nc.dram_tensor("bp_in", [P, DC], F32, kind="ExternalInput").ap()
    gam_in = nc.dram_tensor("gam_in", [P, DC], F32, kind="ExternalInput").ap()
    bet_in = nc.dram_tensor("bet_in", [P, DC], F32, kind="ExternalInput").ap()
    clip_in = nc.dram_tensor("clip_in", [1, 2], F32, kind="ExternalInput").ap()
    ones_rb_in = nc.dram_tensor("ones_rb_in", [1, P], BF16, kind="ExternalInput").ap()
    ones_cb_in = nc.dram_tensor("ones_cb_in", [P, 1], BF16, kind="ExternalInput").ap()
    ones_rf_in = nc.dram_tensor("ones_rf_in", [1, P], F32, kind="ExternalInput").ap()
    pmask_in = nc.dram_tensor("pmask_in", [P, N_CORES], F32, kind="ExternalInput").ap()
    em2p_in = nc.dram_tensor("em2p_in", [P, 2], BF16, kind="ExternalInput").ap()
    em2h_in = nc.dram_tensor("em2h_in", [2, P], BF16, kind="ExternalInput").ap()
    sel2_in = nc.dram_tensor("sel2_in", [P, 2], F32, kind="ExternalInput").ap()

    y_out = nc.dram_tensor("y", [TOK, D], F32, kind="ExternalOutput").ap()

    xn_tm_sp = nc.dram_tensor("xn_tm_sp", [TOK, D], BF16).ap()
    xnf_sp = nc.dram_tensor("xnf_sp", [NT, P, DC * TILE], BF16).ap()
    q_sp = nc.dram_tensor("q_sp", [P, NT, DC * TILE], BF16).ap()
    h_sp = nc.dram_tensor("h_sp", [NT, P, IC * TILE], BF16).ap()
    s_sp = nc.dram_tensor("s_sp", [NT, P, DC * TILE], BF16).ap()
    p_sp = nc.dram_tensor("p_sp", [D, TOK], BF16).ap()
    qg_sp = nc.dram_tensor("qg_sp", [IFF, D], BF16).ap()
    qu_sp = nc.dram_tensor("qu_sp", [IFF, D], BF16).ap()
    qd_sp = nc.dram_tensor("qd_sp", [D, IFF], BF16).ap()
    maxbounce = nc.dram_tensor("maxbounce", [1, P], F32).ap()
    kvk1_part = nc.dram_tensor("kvk1_part", [P, 8 * 65], F32).ap()
    kvk1_gth = nc.dram_tensor("kvk1_gth", [N_CORES * P, 8 * 65], F32,
                              addr_space="Shared").ap()
    mx1_part = nc.dram_tensor("mx1_part", [1, 1], F32).ap()
    mx1_red = nc.dram_tensor("mx1_red", [1, 1], F32, addr_space="Shared").ap()
    mx2_part = nc.dram_tensor("mx2_part", [1, 1], F32).ap()
    mx2_red = nc.dram_tensor("mx2_red", [1, 1], F32, addr_space="Shared").ap()
    qx_gth = nc.dram_tensor("qx_gth", [N_CORES * P, NT * DC * TILE],
                            BF16, addr_space="Shared").ap()

    sems = (["ldc", "wq", "wp", "gs", "gb", "tqg", "tqd"]
            + [f"gl{i}" for i in range(2)]
            + ["pe", "dve", "act", "pool", "cc"]
            + [f"lx{i}" for i in range(4)]
            + [f"{nm}{i}" for nm in ("sptm", "trf", "spf", "spq", "lb",
                                     "sph", "lq", "lxn", "sps", "lh", "ls",
                                     "spp", "tpt", "spy") for i in range(2)])
    p = Prog(nc, sems)

    pb = [nc.alloc_psum_tensor(f"pb{i}", [P, TILE], F32) for i in range(8)]

    import functools
    S = functools.partial(nc.alloc_sbuf_tensor, unsafe_allow_realloc=True)
    kv_bf = S("kv_bf", [P, 4, 65], BF16)
    k1m = S("k1m", [P, 4, 2], BF16)
    em2p = S("em2p", [P, 2], BF16)
    em2h = S("em2h", [2, P], BF16)
    sel2 = S("sel2", [P, 2], F32)
    ones_rb = S("ones_rb", [1, P], BF16)
    ones_cb = S("ones_cb", [P, 1], BF16)
    ones_rf = S("ones_rf", [1, P], F32)
    pmask = S("pmask", [P, N_CORES], F32)
    bq_sb = S("bq_sb", [P, DC], F32)
    bkv_sb = S("bkv_sb", [1, 2 * D], BF16)
    bp_sb = S("bp_sb", [P, DC], F32)
    gam_sb = S("gam_sb", [P, DC], F32)
    bet_sb = S("bet_sb", [P, DC], F32)
    clip_sb = S("clip_sb", [1, 2], F32)
    sg_cols = S("sg_cols", [P, IC], F32)
    su_cols = S("su_cols", [P, IC], F32)
    sd_cols = S("sd_cols", [P, DC], F32)
    cg_cols = S("cg_cols", [P, IC], F32)
    cu_cols = S("cu_cols", [P, IC], F32)
    cd_cols = S("cd_cols", [P, DC], F32)
    rx_col = S("rx_col", [P, 1], F32)
    rd_col = S("rd_col", [P, 1], F32)
    sx_col = S("sx_col", [P, 1], F32)
    sd_col = S("sd_col", [P, 1], F32)
    scal = S("scal", [1, 16], F32)
    brow = S("brow", [1, P], F32)
    zero_rb = S("zero_rb", [1, P], BF16)
    maxxn_col = S("maxxn_col", [P, 1], F32)
    maxh_col = S("maxh_col", [P, 1], F32)
    red_tmp = S("red_tmp", [P, 1], F32)

    _baseA1 = nc.sbuf_base
    xch = [S(f"xch{i}", [P, D], F32) for i in range(4)]
    xn_tm = [S(f"xn_tm{i}", [P, NCH, D], BF16) for i in range(2)]
    xn_fm = []
    for _i in range(2):
        xn_fm.append(S(f"xn_fm{_i}", [P, DC, TILE], BF16))
        S(f"xn_fm_pad{_i}", [P, 4096], mybir.dt.uint8)
    qt = [S(f"qt{i}", [P, DC, TILE], BF16) for i in range(2)]
    kt = [S(f"kt{i}", [P, NCH, D], BF16) for i in range(2)]
    vt = [S(f"vt{i}", [P, NCH, NH, 65], BF16) for i in range(2)]
    wqkv_bf = S("wqkv_bf", [P, DC, E3], BF16)
    ln_sum = S("ln_sum", [P, NCH], F32)
    ln_ssq = S("ln_ssq", [P, NCH], F32)
    ln_mu = S("ln_mu", [P, NCH], F32)
    ln_var = S("ln_var", [P, NCH], F32)
    ln_rstd = S("ln_rstd", [P, NCH], F32)
    ln_t = S("ln_t", [P, D], F32)
    dummy1 = ln_t
    dummy2 = ln_t
    ln_tmp = S("ln_tmp", [P, NCH], F32)
    phi_n = [S(f"phi_n{i}", [P, TILE], BF16) for i in range(3)]
    phi_e = [S(f"phi_e{i}", [P, TILE], BF16) for i in range(3)]
    phi_p = [S(f"phi_p{i}", [P, TILE], BF16) for i in range(3)]
    wst = [S(f"wst{j}", [P, D], F32) for j in range(2)]
    wq_f = S("wq_f", [P, D], F32)
    wq_i = S("wq_i", [P, D], I32)
    wq_b = S("wq_b", [P, D], BF16)
    wmx = S("wmx", [P, 1], F32)
    wmx2 = S("wmx2", [P, 1], F32)
    winv = S("winv", [P, 1], F32)


    for (dst, src) in [
        (bq_sb[:, :], bq_in), (bp_sb[:, :], bp_in),
        (gam_sb[:, :], gam_in), (bet_sb[:, :], bet_in), (clip_sb[:, :], clip_in),
        (ones_rb[:, :], ones_rb_in), (ones_cb[:, :], ones_cb_in),
        (ones_rf[:, :], ones_rf_in),
        (em2p[:, :], em2p_in), (em2h[:, :], em2h_in),
        (sel2[:, :], sel2_in), (pmask[:, :], pmask_in),
    ]:
        p.dma("sync", dst, src, "ldc")
    consts_ld = p.cnt["ldc"]

    p.dma("gpsimd", bkv_sb[:, :], bkv_in, "wq")
    for c in range(DC):
        p.dma("gpsimd", wqkv_bf[:, c, :], wqkv_t[c * P:(c + 1) * P, :], "wq")
    wq_ld = p.cnt["wq"]

    p.wait("vector", "ldc", consts_ld)
    p.do("vector", lambda E: E.memset(zero_rb[:, :], 0.0))
    p.do("vector", lambda E: E.memset(maxxn_col[:, :], 0.0))
    p.do("vector", lambda E: E.memset(maxh_col[:, :], 0.0))
    p.do("vector", lambda E: E.memset(vt[0][:, :, :, 64:65], 1.0))
    p.do("vector", lambda E: E.memset(vt[1][:, :, :, 64:65], 1.0), "dve")
    startup_dve = p.cnt["dve"]

    p.wait("scalar", "ldc", consts_ld)
    p.do("scalar", lambda E: E.activation(scal[0:1, 8:10], clip_sb[0:1, 0:2],
                                          AF.Exp, scale=-1.0), "act")
    sig_act = p.cnt["act"]
    p.wait("vector", "act", sig_act)
    p.seq("vector", lambda E: E.tensor_scalar(scal[0:1, 10:12],
                                              scal[0:1, 8:10],
                                              1.0, None, ALU.add))
    p.fence("vector")
    p.do("vector", lambda E: E.reciprocal(scal[0:1, 0:2], scal[0:1, 10:12]),
         "dve")
    sig_dve = p.cnt["dve"]
    if DEBUG:
        dsc1 = nc.dram_tensor("dbg_scal1", [1, 16], F32,
                              kind="ExternalOutput").ap()
        p.wait("gpsimd", "dve", sig_dve)
        p.dma("gpsimd", dsc1, scal[0:1, :], "gb")

    prep_jobs = ([("g", r) for r in range(IC)] + [("u", r) for r in range(IC)]
                 + [("d", r) for r in range(DC)])
    wst_free = [None, None]
    wqb_free = [None]
    wqf_free = [None]
    prep_slot = [0]

    def _prep_ld(nat, r, c0, c1, eng):
        sl = prep_slot[0] % 2
        prep_slot[0] += 1
        w = c1 - c0
        p.waitv(eng, wst_free[sl])
        sem = f"gl{sl}"
        ldv = p.dma(eng, wst[sl][:, 0:w], nat[r * P:(r + 1) * P, c0:c1],
                    sem)
        return sl, ldv, w, sem

    def gen_prep(job):
        kind, r = job
        nat = {"g": wg_nat, "u": wu_nat, "d": wd_nat}[kind]
        spd = {"g": qg_sp, "u": qu_sp, "d": qd_sp}[kind]
        scol = {"g": sg_cols, "u": su_cols, "d": sd_cols}[kind]
        C = D if kind in ("g", "u") else IFF
        halves = [(0, C)] if C <= D else [(0, D), (D, 2 * D), (2 * D, C)]
        single = len(halves) == 1
        ldq = "sync"
        loads = []
        for hi, (c0, c1) in enumerate(halves):
            sl, ldv, w, lsem = _prep_ld(nat, r, c0, c1, ldq)
            loads.append((sl, ldv, w, lsem))
            p.wait("vector", lsem, ldv)
            mdst = wmx if hi == 0 else wmx2
            mv = p.seq("vector", lambda E, m=mdst, w=w, sl=sl: E.reduce_max(
                m[:, :], wst[sl][:, 0:w], axis=AX.X,
                apply_absolute_value=True))
            if hi > 0:
                p.fence("vector")
                mv = p.seq("vector", lambda E: E.tensor_max(
                    wmx[:, :], wmx[:, :], wmx2[:, :]))
            if not single:
                wst_free[sl] = ("dve", mv)
        p.fence("vector")
        p.seq("vector", lambda E: E.reciprocal(winv[:, :], wmx[:, :]))
        p.do("vector", lambda E, sc=scol, r=r: E.tensor_scalar(
            sc[:, r:r + 1], wmx[:, :], 1.0 / QMAX, None, ALU.mult), "dve")
        inv_v = p.cnt["dve"]
        for hi, (c0, c1) in enumerate(halves):
            if single:
                sl, ldv, w, lsem = loads[hi]
            else:
                sl, ldv, w, lsem = _prep_ld(nat, r, c0, c1, ldq)
                p.wait("vector", lsem, ldv)
            p.waitv("vector", wqf_free[0])
            mdve = p.do("vector", lambda E, w=w, sl=sl: E.tensor_scalar(
                wq_f[:, 0:w], wst[sl][:, 0:w], winv[:, :], QMAX, ALU.mult,
                ALU.mult), "dve")
            wst_free[sl] = ("dve", mdve)
            p.wait("scalar", "dve", mdve)
            p.waitv("scalar", wqb_free[0])
            a1 = p.seq("scalar", lambda E, w=w: E.activation(
                wq_i[:, 0:w], wq_f[:, 0:w], AF.Copy))
            wqf_free[0] = ("act", a1)
            p.fence("scalar")
            a2 = p.do("scalar", lambda E, w=w: E.activation(
                wq_b[:, 0:w], wq_i[:, 0:w], AF.Copy), "act")
            p.wait("gpsimd", "act", a2)
            spv = p.dma("gpsimd", spd[r * P:(r + 1) * P, c0:c1], wq_b[:, 0:w],
                        "gs")
            wqb_free[0] = ("gs", spv)

    prep_iter = iter(prep_jobs[:2 * IC])
    prep_d_iter = iter(prep_jobs[2 * IC:])

    xch_free = [None] * 4
    xn_tm_free = [None, None]
    xn_fm_free = [None, None]
    qt_free = [None, None]
    ktvt_free = [None, None]
    bank_free = [None] * 8
    phi_free = [None, None, None]
    q_spill_v = [None] * NT
    xnf_v = [None] * NT
    prev_var_dve = 0

    for i in range(NT):
        par = i % 2
        t0 = i * TILE
        ldxv = []
        for j in range(NCH):
            slot = j
            p.waitv("sync", xch_free[slot])
            v = p.dma("sync", xch[slot][:, :],
                      x_in[t0 + j * P: t0 + (j + 1) * P, :], f"lx{slot}")
            ldxv.append((slot, (f"lx{slot}", v)))
        p.fence("vector")
        p.do("vector", lambda E: E.memset(ln_sum[:, :], 0.0))
        if i > 0:
            p.wait("scalar", "dve", prev_var_dve)
        p.do("scalar", lambda E: E.memzero(ln_ssq[:, :]))
        for j in range(NCH):
            slot, v = ldxv[j]
            p.waitv("vector", v)
            p.do("vector", lambda E, s=slot, j=j: E.tensor_scalar(
                dummy1[:, :], xch[s][:, :], 1.0, 0.0, ALU.mult, ALU.add,
                accum_out=ln_sum[:, j:j + 1]), "dve")
            p.waitv("scalar", v)
            p.do("scalar", lambda E, s=slot, j=j: E.activation(
                dummy2[:, :], xch[s][:, :], AF.Square,
                accum_out=ln_ssq[:, j:j + 1]), "act")
        sums_act = p.cnt["act"]
        p.wait("vector", "act", sums_act)
        p.fence("vector")
        p.seq("vector", lambda E: E.tensor_scalar(
            ln_mu[:, :], ln_sum[:, :], 1.0 / D, None, ALU.mult))
        p.seq("vector", lambda E: E.tensor_scalar(
            ln_var[:, :], ln_ssq[:, :], 1.0 / D, LN_EPS, ALU.mult, ALU.add))
        p.fence("vector")
        p.seq("vector", lambda E: E.tensor_tensor(
            ln_tmp[:, :], ln_mu[:, :], ln_mu[:, :], ALU.mult))
        p.fence("vector")
        p.do("vector", lambda E: E.tensor_sub(
            ln_var[:, :], ln_var[:, :], ln_tmp[:, :]), "dve")
        var_dve = p.cnt["dve"]
        prev_var_dve = var_dve
        p.wait("scalar", "dve", var_dve)
        p.seq("scalar", lambda E: E.activation(
            ln_tmp[:, :], ln_var[:, :], AF.Ln))
        p.fence("scalar")
        p.do("scalar", lambda E: E.activation(
            ln_rstd[:, :], ln_tmp[:, :], AF.Exp, scale=-0.5), "act")
        rstd_act = p.cnt["act"]
        p.waitv("vector", xn_tm_free[par])
        p.wait("vector", "act", rstd_act)
        p.wait("vector", "dve", var_dve)
        for j in range(NCH):
            slot, v = ldxv[j]
            p.fence("vector")
            p.seq("vector", lambda E, s=slot, j=j: E.tensor_scalar(
                ln_t[:, :], xch[s][:, :], ln_mu[:, j:j + 1],
                ln_rstd[:, j:j + 1], ALU.subtract, ALU.mult))
            p.fence("vector")
            p.seq("vector", lambda E: E.reduce_max(
                ln_tmp[:, 0:1], ln_t[:, :], axis=AX.X,
                apply_absolute_value=True))
            p.fence("vector")
            p.seq("vector", lambda E: E.tensor_max(
                maxxn_col[:, :], maxxn_col[:, :], ln_tmp[:, 0:1]))
            mdve = p.do("vector", lambda E, par=par, j=j: E.tensor_copy(
                xn_tm[par][:, j, :], ln_t[:, :]), "dve")
            xch_free[slot] = ("dve", mdve)
        xn_dve = p.cnt["dve"]

        p.wait("scalar", "dve", xn_dve)
        spv = p.dma("scalar",
                    xn_tm_sp[t0:t0 + TILE, :].rearrange("(j p) d -> p j d",
                                                        p=P),
                    xn_tm[par][:, :, :], f"sptm{par}")
        xn_tm_free[par] = (f"sptm{par}", spv)
        p.wait("sync", f"sptm{par}", spv)
        p.waitv("sync", xn_fm_free[par])
        for c in range(DC):
            trv = p.dma("sync", xn_fm[par][:, c, :],
                        xn_tm_sp[t0:t0 + TILE, c * P:(c + 1) * P],
                        f"trf{par}", transpose=True)
        p.wait("scalar", f"trf{par}", trv)
        xnf_v[i] = (f"spf{par}",
                    p.dma("scalar", xnf_sp[i, :, :], xn_fm[par][:, :, :],
                          f"spf{par}"))

        p.wait("tensor", f"trf{par}", trv)
        if i == 0:
            p.wait("tensor", "wq", wq_ld)
            p.wait("tensor", "dve", startup_dve)
        bank_i = [0]

        def next_bank():
            b = bank_i[0] % 6
            bank_i[0] += 1
            return b

        for c in range(DC):
            b = next_bank()
            pset = b % 3
            p.waitv("tensor", bank_free[b])
            for cc in range(DC):
                p.do("tensor", lambda E, b=b, c=c, cc=cc, par=par: E.matmul(
                    pb[b][:, :], wqkv_bf[:, cc, c * P:(c + 1) * P],
                    xn_fm[par][:, cc, :], start=(cc == 0),
                    stop=(cc == DC - 1)),
                    "pe" if cc == DC - 1 else None)
            pev = p.cnt["pe"]
            p.wait("vector", "pe", pev)
            p.waitv("vector", phi_free[pset])
            p.do("vector", lambda E, b=b, c=c, ps=pset: E.tensor_scalar(
                phi_n[ps][:, :], pb[b][:, :], bq_sb[:, c:c + 1], 0.0,
                ALU.add, ALU.min), "dve")
            nv = p.cnt["dve"]
            p.wait("scalar", "dve", nv)
            p.do("scalar", lambda E, ps=pset: E.activation(
                phi_e[ps][:, :], phi_n[ps][:, :], AF.Exp), "act")
            ev = p.cnt["act"]
            p.do("vector", lambda E, b=b, c=c, ps=pset: E.tensor_scalar(
                phi_p[ps][:, :], pb[b][:, :], bq_sb[:, c:c + 1], 0.0,
                ALU.add, ALU.max), "dve")
            pv = p.cnt["dve"]
            bank_free[b] = ("dve", pv)
            p.wait("gpsimd", "act", ev)
            p.wait("gpsimd", "dve", pv)
            if c == 0:
                p.waitv("gpsimd", qt_free[par])
            plv = p.do("gpsimd", lambda E, par=par, c=c, ps=pset:
                       E.tensor_tensor(qt[par][:, c, :], phi_e[ps][:, :],
                                       phi_p[ps][:, :], ALU.add), "pool")
            phi_free[pset] = ("pool", plv)
        q_pool = p.cnt["pool"]

        first_kv_evac = True
        for j in range(NCH):
            for kindv, base_e, dstt in (("k", D, kt), ("v", 2 * D, vt)):
                for eh in range(2):
                    b = next_bank()
                    pset = b % 3
                    p.waitv("tensor", bank_free[b])
                    p.do("tensor", lambda E, b=b, base_e=base_e, eh=eh:
                         E.matmul(pb[b][:, :], ones_rb[0:1, :],
                                  bkv_sb[0:1, base_e - D + eh * TILE:
                                         base_e - D + (eh + 1) * TILE],
                                  start=True, stop=False))
                    for cc in range(DC):
                        p.do("tensor", lambda E, b=b, cc=cc, j=j, par=par,
                             base_e=base_e, eh=eh: E.matmul(
                            pb[b][:, :],
                            xn_fm[par][:, cc, j * P:(j + 1) * P],
                            wqkv_bf[:, cc, base_e + eh * TILE:
                                    base_e + (eh + 1) * TILE],
                            start=False, stop=(cc == DC - 1)),
                            "pe" if cc == DC - 1 else None)
                    pev = p.cnt["pe"]
                    if kindv == "k":
                        p.wait("vector", "pe", pev)
                        p.waitv("vector", phi_free[pset])
                        p.do("vector", lambda E, b=b, ps=pset: E.tensor_scalar(
                            phi_n[ps][:, :], pb[b][:, :], 0.0, None, ALU.min),
                            "dve")
                        nv = p.cnt["dve"]
                        p.wait("scalar", "dve", nv)
                        p.do("scalar", lambda E, ps=pset: E.activation(
                            phi_e[ps][:, :], phi_n[ps][:, :], AF.Exp), "act")
                        ev = p.cnt["act"]
                        p.do("vector", lambda E, b=b, ps=pset: E.tensor_scalar(
                            phi_p[ps][:, :], pb[b][:, :], 0.0, None, ALU.max),
                            "dve")
                        pv = p.cnt["dve"]
                        bank_free[b] = ("dve", pv)
                        p.wait("gpsimd", "act", ev)
                        p.wait("gpsimd", "dve", pv)
                        if first_kv_evac:
                            p.waitv("gpsimd", ktvt_free[par])
                            first_kv_evac = False
                        plv = p.do("gpsimd",
                                   lambda E, par=par, j=j, eh=eh, ps=pset:
                                   E.tensor_tensor(
                                       kt[par][:, j,
                                               eh * TILE:(eh + 1) * TILE],
                                       phi_e[ps][:, :], phi_p[ps][:, :],
                                       ALU.add), "pool")
                        phi_free[pset] = ("pool", plv)
                    else:
                        p.wait("scalar", "pe", pev)
                        if first_kv_evac:
                            p.waitv("scalar", ktvt_free[par])
                            first_kv_evac = False
                        av = p.do("scalar", lambda E, b=b, par=par, j=j, eh=eh:
                                  E.activation(
                                      vt[par][:, j, eh * 8:(eh + 1) * 8, 0:64],
                                      pb[b][:, :], AF.Copy), "act")
                        bank_free[b] = ("act", av)
        kv_ready_pool = p.cnt["pool"]
        kv_ready_act = p.cnt["act"]

        p.wait("tensor", "pool", kv_ready_pool)
        p.wait("tensor", "act", kv_ready_act)
        if i == 0:
            p.do("tensor", lambda E: E.matmul(
                pb[6][:, :], zero_rb[0:1, :], bkv_sb[0:1, 0:TILE],
                start=True, stop=False))
            p.do("tensor", lambda E: E.matmul(
                pb[7][:, :], zero_rb[0:1, :], bkv_sb[0:1, 0:TILE],
                start=True, stop=False))
        for j in range(NCH):
            for h in range(NH):
                pr = h // 2
                bank = 6 if pr < 4 else 7
                po = 64 * (h % 2)
                fo = (pr % 4) * 65
                st = False
                sp_ = False
                lastmm = (j == NCH - 1 and h == NH - 1)
                p.do("tensor", lambda E, bank=bank, po=po, fo=fo, par=par,
                     i=i, j=j, h=h, st=st, sp_=sp_: E.matmul(
                    pb[bank][po:po + 64, fo:fo + 65],
                    kt[par][:, j, h * HD:(h + 1) * HD],
                    vt[par][:, j, h, 0:65],
                    start=st, stop=sp_).annotate(f"kv i{i} j{j} h{h}"),
                    "pe" if lastmm else None)
        if i == NT - 1:
            p.do("tensor", lambda E: E.matmul(
                pb[6][:, :], zero_rb[0:1, :], bkv_sb[0:1, 0:TILE],
                start=False, stop=True))
            p.do("tensor", lambda E: E.matmul(
                pb[7][:, :], zero_rb[0:1, :], bkv_sb[0:1, 0:TILE],
                start=False, stop=True), "pe")
        kvv = p.cnt["pe"]
        ktvt_free[par] = ("pe", kvv)
        xn_fm_free[par] = [("pe", kvv), xnf_v[i]]

        p.wait("scalar", "pool", q_pool)
        q_spill_v[i] = (f"spq{par}",
                        p.dma("scalar", q_sp[:, i, :], qt[par][:, :, :],
                              f"spq{par}"))
        qt_free[par] = q_spill_v[i]

        for _ in range(6):
            job = next(prep_iter, None)
            if job is not None:
                gen_prep(job)

    nc.sbuf_base = _baseA1
    _baseA2 = nc.sbuf_base
    kvk1_sb = S("kvk1_sb", [P, 8 * 65], F32)
    kvg = S("kvg", [P, N_CORES, 8 * 65], F32)
    kvacc = [S(f"kvacc{i}", [P, 8 * 65], F32) for i in range(2)]
    pe_a_done = p.cnt["pe"]
    p.wait("scalar", "pe", pe_a_done)
    p.do("scalar", lambda E: E.activation(
        kvk1_sb[:, 0:260], pb[6][:, 0:260], AF.Copy))
    ev = p.do("scalar", lambda E: E.activation(
        kvk1_sb[:, 260:520], pb[7][:, 0:260], AF.Copy), "act")
    bank_free[6] = ("act", ev)
    bank_free[7] = ("act", ev)
    p.wait("gpsimd", "act", ev)
    p.dma("gpsimd", kvk1_part[:, :], kvk1_sb[:, :], "gb")
    adve = p.cnt["dve"]
    p.wait("gpsimd", "dve", adve)
    mxv = p.dma("gpsimd", maxbounce[0:1, :].rearrange("o p -> p o"),
                maxxn_col[:, :], "gb")
    p.wait("sync", "gb", mxv)
    bv = p.dma("sync", brow[0:1, :], maxbounce[0:1, :], "ldc")
    p.wait("vector", "ldc", bv)
    mv = p.do("vector", lambda E: E.reduce_max(
        scal[0:1, 2:3], brow[0:1, :], axis=AX.X), "dve")
    p.wait("gpsimd", "dve", mv)
    p.dma("gpsimd", mx1_part[:, :], scal[0:1, 2:3], "gb")
    p.wait("gpsimd", "gb", p.cnt["gb"])
    p.do("gpsimd", lambda E: E.collective_compute(
        "AllGather", ALU.bypass, replica_groups=ALLC,
        ins=[kvk1_part[:, :]], outs=[kvk1_gth[:, :]]), "cc")
    ccv1 = p.do("gpsimd", lambda E: E.collective_compute(
        "AllReduce", ALU.max, replica_groups=ALLC,
        ins=[mx1_part[:, :]], outs=[mx1_red[:, :]]), "cc")
    p.wait("gpsimd", "spq0", p.cnt["spq0"])
    p.wait("gpsimd", "spq1", p.cnt["spq1"])
    ccq = p.do("gpsimd", lambda E: E.collective_compute(
        "AllGather", ALU.bypass, replica_groups=ALLC,
        ins=[q_sp.rearrange("p i f -> p (i f)")],
        outs=[qx_gth[:, :]]), "cc")
    p.wait("gpsimd", "cc", ccv1)
    p.dma("gpsimd", kvg[:, :, :],
          kvk1_gth.rearrange("(c p) f -> p c f", p=P), "gb")
    gmxv = p.dma("gpsimd", scal[0:1, 2:3], mx1_red[:, :], "gb")
    p.wait("vector", "gb", gmxv)
    p.seq("vector", lambda E: E.tensor_scalar(
        kvacc[0][:, :], kvg[:, 0, :], pmask[:, 0:1], None, ALU.mult))
    for c in range(1, N_CORES):
        a, b_ = kvacc[(c - 1) % 2], kvacc[c % 2]
        p.fence("vector")
        p.seq("vector", lambda E, a=a, b_=b_, c=c: E.scalar_tensor_tensor(
            b_[:, :], kvg[:, c, :], pmask[:, c:c + 1], a[:, :],
            ALU.mult, ALU.add))
    kvfin = kvacc[(N_CORES - 1) % 2]
    kvmy = kvacc[(N_CORES) % 2]
    p.fence("vector")
    for c in range(4):
        p.seq("vector", lambda E, c=c: E.tensor_scalar(
            kvmy[:, c * 65:(c + 1) * 65], kvfin[:, c * 65:(c + 1) * 65],
            sel2[:, 0:1], None, ALU.mult))
        p.fence("vector")
        p.seq("vector", lambda E, c=c: E.scalar_tensor_tensor(
            kvmy[:, c * 65:(c + 1) * 65],
            kvfin[:, (4 + c) * 65:(5 + c) * 65], sel2[:, 1:2],
            kvmy[:, c * 65:(c + 1) * 65], ALU.mult, ALU.add))
        p.fence("vector")
        p.do("vector", lambda E, c=c: E.tensor_copy(
            kv_bf[:, c, :], kvmy[:, c * 65:(c + 1) * 65]))
        p.do("vector", lambda E, c=c: E.tensor_scalar(
            k1m[:, c, :], em2p[:, :],
            kvmy[:, c * 65 + 64: c * 65 + 65], None, ALU.mult))
    p.wait("vector", "dve", sig_dve)
    p.seq("vector", lambda E: E.tensor_scalar(
        scal[0:1, 3:4], scal[0:1, 2:3], scal[0:1, 0:1], 1.0 / QMAX,
        ALU.mult, ALU.mult))
    p.fence("vector")
    kvdve = p.do("vector", lambda E: E.reciprocal(scal[0:1, 4:5],
                                                  scal[0:1, 3:4]), "dve")
    p.wait("tensor", "dve", kvdve)
    p.waitv("tensor", bank_free[0])
    p.do("tensor", lambda E: E.matmul(
        pb[0][:, 0:1], ones_rf[0:1, :], scal[0:1, 4:5], start=True, stop=True))
    bcv = p.do("tensor", lambda E: E.matmul(
        pb[0][:, 1:2], ones_rf[0:1, :], scal[0:1, 3:4], start=True,
        stop=True), "pe")
    p.wait("scalar", "pe", bcv)
    p.do("scalar", lambda E: E.activation(rx_col[:, :], pb[0][:, 0:1],
                                          AF.Copy))
    bco = p.do("scalar", lambda E: E.activation(sx_col[:, :], pb[0][:, 1:2],
                                                AF.Copy), "act")
    bank_free[0] = ("act", bco)
    p.wait("vector", "act", bco)
    p.fence("vector")
    bar1_kv_dve = p.cnt["dve"]
    for job in prep_iter:
        gen_prep(job)
    prep_gs_gu = p.cnt["gs"]

    p.fence("vector")
    p.do("vector", lambda E: E.tensor_scalar(
        cg_cols[:, :], sg_cols[:, :], sx_col[:, :], None, ALU.mult))
    p.do("vector", lambda E: E.tensor_scalar(
        cu_cols[:, :], su_cols[:, :], sx_col[:, :], None, ALU.mult), "dve")
    bar1_dve = p.cnt["dve"]
    if DEBUG:
        dsc2 = nc.dram_tensor("dbg_scal2", [1, 16], F32,
                              kind="ExternalOutput").ap()
        p.wait("gpsimd", "dve", bar1_dve)
        p.dma("gpsimd", dsc2, scal[0:1, :], "gb")

    nc.sbuf_base = _baseA2
    _baseB = nc.sbuf_base
    qg_bf = [S(f"qg_bf{c}", [P, IFF], BF16) for c in range(DC)]
    S("qg_pad", [P, 4096], mybir.dt.uint8)
    qu_bf = [S(f"qu_bf{c}", [P, IFF], BF16) for c in range(DC)]
    canary = S("canary", [P, 1024], F32)
    p.wait("sync", "gs", prep_gs_gu)
    p.wait("sync", "pe", pe_a_done)
    p.wait("sync", "dve", bar1_kv_dve)
    for s_ in ("sptm0", "sptm1", "spf0", "spf1", "spq0", "spq1"):
        p.wait("sync", s_, p.cnt[s_])
    for wsp, wbf in ((qg_sp, qg_bf), (qu_sp, qu_bf)):
        for c in range(DC):
            p.dma("sync", wbf[c][:, :], wsp[:, c * P:(c + 1) * P], "tqg",
                  transpose=True)
    qgu_ld = p.cnt["tqg"]
    if DEBUG:
        dbg_qgb = nc.dram_tensor("dbg_qgb", [DC * P, IFF], BF16,
                                 kind="ExternalOutput").ap()
        dbg_qub = nc.dram_tensor("dbg_qub", [DC * P, IFF], BF16,
                                 kind="ExternalOutput").ap()
        p.wait("gpsimd", "tqg", qgu_ld)
        for c in range(DC):
            p.dma("gpsimd", dbg_qgb[c * P:(c + 1) * P, :], qg_bf[c][:, :],
                  "gb")
            p.dma("gpsimd", dbg_qub[c * P:(c + 1) * P, :], qu_bf[c][:, :],
                  "gb")
        qgb_dump = p.cnt["gb"]
        p.wait("tensor", "gb", qgb_dump)
    p.wait("vector", "dve", bar1_dve)
    p.do("vector", lambda E: E.memset(canary[:, :], 1234.5), "dve")
    canary_set = p.cnt["dve"]

    hb = [S("hb0", [P, IC, TILE], BF16)]
    xnb = [S(f"xnb{i}", [P, DC * TILE], BF16) for i in range(2)]
    ixb = [S(f"ixb{j}", [P, DC, TILE], BF16) for j in range(2)]
    qf_scr = [S(f"qf_scr{j}", [P, TILE], F32) for j in range(2)]
    qi_scr = [S(f"qi_scr{j}", [P, TILE], I32) for j in range(2)]
    sg_scr = S("sg_scr", [P, TILE], F32)
    su_scr = S("su_scr", [P, TILE], F32)
    hf_scr = S("hf_scr", [P, TILE], F32)

    xnb_free = [None, None]
    hb_free = [None]
    hsp_v = [None] * NT
    scr_free = None
    pe_up_last = None
    qfB_free = [None, None]
    qiB_free = [None, None]
    ixb_guard = [None, None]
    if DEBUG:
        dbg_ix = nc.dram_tensor("dbg_ix", [NT, P, DC * TILE], BF16,
                                kind="ExternalOutput").ap()
        dbg_xnb = nc.dram_tensor("dbg_xnb", [NT, P, DC * TILE], BF16,
                                 kind="ExternalOutput").ap()
        dbg_ix_done = None

    for i in range(NT):
        par = i % 2
        p.waitv("sync", xnb_free[par])
        p.waitv("sync", xnf_v[i])
        xv = p.dma("sync", xnb[par][:, :], xnf_sp[i, :, :], f"lb{par}")
        if DEBUG:
            p.wait("gpsimd", f"lb{par}", xv)
            xnbd = p.dma("gpsimd", dbg_xnb[i, :, :], xnb[par][:, :], "gb")
        p.wait("vector", f"lb{par}", xv)
        if DEBUG:
            p.wait("vector", "gb", xnbd)
        for c in range(DC):
            s = c % 2
            p.waitv("vector", qfB_free[s])
            mv = p.do("vector", lambda E, par=par, c=c, s=s: E.tensor_scalar(
                qf_scr[s][:, :], xnb[par][:, c * TILE:(c + 1) * TILE],
                rx_col[:, :], None, ALU.mult), "dve")
            p.wait("scalar", "dve", mv)
            p.waitv("scalar", qiB_free[s])
            av = p.do("scalar", lambda E, s=s: E.activation(
                qi_scr[s][:, :], qf_scr[s][:, :], AF.Copy), "act")
            qfB_free[s] = ("act", av)
            p.wait("gpsimd", "act", av)
            if c == 0:
                p.waitv("gpsimd", ixb_guard[par])
            cv = p.do("gpsimd", lambda E, c=c, s=s, par=par: E.tensor_scalar(
                ixb[par][:, c, :], qi_scr[s][:, :], 127.0, -128.0, ALU.min,
                ALU.max), "pool")
            qiB_free[s] = ("pool", cv)
        ix_pool = p.cnt["pool"]
        xnb_free[par] = ("dve", p.cnt["dve"])
        if DEBUG:
            dbg_ix_done = p.dma("gpsimd", dbg_ix[i, :, :],
                                ixb[par][:, :, :], "gb")
        p.wait("tensor", "pool", ix_pool)
        if DEBUG:
            p.wait("tensor", "gb", dbg_ix_done)
        if i == 0:
            p.wait("tensor", "tqg", qgu_ld)
        p.waitv("vector", hb_free[0])
        for r in range(IC):
            bg = (2 * r) % 8
            bu = (2 * r + 1) % 8
            p.waitv("tensor", bank_free[bg])
            for cc in range(DC):
                p.do("tensor", lambda E, bg=bg, r=r, cc=cc, par=par: E.matmul(
                    pb[bg][:, :], qg_bf[cc][:, r * P:(r + 1) * P],
                    ixb[par][:, cc, :], start=(cc == 0), stop=(cc == DC - 1)),
                    None)
            p.waitv("tensor", bank_free[bu])
            for cc in range(DC):
                p.do("tensor", lambda E, bu=bu, r=r, cc=cc, par=par: E.matmul(
                    pb[bu][:, :], qu_bf[cc][:, r * P:(r + 1) * P],
                    ixb[par][:, cc, :], start=(cc == 0), stop=(cc == DC - 1)),
                    "pe" if cc == DC - 1 else None)
            uv = p.cnt["pe"]
            if r == IC - 1:
                pe_up_last = uv
                ixb_guard[par] = ("pe", uv)
            p.wait("scalar", "pe", uv)
            if scr_free is not None:
                p.waitv("scalar", scr_free)
            av = p.do("scalar", lambda E, bg=bg, r=r: E.activation(
                sg_scr[:, :], pb[bg][:, :], AF.Silu,
                scale=cg_cols[:, r:r + 1]), "act")
            bank_free[bg] = ("act", av)
            p.wait("vector", "act", av)
            p.fence("vector")
            uev = p.do("vector", lambda E, bu=bu, r=r: E.tensor_scalar(
                su_scr[:, :], pb[bu][:, :], cu_cols[:, r:r + 1], None,
                ALU.mult), "dve")
            bank_free[bu] = ("dve", uev)
            p.fence("vector")
            p.seq("vector", lambda E: E.tensor_mul(
                hf_scr[:, :], sg_scr[:, :], su_scr[:, :]))
            p.fence("vector")
            p.seq("vector", lambda E: E.reduce_max(
                red_tmp[:, 0:1], hf_scr[:, :], axis=AX.X,
                apply_absolute_value=True))
            p.fence("vector")
            p.seq("vector", lambda E: E.tensor_max(
                maxh_col[:, :], maxh_col[:, :], red_tmp[:, 0:1]))
            hcv = p.do("vector", lambda E, par=par, r=r: E.tensor_copy(
                hb[0][:, r, :], hf_scr[:, :]), "dve")
            scr_free = ("dve", hcv)
        h_dve = p.cnt["dve"]
        p.wait("scalar", "dve", h_dve)
        hsp_v[i] = (f"sph{par}",
                    p.dma("scalar", h_sp[i, :, :], hb[0][:, :, :],
                          f"sph{par}"))
        hb_free[0] = hsp_v[i]

        job = next(prep_d_iter, None)
        if job is not None:
            gen_prep(job)

    for job in prep_d_iter:
        gen_prep(job)
    prep_gs = p.cnt["gs"]

    if DEBUG:
        dbg_canB = nc.dram_tensor("dbg_canB", [P, 1024], F32,
                                  kind="ExternalOutput").ap()
        p.wait("gpsimd", "sph0", p.cnt["sph0"])
        p.wait("gpsimd", "sph1", p.cnt["sph1"])
        p.dma("gpsimd", dbg_canB, canary[:, :], "gb")
    bdve = p.cnt["dve"]
    p.wait("gpsimd", "dve", bdve)
    mxv2 = p.dma("gpsimd", maxbounce[0:1, :].rearrange("o p -> p o"),
                 maxh_col[:, :], "gb")
    p.wait("sync", "gb", mxv2)
    bv2 = p.dma("sync", brow[0:1, :], maxbounce[0:1, :], "ldc")
    p.wait("vector", "ldc", bv2)
    mv2 = p.do("vector", lambda E: E.reduce_max(
        scal[0:1, 5:6], brow[0:1, :], axis=AX.X), "dve")
    p.wait("gpsimd", "dve", mv2)
    p.dma("gpsimd", mx2_part[:, :], scal[0:1, 5:6], "gb")
    p.wait("gpsimd", "gb", p.cnt["gb"])
    ccv2 = p.do("gpsimd", lambda E: E.collective_compute(
        "AllReduce", ALU.max, replica_groups=ALLC,
        ins=[mx2_part[:, :]], outs=[mx2_red[:, :]]), "cc")
    p.wait("gpsimd", "cc", ccv2)
    gmx2 = p.dma("gpsimd", scal[0:1, 5:6], mx2_red[:, :], "gb")
    p.wait("vector", "gb", gmx2)
    p.seq("vector", lambda E: E.tensor_scalar(
        scal[0:1, 6:7], scal[0:1, 5:6], scal[0:1, 1:2], 1.0 / QMAX,
        ALU.mult, ALU.mult))
    p.fence("vector")
    sdv = p.do("vector", lambda E: E.reciprocal(scal[0:1, 7:8],
                                                scal[0:1, 6:7]), "dve")
    p.wait("tensor", "dve", sdv)
    p.waitv("tensor", bank_free[0])
    p.do("tensor", lambda E: E.matmul(
        pb[0][:, 0:1], ones_rf[0:1, :], scal[0:1, 7:8], start=True,
        stop=True))
    bcv2 = p.do("tensor", lambda E: E.matmul(
        pb[0][:, 1:2], ones_rf[0:1, :], scal[0:1, 6:7], start=True,
        stop=True), "pe")
    p.wait("scalar", "pe", bcv2)
    p.do("scalar", lambda E: E.activation(rd_col[:, :], pb[0][:, 0:1],
                                          AF.Copy))
    bco2 = p.do("scalar", lambda E: E.activation(sd_col[:, :], pb[0][:, 1:2],
                                                 AF.Copy), "act")
    bank_free[0] = ("act", bco2)
    p.wait("vector", "act", bco2)
    p.fence("vector")
    p.do("vector", lambda E: E.tensor_scalar(
        cd_cols[:, :], sd_cols[:, :], sd_col[:, :], None, ALU.mult), "dve")
    bar2_dve = p.cnt["dve"]

    nc.sbuf_base = _baseB
    qd_bf = [S(f"qd_bf{c}", [P, D], BF16) for c in range(IC)]
    S("qd_pad", [P, 4096], mybir.dt.uint8)
    _baseC1 = nc.sbuf_base
    pe_b_done = p.cnt["pe"]
    p.wait("sync", "pe", pe_b_done)
    p.wait("sync", "gs", prep_gs)
    p.wait("sync", "sph0", p.cnt["sph0"])
    p.wait("sync", "sph1", p.cnt["sph1"])
    p.wait("sync", "lb0", p.cnt["lb0"])
    p.wait("sync", "lb1", p.cnt["lb1"])
    for c in range(IC):
        p.dma("sync", qd_bf[c][:, :], qd_sp[:, c * P:(c + 1) * P], "tqd",
              transpose=True)
    qd_ld = p.cnt["tqd"]

    qca = [S("qca0", [P, 2 * NT, TILE], BF16)]
    qtil = [S(f"qtil{i}", [P, 2 * NT, TILE], BF16) for i in range(2)]
    xnc = [S(f"xnc{i}", [P, DC * TILE], BF16) for i in range(2)]
    rb_f = S("rb_f", [2, TILE], F32)
    rb_f2 = S("rb_f2", [2, TILE], F32)
    rb_b = S("rb_b", [2, TILE], BF16)
    rbx = [S(f"rbx{i}", [P, TILE], BF16) for i in range(2)]
    attn_b = S("attn_b", [P, DC, TILE], BF16)
    z_b = S("z_b", [P, DC, TILE], BF16)
    zsq = S("zsq", [P, TILE], BF16)
    srow = S("srow", [1, TILE], F32)
    sqrow = S("sqrow", [1, TILE], F32)
    murow = S("murow", [1, TILE], F32)
    varrow = S("varrow", [1, TILE], F32)
    rstdrow = S("rstdrow", [1, TILE], F32)
    mu_bb = S("mu_bb", [1, TILE], BF16)
    rstd_bb = S("rstd_bb", [1, TILE], BF16)
    mub = S("mub", [P, TILE], BF16)
    rstdb = S("rstdb", [P, TILE], BF16)
    ln2_t = S("ln2_t", [P, TILE], F32)
    ln2_u = S("ln2_u", [P, TILE], F32)
    sb_t = [S(f"sb_t{i}", [P, DC, TILE], BF16) for i in range(2)]
    wproj_bf = S("wproj_bf", [P, DC, D], BF16)
    p.wait("gpsimd", "pe", pe_b_done)
    p.wait("gpsimd", "gs", prep_gs)
    p.wait("gpsimd", "sph0", p.cnt["sph0"])
    p.wait("gpsimd", "sph1", p.cnt["sph1"])
    p.wait("gpsimd", "lb0", p.cnt["lb0"])
    p.wait("gpsimd", "lb1", p.cnt["lb1"])
    for _c in range(DC):
        p.dma("gpsimd", wproj_bf[:, _c, :], wproj_t[_c * P:(_c + 1) * P, :],
              "wp")
    wp_ld = p.cnt["wp"]
    p.wait("tensor", "wp", wp_ld)

    if DEBUG:
        dbg_z = nc.dram_tensor("dbg_z", [NT, P, DC * TILE], BF16,
                               kind="ExternalOutput").ap()
        dbg_z_done = None
    xnc_free = [None, None]
    sbt_free = [None, None]
    rbx_free = [None, None]
    qca_free = [None, None]
    qtil_free = [None, None]
    ssp_v = [None] * NT
    prev_sdve = None

    regA = nc.alloc_register(mybir.EngineType.SP, "rq_ev")
    regB = nc.alloc_register(mybir.EngineType.SP, "rq_od")
    regGC = [nc.alloc_register(mybir.EngineType.SP, f"rq_gc{i}")
             for i in range(4)]
    holder = {}

    def _dynsetup(E):
        pid = E.partition_id()
        E.reg_mov(regA, pid)
        E.reg_alu(regA, regA, 6, ALU.bitwise_and)
        E.reg_alu(regA, regA, P, ALU.mult)
        E.reg_mov(regB, pid)
        E.reg_alu(regB, regB, 1, ALU.bitwise_or)
        E.reg_alu(regB, regB, P, ALU.mult)
        holder["ev"] = E.snap(regA, min_val=0, max_val=(N_CORES - 2) * P)
        holder["od"] = E.snap(regB, min_val=0, max_val=(N_CORES - 1) * P)
        for cc_ in range(4):
            rg = regGC[cc_]
            E.reg_mov(rg, pid)
            E.reg_alu(rg, rg, 1, ALU.bitwise_and)
            E.reg_alu(rg, rg, 4, ALU.mult)
            if cc_ > 0:
                E.reg_alu(rg, rg, cc_, ALU.add)
            holder[f"gc{cc_}"] = E.snap(rg, min_val=0, max_val=DC - 1)

    p.do("sync", _dynsetup)
    p.wait("sync", "cc", ccq)

    for c in range(4):
        parc = c % 2
        qci = 0
        p.waitv("sync", qca_free[qci])
        qx4 = qx_gth.rearrange("r (i g t) -> r i g t", i=NT, g=DC)
        for ki, key in enumerate(("ev", "od")):
            qcav = p.do(
                "sync",
                lambda E, ki=ki, key=key, qci=qci, c=c: E.dma_start(
                    out=qca[qci][:, ki * NT:(ki + 1) * NT, :],
                    in_=qx4[bass.ds(holder[key], P), 0:NT,
                            bass.ds(holder[f"gc{c}"], 1), :]),
                f"lq{parc}", 16)
        p.wait("vector", f"lq{parc}", qcav)
        p.wait("tensor", f"lq{parc}", qcav)
        if c == 0:
            p.wait("tensor", "dve", bar1_dve)
        p.waitv("tensor", qtil_free[parc])
        for g in range(2 * NT):
            b0 = g % 2
            p.waitv("tensor", bank_free[b0])
            qk1v = p.do("tensor", lambda E, b=b0, c=c, parc=parc, g=g:
                        E.matmul(pb[b][0:2, :], k1m[:, c, :],
                                 qca[qci][:, g, :], start=True, stop=True),
                        "pe")
            p.wait("vector", "pe", qk1v)
            p.fence("vector")
            p.seq("vector", lambda E, b=b0: E.tensor_scalar(
                rb_f[:, :], pb[b][0:2, :], 1e-6, None, ALU.max))
            p.fence("vector")
            p.seq("vector", lambda E: E.reciprocal(rb_f2[:, :], rb_f[:, :]))
            p.fence("vector")
            rbv = p.do("vector", lambda E: E.tensor_copy(rb_b[:, :],
                                                         rb_f2[:, :]), "dve")
            bank_free[b0] = ("dve", rbv)
            b1 = 4 + (g % 2)
            p.waitv("tensor", bank_free[b1])
            p.wait("tensor", "dve", rbv)
            rbm = p.do("tensor", lambda E, b=b1: E.matmul(
                pb[b][:, :], em2h[:, :], rb_b[:, :], start=True, stop=True),
                "pe")
            rset = g % 2
            p.wait("scalar", "pe", rbm)
            p.waitv("scalar", rbx_free[rset])
            av = p.do("scalar", lambda E, b=b1, rs=rset: E.activation(
                rbx[rs][:, :], pb[b][:, :], AF.Copy), "act")
            bank_free[b1] = ("act", av)
            p.wait("vector", "act", av)
            qtv = p.do("vector", lambda E, parc=parc, g=g, rs=rset:
                       E.tensor_tensor(qtil[parc][:, g, :],
                                       qca[qci][:, g, :], rbx[rs][:, :],
                                       ALU.mult), "dve")
            rbx_free[rset] = ("dve", qtv)
        qt_dve = p.cnt["dve"]
        qca_free[qci] = ("dve", qt_dve)
        p.wait("tensor", "dve", qt_dve)

        for hh in range(2):
            bh = 2 * c + hh
            par = bh % 2
            p.waitv("sync", xnc_free[par])
            p.waitv("sync", xnf_v[bh])
            xv = p.dma("sync", xnc[par][:, :], xnf_sp[bh, :, :], f"lxn{par}")
            for cc in range(DC):
                bq_ = 2 + (cc % 2)
                p.waitv("tensor", bank_free[bq_])
                for u in range(2):
                    jj = 2 * cc + u
                    p.do("tensor", lambda E, b=bq_, c=c, hh=hh, u=u, jj=jj,
                         parc=parc: E.matmul(
                        pb[b][u * 64:(u + 1) * 64, :],
                        kv_bf[hh * 64:(hh + 1) * 64, c, 0:64],
                        qtil[parc][hh * 64:(hh + 1) * 64, :, jj::16],
                        start=True, stop=True),
                        "pe" if u == 1 else None)
                qov = p.cnt["pe"]
                p.wait("vector", "pe", qov)
                p.fence("vector")
                dv = p.do("vector", lambda E, b=bq_, cc=cc: E.tensor_copy(
                    attn_b[:, cc, :], pb[b][:, :]), "dve")
                bank_free[bq_] = ("dve", dv)
            attn_dve = p.cnt["dve"]
            p.wait("tensor", "dve", attn_dve)
            for cc2 in range(DC):
                bz = 6 + (cc2 % 2)
                p.waitv("tensor", bank_free[bz])
                for cc in range(DC):
                    p.do("tensor", lambda E, b=bz, c2=cc2, cc=cc: E.matmul(
                        pb[b][:, :], wproj_bf[:, cc, c2 * P:(c2 + 1) * P],
                        attn_b[:, cc, :], start=(cc == 0),
                        stop=(cc == DC - 1)),
                        "pe" if cc == DC - 1 else None)
                zv = p.cnt["pe"]
                p.wait("vector", "pe", zv)
                if cc2 == 0 and prev_sdve is not None:
                    p.wait("vector", "dve", prev_sdve)
                if DEBUG and cc2 == 0 and dbg_z_done is not None:
                    p.wait("vector", "gb", dbg_z_done)
                av = p.do("vector", lambda E, b=bz, c2=cc2: E.tensor_scalar(
                    z_b[:, c2, :], pb[b][:, :], bp_sb[:, c2:c2 + 1], None,
                    ALU.add), "dve")
                bank_free[bz] = ("dve", av)
            z_act = p.cnt["dve"]
            bsu, bsq = (bh % 2), 2 + (bh % 2)
            p.wait("tensor", "dve", z_act)
            p.waitv("tensor", bank_free[bsu])
            for cc in range(DC):
                p.do("tensor", lambda E, b=bsu, cc=cc: E.matmul(
                    pb[b][0:1, :], ones_cb[:, :], z_b[:, cc, :],
                    start=(cc == 0), stop=(cc == DC - 1)),
                    "pe" if cc == DC - 1 else None)
            p.waitv("tensor", bank_free[bsq])
            sqmv_prev_pe = None
            p.wait("scalar", "dve", z_act)
            for cc in range(DC):
                if sqmv_prev_pe is not None:
                    p.wait("scalar", "pe", sqmv_prev_pe)
                sqv = p.do("scalar", lambda E, cc=cc: E.activation(
                    zsq[:, :], z_b[:, cc, :], AF.Square), "act")
                p.wait("tensor", "act", sqv)
                sqmv_prev_pe = p.do("tensor", lambda E, b=bsq, cc=cc:
                                    E.matmul(pb[b][0:1, :], ones_cb[:, :],
                                             zsq[:, :], start=(cc == 0),
                                             stop=(cc == DC - 1)), "pe")
            sqmv = p.cnt["pe"]
            p.wait("vector", "pe", sqmv)
            p.seq("vector", lambda E, b=bsu: E.tensor_scalar(
                murow[0:1, :], pb[b][0:1, :], 1.0 / D, None, ALU.mult))
            p.seq("vector", lambda E, b=bsq: E.tensor_scalar(
                varrow[0:1, :], pb[b][0:1, :], 1.0 / D, LN_EPS, ALU.mult,
                ALU.add))
            p.fence("vector")
            p.seq("vector", lambda E: E.tensor_tensor(
                srow[0:1, :], murow[0:1, :], murow[0:1, :], ALU.mult))
            p.fence("vector")
            muv = p.do("vector", lambda E: E.tensor_sub(
                varrow[0:1, :], varrow[0:1, :], srow[0:1, :]), "dve")
            bank_free[bsu] = ("dve", muv)
            bank_free[bsq] = ("dve", muv)
            p.wait("scalar", "dve", muv)
            p.seq("scalar", lambda E: E.activation(
                sqrow[0:1, :], varrow[0:1, :], AF.Ln))
            p.fence("scalar")
            rsv = p.do("scalar", lambda E: E.activation(
                rstdrow[0:1, :], sqrow[0:1, :], AF.Exp, scale=-0.5), "act")
            p.wait("vector", "act", rsv)
            p.fence("vector")
            p.do("vector", lambda E: E.tensor_copy(mu_bb[0:1, :],
                                                   murow[0:1, :]))
            mbv = p.do("vector", lambda E: E.tensor_copy(
                rstd_bb[0:1, :], rstdrow[0:1, :]), "dve")
            p.wait("tensor", "dve", mbv)
            p.waitv("tensor", bank_free[bsu])
            p.waitv("tensor", bank_free[bsq])
            p.do("tensor", lambda E, b=bsu: E.matmul(
                pb[b][:, :], ones_rb[0:1, :], mu_bb[0:1, :], start=True,
                stop=True))
            bb2 = p.do("tensor", lambda E, b=bsq: E.matmul(
                pb[b][:, :], ones_rb[0:1, :], rstd_bb[0:1, :], start=True,
                stop=True), "pe")
            p.wait("scalar", "pe", bb2)
            p.do("scalar", lambda E, b=bsu: E.activation(
                mub[:, :], pb[b][:, :], AF.Copy))
            bb3 = p.do("scalar", lambda E, b=bsq: E.activation(
                rstdb[:, :], pb[b][:, :], AF.Copy), "act")
            bank_free[bsu] = ("act", bb3)
            bank_free[bsq] = ("act", bb3)
            p.wait("vector", "act", bb3)
            p.wait("vector", f"lxn{par}", xv)
            p.waitv("vector", sbt_free[par])
            p.fence("vector")
            for cc in range(DC):
                p.seq("vector", lambda E, cc=cc: E.scalar_tensor_tensor(
                    ln2_t[:, :], z_b[:, cc, :], 1.0, mub[:, :], ALU.mult,
                    ALU.subtract))
                p.fence("vector")
                p.seq("vector", lambda E: E.tensor_mul(
                    ln2_t[:, :], ln2_t[:, :], rstdb[:, :]))
                p.fence("vector")
                p.seq("vector", lambda E, cc=cc: E.tensor_scalar(
                    ln2_u[:, :], ln2_t[:, :], gam_sb[:, cc:cc + 1],
                    bet_sb[:, cc:cc + 1], ALU.mult, ALU.add))
                p.fence("vector")
                p.seq("vector", lambda E, cc=cc, par=par: E.tensor_tensor(
                    sb_t[par][:, cc, :], ln2_u[:, :],
                    xnc[par][:, cc * TILE:(cc + 1) * TILE], ALU.add))
            sdve = p.cnt["dve"]
            if DEBUG:
                p.wait("gpsimd", "dve", sdve)
                dbg_z_done = p.dma("gpsimd", dbg_z[bh, :, :], z_b[:, :, :],
                                   "gb")
            prev_sdve = sdve
            xnc_free[par] = ("dve", sdve)
            p.wait("scalar", "dve", sdve)
            ssp_v[bh] = (f"sps{par}",
                         p.dma("scalar", s_sp[bh, :, :], sb_t[par][:, :, :],
                               f"sps{par}"))
            sbt_free[par] = ssp_v[bh]
        qtil_free[parc] = ("pe", p.cnt["pe"])

    nc.sbuf_base = _baseC1
    hc = [S(f"hc{i}", [P, IC * TILE], BF16) for i in range(2)]
    qf2_scr = [S("qf2_scr0", [P, TILE], F32)]
    qi2_scr = [S("qi2_scr0", [P, TILE], I32)]
    ihb = [S(f"ihb{j}", [P, IC, TILE], BF16) for j in range(2)]
    sc_t = [S(f"sc_t{i}", [P, DC * TILE], BF16) for i in range(2)]
    g_scr = S("g_scr", [P, TILE], BF16)
    p_scr = [S(f"p_scr{i}", [P, TILE], BF16) for i in range(2)]
    ptm = []
    for _i in range(2):
        ptm.append(S(f"ptm{_i}", [P, NCH, 2, TILE], BF16))
        S(f"ptm_pad{_i}", [P, 4096], mybir.dt.uint8)
    xc2 = [S(f"xc2{i}", [P, D], F32) for i in range(2)]
    yst = [S(f"yst{i}", [P, D], F32) for i in range(2)]

    hc_free = [None, None]
    sct_free = [None, None]
    pscr_free = [None, None]
    ptm_free = [None, None]
    xc2_free = [None] * 2
    yst_free = [None, None]
    gscr_free = None
    qfC_free = [None, None]
    qiC_free = [None, None]
    ihb_guard = [None, None]
    spp_rec = [None] * NT

    def c2_assembly(k):
        par = k % 2
        t0 = k * TILE
        p.waitv("sync", ptm_free[par])
        p.wait("sync", "spp0", spp_rec[k][0])
        p.wait("sync", "spp1", spp_rec[k][1])
        ptv = None
        for j in range(NCH):
            for eh in range(2):
                ptv = p.dma("sync", ptm[par][:, j, eh, :],
                            p_sp[eh * TILE:(eh + 1) * TILE,
                                 t0 + j * P: t0 + (j + 1) * P],
                            f"tpt{par}", transpose=True)
        yv = None
        for j in range(NCH):
            slot = j % 2
            p.waitv("sync", xc2_free[slot])
            xvv = p.dma("sync", xc2[slot][:, :],
                        x_in[t0 + j * P: t0 + (j + 1) * P, :], f"lx{slot}")
            yb = j % 2
            p.waitv("vector", yst_free[yb])
            p.wait("vector", f"lx{slot}", xvv)
            p.wait("vector", f"tpt{par}", ptv)
            yv = p.do("vector", lambda E, s=slot, par=par, j=j, yb=yb:
                      E.tensor_tensor(yst[yb][:, :], xc2[s][:, :],
                                      ptm[par][:, j, :, :], ALU.add), "dve")
            xc2_free[slot] = ("dve", yv)
            p.wait("scalar", "dve", yv)
            ov = p.dma("scalar", y_out[t0 + j * P: t0 + (j + 1) * P, :],
                       yst[yb][:, :], f"spy{yb}")
            yst_free[yb] = (f"spy{yb}", ov)
        ptm_free[par] = ("dve", yv)
    pe_down_last = None

    for i in range(NT):
        par = i % 2
        t0 = i * TILE
        if i == 0:
            p.wait("sync", "sps0", p.cnt["sps0"])
            p.wait("sync", "sps1", p.cnt["sps1"])
        p.waitv("sync", hc_free[par])
        p.waitv("sync", hsp_v[i])
        hv = p.dma("sync", hc[par][:, :], h_sp[i, :, :], f"lh{par}")
        p.waitv("sync", sct_free[par])
        p.waitv("sync", ssp_v[i])
        sv_ = p.dma("sync", sc_t[par][:, :], s_sp[i, :, :], f"ls{par}")
        p.wait("vector", f"lh{par}", hv)
        if i == 0:
            p.wait("vector", "dve", bar2_dve)
        for r in range(IC):
            s = 0
            p.waitv("vector", qfC_free[s])
            mv = p.do("vector", lambda E, par=par, r=r, s=s: E.tensor_scalar(
                qf2_scr[s][:, :], hc[par][:, r * TILE:(r + 1) * TILE],
                rd_col[:, :], None, ALU.mult), "dve")
            p.wait("scalar", "dve", mv)
            p.waitv("scalar", qiC_free[s])
            av = p.do("scalar", lambda E, s=s: E.activation(
                qi2_scr[s][:, :], qf2_scr[s][:, :], AF.Copy), "act")
            qfC_free[s] = ("act", av)
            p.wait("gpsimd", "act", av)
            if r == 0:
                p.waitv("gpsimd", ihb_guard[par])
            cv = p.do("gpsimd", lambda E, r=r, s=s, par=par: E.tensor_scalar(
                ihb[par][:, r, :], qi2_scr[s][:, :], 127.0, -128.0, ALU.min,
                ALU.max), "pool")
            qiC_free[s] = ("pool", cv)
        ih_pool = p.cnt["pool"]
        hc_free[par] = ("dve", p.cnt["dve"])
        p.wait("tensor", "pool", ih_pool)
        if i == 0:
            p.wait("tensor", "tqd", qd_ld)
        for c in range(DC):
            bz = 4 + (c % 4)
            p.waitv("tensor", bank_free[bz])
            for rr in range(IC):
                p.do("tensor", lambda E, b=bz, c=c, rr=rr, par=par: E.matmul(
                    pb[b][:, :], qd_bf[rr][:, c * P:(c + 1) * P],
                    ihb[par][:, rr, :], start=(rr == 0), stop=(rr == IC - 1)),
                    "pe" if rr == IC - 1 else None)
            dvn = p.cnt["pe"]
            if c == DC - 1:
                pe_down_last = dvn
                ihb_guard[par] = ("pe", dvn)
            p.wait("scalar", "pe", dvn)
            if gscr_free is not None:
                p.waitv("scalar", gscr_free)
            av = p.do("scalar", lambda E, b=bz, c=c: E.activation(
                g_scr[:, :], pb[b][:, :], AF.Silu,
                scale=cd_cols[:, c:c + 1]), "act")
            bank_free[bz] = ("act", av)
            ps = c % 2
            p.wait("vector", "act", av)
            p.wait("vector", f"ls{par}", sv_)
            p.waitv("vector", pscr_free[ps])
            pv = p.do("vector", lambda E, par=par, c=c, ps=ps: E.tensor_tensor(
                p_scr[ps][:, :], g_scr[:, :],
                sc_t[par][:, c * TILE:(c + 1) * TILE], ALU.mult), "dve")
            gscr_free = ("dve", pv)
            if c == DC - 1:
                sct_free[par] = ("dve", pv)
            p.wait("scalar", "dve", pv)
            pspv = p.dma("scalar", p_sp[c * P:(c + 1) * P, t0:t0 + TILE],
                         p_scr[ps][:, :], f"spp{ps}")
            pscr_free[ps] = (f"spp{ps}", pspv)
        spp_rec[i] = (p.cnt["spp0"], p.cnt["spp1"])
        if i > 0:
            c2_assembly(i - 1)
    c2_assembly(NT - 1)

    p.wait("scalar", "spy0", p.cnt["spy0"])
    p.wait("scalar", "spy1", p.cnt["spy1"])
    p.wait("gpsimd", "spy0", p.cnt["spy0"])
    p.wait("gpsimd", "spy1", p.cnt["spy1"])

    if DEBUG:
        dbg_can = nc.dram_tensor("dbg_canary", [P, 1024], F32,
                                 kind="ExternalOutput").ap()
        p.dma("gpsimd", dbg_can, canary[:, :], "gb")
        dbg = [
            ("qgsp", qg_sp, [IFF, D], BF16),
            ("qusp", qu_sp, [IFF, D], BF16),
            ("qdsp", qd_sp, [D, IFF], BF16),
            ("xntm", xn_tm_sp, [TOK, D], BF16),
            ("xnf", xnf_sp, [NT, P, DC * TILE], BF16),
            ("q", q_sp, [P, NT, DC * TILE], BF16),
            ("kvp", kvk1_part, [P, 8 * 65], F32),
            ("h", h_sp, [NT, P, IC * TILE], BF16),
            ("s", s_sp, [NT, P, DC * TILE], BF16),
        ]
        for name, src_, shape, dt in dbg:
            dd = nc.dram_tensor("dbg_" + name, shape, dt,
                                kind="ExternalOutput").ap()
            p.dma("gpsimd", dd, src_, "gb")
        dsc = nc.dram_tensor("dbg_scal", [1, 16], F32,
                             kind="ExternalOutput").ap()
        p.dma("gpsimd", dsc, scal[0:1, :], "gb")
        p.wait("gpsimd", "gb", p.cnt["gb"])

    p.emit()
    return nc


def _consts():
    ones_rb = np.ones((1, P), dtype=ml_dtypes.bfloat16)
    ones_cb = np.ones((P, 1), dtype=ml_dtypes.bfloat16)
    ones_rf = np.ones((1, P), dtype=np.float32)
    em2p = np.zeros((P, 2), dtype=ml_dtypes.bfloat16)
    em2h = np.zeros((2, P), dtype=ml_dtypes.bfloat16)
    for pp in range(P):
        em2p[pp, pp // 64] = 1.0
        em2h[pp // 64, pp] = 1.0
    return ones_rb, ones_cb, ones_rf, em2p, em2h


def kernel(L0_attn_H, L0_attn_W_qkv, L0_attn_b_qkv, L0_attn_W_proj,
           L0_attn_b_proj, L0_attn_gamma, L0_attn_beta, L0_ffn_clip_x,
           L0_ffn_clip_down, L0_ffn_w_gate, L0_ffn_w_up, L0_ffn_w_down):
    x = np.ascontiguousarray(np.asarray(L0_attn_H)).reshape(4 * 8192, D)
    wqkv_t = np.ascontiguousarray(np.asarray(L0_attn_W_qkv).T)
    wproj_t = np.ascontiguousarray(np.asarray(L0_attn_W_proj).T)
    wg = np.ascontiguousarray(np.asarray(L0_ffn_w_gate))
    wu = np.ascontiguousarray(np.asarray(L0_ffn_w_up))
    wd = np.ascontiguousarray(np.asarray(L0_ffn_w_down))
    bqkv = np.asarray(L0_attn_b_qkv)
    bq_cols = np.ascontiguousarray(bqkv[0:D].reshape(DC, P).T)
    bkv_row = np.ascontiguousarray(bqkv[D:3 * D].reshape(1, 2 * D))
    bp_cols = np.ascontiguousarray(np.asarray(L0_attn_b_proj).reshape(DC, P).T)
    gam_cols = np.ascontiguousarray(np.asarray(L0_attn_gamma).reshape(DC, P).T)
    bet_cols = np.ascontiguousarray(np.asarray(L0_attn_beta).reshape(DC, P).T)
    clip = np.zeros((1, 2), dtype=np.float32)
    clip[0, 0] = np.asarray(L0_ffn_clip_x).reshape(-1)[0]
    clip[0, 1] = np.asarray(L0_ffn_clip_down).reshape(-1)[0]
    ones_rb, ones_cb, ones_rf, em2p, em2h = _consts()

    nc = build()
    in_maps = []
    for c in range(N_CORES):
        b, half = c // 2, c % 2
        r0 = b * 8192 + half * TOK
        pmask = np.zeros((P, N_CORES), dtype=np.float32)
        pmask[:, 2 * b] = 1.0
        pmask[:, 2 * b + 1] = 1.0
        sel2 = np.zeros((P, 2), dtype=np.float32)
        sel2[:, half] = 1.0
        in_maps.append({
            "x": np.ascontiguousarray(x[r0:r0 + TOK, :]),
            "wqkv_t": wqkv_t, "wproj_t": wproj_t,
            "wg_nat": wg, "wu_nat": wu, "wd_nat": wd,
            "bq_in": bq_cols, "bkv_in": bkv_row, "bp_in": bp_cols,
            "gam_in": gam_cols, "bet_in": bet_cols, "clip_in": clip,
            "ones_rb_in": ones_rb, "ones_cb_in": ones_cb,
            "ones_rf_in": ones_rf, "em2p_in": em2p, "em2h_in": em2h,
            "sel2_in": sel2, "pmask_in": pmask,
        })
    res = run_bass_kernel_spmd(nc, in_maps, core_ids=list(range(N_CORES)))
    out = np.concatenate([res.results[c]["y"] for c in range(N_CORES)],
                         axis=0)
    return out.reshape(4, 8192, D)


# version 16
# speedup vs baseline: 1.0055x; 1.0055x over previous
import numpy as np
import ml_dtypes
import concourse.bass as bass
import concourse.mybir as mybir
from concourse.bass_utils import run_bass_kernel_spmd

F32 = mybir.dt.float32
BF16 = mybir.dt.bfloat16
I32 = mybir.dt.int32
AF = mybir.ActivationFunctionType
ALU = mybir.AluOpType
AX = mybir.AxisListType

P = 128
D = 1024
DC = D // P
E3 = 3072
NH = 16
HD = 64
IFF = 2816
IC = IFF // P
TILE = 512
NT = 8
TOK = TILE * NT
NCH = TILE // P
QMAX = 127.0
LN_EPS = 1e-5
N_CORES = 8
ALLC = [list(range(N_CORES))]

ENGS = ("sync", "scalar", "vector", "tensor", "gpsimd")
DEBUG = False


class Prog:

    def __init__(self, nc, sems):
        self.nc = nc
        self.ops = {e: [] for e in ENGS}
        self.cnt = {s: 0 for s in sems}
        self.h = {s: nc.alloc_semaphore(s) for s in sems}

    def do(self, eng, fn, sem=None, n=1):
        if sem is None:
            self.ops[eng].append(fn)
            return None
        self.cnt[sem] += n
        h = self.h[sem]
        self.ops[eng].append(lambda E, fn=fn, h=h, n=n: fn(E).then_inc(h, n))
        return self.cnt[sem]

    def dma(self, eng, out, in_, sem, **kw):
        return self.do(
            eng,
            lambda E, o=out, i=in_, kw=kw: E.dma_start(out=o, in_=i, **kw),
            sem, 16)

    def wait(self, eng, sem, val):
        if val is None or val <= 0:
            return
        h = self.h[sem]
        self.ops[eng].append(lambda E, h=h, v=val: E.wait_ge(h, v))

    ENG_SEM = {"vector": "dve", "scalar": "act", "gpsimd": "pool",
               "tensor": "pe"}

    def fence(self, eng):
        sem = self.ENG_SEM[eng]
        self.wait(eng, sem, self.cnt[sem])

    def seq(self, eng, fn):
        return self.do(eng, fn, self.ENG_SEM[eng])

    def waitv(self, eng, sv):
        if sv is None:
            return
        if isinstance(sv, list):
            for e in sv:
                self.waitv(eng, e)
            return
        self.wait(eng, sv[0], sv[1])

    def emit(self):
        with self.nc.Block() as block:
            for eng in ENGS:
                ops = self.ops[eng]

                def runner(E, ops=ops):
                    for f in ops:
                        f(E)

                getattr(block, eng)(runner)


def build():
    nc = bass.Bass("TRN2", target_bir_lowering=False, detect_race_conditions=False)

    x_in = nc.dram_tensor("x", [TOK, D], F32, kind="ExternalInput").ap()
    wqkv_t = nc.dram_tensor("wqkv_t", [D, E3], F32, kind="ExternalInput").ap()
    wproj_t = nc.dram_tensor("wproj_t", [D, D], F32, kind="ExternalInput").ap()
    wg_nat = nc.dram_tensor("wg_nat", [IFF, D], F32, kind="ExternalInput").ap()
    wu_nat = nc.dram_tensor("wu_nat", [IFF, D], F32, kind="ExternalInput").ap()
    wd_nat = nc.dram_tensor("wd_nat", [D, IFF], F32, kind="ExternalInput").ap()
    bq_in = nc.dram_tensor("bq_in", [P, DC], F32, kind="ExternalInput").ap()
    bkv_in = nc.dram_tensor("bkv_in", [1, 2 * D], F32, kind="ExternalInput").ap()
    bp_in = nc.dram_tensor("bp_in", [P, DC], F32, kind="ExternalInput").ap()
    gam_in = nc.dram_tensor("gam_in", [P, DC], F32, kind="ExternalInput").ap()
    bet_in = nc.dram_tensor("bet_in", [P, DC], F32, kind="ExternalInput").ap()
    clip_in = nc.dram_tensor("clip_in", [1, 2], F32, kind="ExternalInput").ap()
    ones_rb_in = nc.dram_tensor("ones_rb_in", [1, P], BF16, kind="ExternalInput").ap()
    ones_cb_in = nc.dram_tensor("ones_cb_in", [P, 1], BF16, kind="ExternalInput").ap()
    ones_rf_in = nc.dram_tensor("ones_rf_in", [1, P], F32, kind="ExternalInput").ap()
    pmask_in = nc.dram_tensor("pmask_in", [P, N_CORES], F32, kind="ExternalInput").ap()
    ident_in = nc.dram_tensor("ident_in", [P, P], F32, kind="ExternalInput").ap()
    em2p_in = nc.dram_tensor("em2p_in", [P, 2], BF16, kind="ExternalInput").ap()
    em2h_in = nc.dram_tensor("em2h_in", [2, P], BF16, kind="ExternalInput").ap()
    sel2_in = nc.dram_tensor("sel2_in", [P, 2], F32, kind="ExternalInput").ap()

    y_out = nc.dram_tensor("y", [TOK, D], F32, kind="ExternalOutput").ap()

    xn_tm_sp = nc.dram_tensor("xn_tm_sp", [TOK, D], BF16).ap()
    xnf_sp = nc.dram_tensor("xnf_sp", [NT, P, DC * TILE], BF16).ap()
    q_sp = nc.dram_tensor("q_sp", [P, NT, DC * TILE], BF16).ap()
    h_sp = nc.dram_tensor("h_sp", [NT, P, IC * TILE], BF16).ap()
    s_sp = nc.dram_tensor("s_sp", [NT, P, DC * TILE], BF16).ap()
    p_sp = nc.dram_tensor("p_sp", [D, TOK], BF16).ap()
    qg_sp = nc.dram_tensor("qg_sp", [IFF, D], BF16).ap()
    qu_sp = nc.dram_tensor("qu_sp", [IFF, D], BF16).ap()
    qd_sp = nc.dram_tensor("qd_sp", [D, IFF], BF16).ap()
    maxbounce = nc.dram_tensor("maxbounce", [1, P], F32).ap()
    kvk1_part = nc.dram_tensor("kvk1_part", [P, 8 * 65], F32).ap()
    kvk1_gth = nc.dram_tensor("kvk1_gth", [N_CORES * P, 8 * 65], F32,
                              addr_space="Shared").ap()
    mx1_part = nc.dram_tensor("mx1_part", [1, 1], F32).ap()
    mx1_red = nc.dram_tensor("mx1_red", [1, 1], F32, addr_space="Shared").ap()
    mx2_part = nc.dram_tensor("mx2_part", [1, 1], F32).ap()
    mx2_red = nc.dram_tensor("mx2_red", [1, 1], F32, addr_space="Shared").ap()
    qx_gth = nc.dram_tensor("qx_gth", [N_CORES * P, NT * DC * TILE],
                            BF16, addr_space="Shared").ap()

    sems = (["ldc", "wq", "wp", "gs", "gb", "tqg", "tqd"]
            + [f"gl{i}" for i in range(2)]
            + ["pe", "dve", "act", "pool", "cc"]
            + [f"lx{i}" for i in range(4)]
            + [f"{nm}{i}" for nm in ("sptm", "trf", "spf", "spq", "lb",
                                     "sph", "lq", "lxn", "sps", "lh", "ls",
                                     "spp", "tpt", "spy") for i in range(2)])
    p = Prog(nc, sems)

    pb = [nc.alloc_psum_tensor(f"pb{i}", [P, TILE], F32) for i in range(8)]

    import functools
    S = functools.partial(nc.alloc_sbuf_tensor, unsafe_allow_realloc=True)
    kv_bf = S("kv_bf", [P, 4, 65], BF16)
    k1m = S("k1m", [P, 4, 2], BF16)
    em2p = S("em2p", [P, 2], BF16)
    em2h = S("em2h", [2, P], BF16)
    sel2 = S("sel2", [P, 2], F32)
    ones_rb = S("ones_rb", [1, P], BF16)
    ones_cb = S("ones_cb", [P, 1], BF16)
    ones_rf = S("ones_rf", [1, P], F32)
    pmask = S("pmask", [P, N_CORES], F32)
    ident = S("ident", [P, P], F32)
    bq_sb = S("bq_sb", [P, DC], F32)
    bkv_sb = S("bkv_sb", [1, 2 * D], BF16)
    bp_sb = S("bp_sb", [P, DC], F32)
    gam_sb = S("gam_sb", [P, DC], F32)
    bet_sb = S("bet_sb", [P, DC], F32)
    clip_sb = S("clip_sb", [1, 2], F32)
    sg_cols = S("sg_cols", [P, IC], F32)
    su_cols = S("su_cols", [P, IC], F32)
    sd_cols = S("sd_cols", [P, DC], F32)
    cg_cols = S("cg_cols", [P, IC], F32)
    cu_cols = S("cu_cols", [P, IC], F32)
    cd_cols = S("cd_cols", [P, DC], F32)
    rx_col = S("rx_col", [P, 1], F32)
    rd_col = S("rd_col", [P, 1], F32)
    sx_col = S("sx_col", [P, 1], F32)
    sd_col = S("sd_col", [P, 1], F32)
    scal = S("scal", [1, 16], F32)
    brow = S("brow", [1, P], F32)
    zero_rb = S("zero_rb", [1, P], BF16)
    maxxn_col = S("maxxn_col", [P, 1], F32)
    maxh_col = S("maxh_col", [P, 1], F32)
    red_tmp = S("red_tmp", [P, 1], F32)

    _baseA1 = nc.sbuf_base
    xch = [S(f"xch{i}", [P, D], F32) for i in range(4)]
    xn_tm = [S(f"xn_tm{i}", [P, NCH, D], BF16) for i in range(2)]
    xn_fm = []
    for _i in range(2):
        xn_fm.append(S(f"xn_fm{_i}", [P, DC, TILE], BF16))
        S(f"xn_fm_pad{_i}", [P, 4096], mybir.dt.uint8)
    qt = [S(f"qt{i}", [P, DC, TILE], BF16) for i in range(2)]
    kt = [S(f"kt{i}", [P, NCH, D], BF16) for i in range(2)]
    vt = [S(f"vt{i}", [P, NCH, NH, 65], BF16) for i in range(2)]
    wqkv_bf = S("wqkv_bf", [P, DC, E3], BF16)
    ln_sum = S("ln_sum", [P, NCH], F32)
    ln_ssq = S("ln_ssq", [P, NCH], F32)
    ln_mu = S("ln_mu", [P, NCH], F32)
    ln_var = S("ln_var", [P, NCH], F32)
    ln_rstd = S("ln_rstd", [P, NCH], F32)
    ln_t = S("ln_t", [P, D], F32)
    dummy1 = ln_t
    dummy2 = ln_t
    ln_tmp = S("ln_tmp", [P, NCH], F32)
    phi_n = [S(f"phi_n{i}", [P, TILE], BF16) for i in range(3)]
    phi_e = [S(f"phi_e{i}", [P, TILE], BF16) for i in range(3)]
    phi_p = [S(f"phi_p{i}", [P, TILE], BF16) for i in range(3)]
    wst = [S(f"wst{j}", [P, D], F32) for j in range(2)]
    wq_f = S("wq_f", [P, D], F32)
    wq_i = S("wq_i", [P, D], I32)
    wq_b = S("wq_b", [P, D], BF16)
    wmx = S("wmx", [P, 1], F32)
    wmx2 = S("wmx2", [P, 1], F32)
    winv = S("winv", [P, 1], F32)


    for (dst, src) in [
        (bq_sb[:, :], bq_in), (bp_sb[:, :], bp_in),
        (gam_sb[:, :], gam_in), (bet_sb[:, :], bet_in), (clip_sb[:, :], clip_in),
        (ones_rb[:, :], ones_rb_in), (ones_cb[:, :], ones_cb_in),
        (ones_rf[:, :], ones_rf_in),
        (em2p[:, :], em2p_in), (em2h[:, :], em2h_in),
        (sel2[:, :], sel2_in), (pmask[:, :], pmask_in),
        (ident[:, :], ident_in),
    ]:
        p.dma("sync", dst, src, "ldc")
    consts_ld = p.cnt["ldc"]

    p.dma("gpsimd", bkv_sb[:, :], bkv_in, "wq")
    for c in range(DC):
        p.dma("gpsimd", wqkv_bf[:, c, :], wqkv_t[c * P:(c + 1) * P, :], "wq")
    wq_ld = p.cnt["wq"]

    p.wait("vector", "ldc", consts_ld)
    p.do("vector", lambda E: E.memset(zero_rb[:, :], 0.0))
    p.do("vector", lambda E: E.memset(maxxn_col[:, :], 0.0))
    p.do("vector", lambda E: E.memset(maxh_col[:, :], 0.0))
    p.do("vector", lambda E: E.memset(vt[0][:, :, :, 64:65], 1.0))
    p.do("vector", lambda E: E.memset(vt[1][:, :, :, 64:65], 1.0), "dve")
    startup_dve = p.cnt["dve"]

    p.wait("scalar", "ldc", consts_ld)
    p.do("scalar", lambda E: E.activation(scal[0:1, 8:10], clip_sb[0:1, 0:2],
                                          AF.Exp, scale=-1.0), "act")
    sig_act = p.cnt["act"]
    p.wait("vector", "act", sig_act)
    p.seq("vector", lambda E: E.tensor_scalar(scal[0:1, 10:12],
                                              scal[0:1, 8:10],
                                              1.0, None, ALU.add))
    p.fence("vector")
    p.do("vector", lambda E: E.reciprocal(scal[0:1, 0:2], scal[0:1, 10:12]),
         "dve")
    sig_dve = p.cnt["dve"]
    if DEBUG:
        dsc1 = nc.dram_tensor("dbg_scal1", [1, 16], F32,
                              kind="ExternalOutput").ap()
        p.wait("gpsimd", "dve", sig_dve)
        p.dma("gpsimd", dsc1, scal[0:1, :], "gb")

    prep_jobs = ([("g", r) for r in range(IC)] + [("u", r) for r in range(IC)]
                 + [("d", r) for r in range(DC)])
    wst_free = [None, None]
    wqb_free = [None]
    wqf_free = [None]
    prep_slot = [0]

    def _prep_ld(nat, r, c0, c1, eng):
        sl = prep_slot[0] % 2
        prep_slot[0] += 1
        w = c1 - c0
        p.waitv(eng, wst_free[sl])
        sem = f"gl{sl}"
        ldv = p.dma(eng, wst[sl][:, 0:w], nat[r * P:(r + 1) * P, c0:c1],
                    sem)
        return sl, ldv, w, sem

    def gen_prep(job):
        kind, r = job
        nat = {"g": wg_nat, "u": wu_nat, "d": wd_nat}[kind]
        spd = {"g": qg_sp, "u": qu_sp, "d": qd_sp}[kind]
        scol = {"g": sg_cols, "u": su_cols, "d": sd_cols}[kind]
        C = D if kind in ("g", "u") else IFF
        halves = [(0, C)] if C <= D else [(0, D), (D, 2 * D), (2 * D, C)]
        single = len(halves) == 1
        ldq = "sync"
        loads = []
        for hi, (c0, c1) in enumerate(halves):
            sl, ldv, w, lsem = _prep_ld(nat, r, c0, c1, ldq)
            loads.append((sl, ldv, w, lsem))
            p.wait("vector", lsem, ldv)
            mdst = wmx if hi == 0 else wmx2
            mv = p.seq("vector", lambda E, m=mdst, w=w, sl=sl: E.reduce_max(
                m[:, :], wst[sl][:, 0:w], axis=AX.X,
                apply_absolute_value=True))
            if hi > 0:
                p.fence("vector")
                mv = p.seq("vector", lambda E: E.tensor_max(
                    wmx[:, :], wmx[:, :], wmx2[:, :]))
            if not single:
                wst_free[sl] = ("dve", mv)
        p.fence("vector")
        p.seq("vector", lambda E: E.reciprocal(winv[:, :], wmx[:, :]))
        p.do("vector", lambda E, sc=scol, r=r: E.tensor_scalar(
            sc[:, r:r + 1], wmx[:, :], 1.0 / QMAX, None, ALU.mult), "dve")
        inv_v = p.cnt["dve"]
        for hi, (c0, c1) in enumerate(halves):
            if single:
                sl, ldv, w, lsem = loads[hi]
            else:
                sl, ldv, w, lsem = _prep_ld(nat, r, c0, c1, ldq)
                p.wait("vector", lsem, ldv)
            p.waitv("vector", wqf_free[0])
            mdve = p.do("vector", lambda E, w=w, sl=sl: E.tensor_scalar(
                wq_f[:, 0:w], wst[sl][:, 0:w], winv[:, :], QMAX, ALU.mult,
                ALU.mult), "dve")
            wst_free[sl] = ("dve", mdve)
            p.wait("scalar", "dve", mdve)
            p.waitv("scalar", wqb_free[0])
            a1 = p.seq("scalar", lambda E, w=w: E.activation(
                wq_i[:, 0:w], wq_f[:, 0:w], AF.Copy))
            wqf_free[0] = ("act", a1)
            p.fence("scalar")
            a2 = p.do("scalar", lambda E, w=w: E.activation(
                wq_b[:, 0:w], wq_i[:, 0:w], AF.Copy), "act")
            p.wait("gpsimd", "act", a2)
            spv = p.dma("gpsimd", spd[r * P:(r + 1) * P, c0:c1], wq_b[:, 0:w],
                        "gs")
            wqb_free[0] = ("gs", spv)

    prep_iter = iter(prep_jobs[:2 * IC])
    prep_d_iter = iter(prep_jobs[2 * IC:])

    xch_free = [None] * 4
    xn_tm_free = [None, None]
    xn_fm_free = [None, None]
    qt_free = [None, None]
    ktvt_free = [None, None]
    bank_free = [None] * 8
    phi_free = [None, None, None]
    q_spill_v = [None] * NT
    xnf_v = [None] * NT
    prev_var_dve = 0

    for i in range(NT):
        par = i % 2
        t0 = i * TILE
        ldxv = []
        for j in range(NCH):
            slot = j
            p.waitv("sync", xch_free[slot])
            v = p.dma("sync", xch[slot][:, :],
                      x_in[t0 + j * P: t0 + (j + 1) * P, :], f"lx{slot}")
            ldxv.append((slot, (f"lx{slot}", v)))
        p.fence("vector")
        p.do("vector", lambda E: E.memset(ln_sum[:, :], 0.0))
        if i > 0:
            p.wait("scalar", "dve", prev_var_dve)
        p.do("scalar", lambda E: E.memzero(ln_ssq[:, :]))
        for j in range(NCH):
            slot, v = ldxv[j]
            p.waitv("vector", v)
            p.do("vector", lambda E, s=slot, j=j: E.tensor_scalar(
                dummy1[:, :], xch[s][:, :], 1.0, 0.0, ALU.mult, ALU.add,
                accum_out=ln_sum[:, j:j + 1]), "dve")
            p.waitv("scalar", v)
            p.do("scalar", lambda E, s=slot, j=j: E.activation(
                dummy2[:, :], xch[s][:, :], AF.Square,
                accum_out=ln_ssq[:, j:j + 1]), "act")
        sums_act = p.cnt["act"]
        p.wait("vector", "act", sums_act)
        p.fence("vector")
        p.seq("vector", lambda E: E.tensor_scalar(
            ln_mu[:, :], ln_sum[:, :], 1.0 / D, None, ALU.mult))
        p.seq("vector", lambda E: E.tensor_scalar(
            ln_var[:, :], ln_ssq[:, :], 1.0 / D, LN_EPS, ALU.mult, ALU.add))
        p.fence("vector")
        p.seq("vector", lambda E: E.tensor_tensor(
            ln_tmp[:, :], ln_mu[:, :], ln_mu[:, :], ALU.mult))
        p.fence("vector")
        p.do("vector", lambda E: E.tensor_sub(
            ln_var[:, :], ln_var[:, :], ln_tmp[:, :]), "dve")
        var_dve = p.cnt["dve"]
        prev_var_dve = var_dve
        p.wait("scalar", "dve", var_dve)
        p.seq("scalar", lambda E: E.activation(
            ln_tmp[:, :], ln_var[:, :], AF.Ln))
        p.fence("scalar")
        p.do("scalar", lambda E: E.activation(
            ln_rstd[:, :], ln_tmp[:, :], AF.Exp, scale=-0.5), "act")
        rstd_act = p.cnt["act"]
        p.waitv("vector", xn_tm_free[par])
        p.wait("vector", "act", rstd_act)
        p.wait("vector", "dve", var_dve)
        for j in range(NCH):
            slot, v = ldxv[j]
            p.fence("vector")
            p.seq("vector", lambda E, s=slot, j=j: E.tensor_scalar(
                ln_t[:, :], xch[s][:, :], ln_mu[:, j:j + 1],
                ln_rstd[:, j:j + 1], ALU.subtract, ALU.mult))
            p.fence("vector")
            p.seq("vector", lambda E: E.reduce_max(
                ln_tmp[:, 0:1], ln_t[:, :], axis=AX.X,
                apply_absolute_value=True))
            p.fence("vector")
            p.seq("vector", lambda E: E.tensor_max(
                maxxn_col[:, :], maxxn_col[:, :], ln_tmp[:, 0:1]))
            mdve = p.do("vector", lambda E, par=par, j=j: E.tensor_copy(
                xn_tm[par][:, j, :], ln_t[:, :]), "dve")
            xch_free[slot] = ("dve", mdve)
        xn_dve = p.cnt["dve"]

        p.wait("scalar", "dve", xn_dve)
        spv = p.dma("scalar",
                    xn_tm_sp[t0:t0 + TILE, :].rearrange("(j p) d -> p j d",
                                                        p=P),
                    xn_tm[par][:, :, :], f"sptm{par}")
        xn_tm_free[par] = (f"sptm{par}", spv)
        p.wait("sync", f"sptm{par}", spv)
        p.waitv("sync", xn_fm_free[par])
        for c in range(DC):
            trv = p.dma("sync", xn_fm[par][:, c, :],
                        xn_tm_sp[t0:t0 + TILE, c * P:(c + 1) * P],
                        f"trf{par}", transpose=True)
        p.wait("scalar", f"trf{par}", trv)
        xnf_v[i] = (f"spf{par}",
                    p.dma("scalar", xnf_sp[i, :, :], xn_fm[par][:, :, :],
                          f"spf{par}"))

        p.wait("tensor", f"trf{par}", trv)
        if i == 0:
            p.wait("tensor", "wq", wq_ld)
            p.wait("tensor", "dve", startup_dve)
        bank_i = [0]

        def next_bank():
            b = bank_i[0] % 6
            bank_i[0] += 1
            return b

        for c in range(DC):
            b = next_bank()
            pset = b % 3
            p.waitv("tensor", bank_free[b])
            for cc in range(DC):
                p.do("tensor", lambda E, b=b, c=c, cc=cc, par=par: E.matmul(
                    pb[b][:, :], wqkv_bf[:, cc, c * P:(c + 1) * P],
                    xn_fm[par][:, cc, :], start=(cc == 0),
                    stop=(cc == DC - 1)),
                    "pe" if cc == DC - 1 else None)
            pev = p.cnt["pe"]
            p.wait("vector", "pe", pev)
            p.waitv("vector", phi_free[pset])
            p.do("vector", lambda E, b=b, c=c, ps=pset: E.tensor_scalar(
                phi_n[ps][:, :], pb[b][:, :], bq_sb[:, c:c + 1], 0.0,
                ALU.add, ALU.min), "dve")
            nv = p.cnt["dve"]
            p.wait("scalar", "dve", nv)
            p.do("scalar", lambda E, ps=pset: E.activation(
                phi_e[ps][:, :], phi_n[ps][:, :], AF.Exp), "act")
            ev = p.cnt["act"]
            p.do("vector", lambda E, b=b, c=c, ps=pset: E.tensor_scalar(
                phi_p[ps][:, :], pb[b][:, :], bq_sb[:, c:c + 1], 0.0,
                ALU.add, ALU.max), "dve")
            pv = p.cnt["dve"]
            bank_free[b] = ("dve", pv)
            p.wait("gpsimd", "act", ev)
            p.wait("gpsimd", "dve", pv)
            if c == 0:
                p.waitv("gpsimd", qt_free[par])
            plv = p.do("gpsimd", lambda E, par=par, c=c, ps=pset:
                       E.tensor_tensor(qt[par][:, c, :], phi_e[ps][:, :],
                                       phi_p[ps][:, :], ALU.add), "pool")
            phi_free[pset] = ("pool", plv)
        q_pool = p.cnt["pool"]

        first_kv_evac = True
        for j in range(NCH):
            for kindv, base_e, dstt in (("k", D, kt), ("v", 2 * D, vt)):
                for eh in range(2):
                    b = next_bank()
                    pset = b % 3
                    p.waitv("tensor", bank_free[b])
                    p.do("tensor", lambda E, b=b, base_e=base_e, eh=eh:
                         E.matmul(pb[b][:, :], ones_rb[0:1, :],
                                  bkv_sb[0:1, base_e - D + eh * TILE:
                                         base_e - D + (eh + 1) * TILE],
                                  start=True, stop=False))
                    for cc in range(DC):
                        p.do("tensor", lambda E, b=b, cc=cc, j=j, par=par,
                             base_e=base_e, eh=eh: E.matmul(
                            pb[b][:, :],
                            xn_fm[par][:, cc, j * P:(j + 1) * P],
                            wqkv_bf[:, cc, base_e + eh * TILE:
                                    base_e + (eh + 1) * TILE],
                            start=False, stop=(cc == DC - 1)),
                            "pe" if cc == DC - 1 else None)
                    pev = p.cnt["pe"]
                    if kindv == "k":
                        p.wait("vector", "pe", pev)
                        p.waitv("vector", phi_free[pset])
                        p.do("vector", lambda E, b=b, ps=pset: E.tensor_scalar(
                            phi_n[ps][:, :], pb[b][:, :], 0.0, None, ALU.min),
                            "dve")
                        nv = p.cnt["dve"]
                        p.wait("scalar", "dve", nv)
                        p.do("scalar", lambda E, ps=pset: E.activation(
                            phi_e[ps][:, :], phi_n[ps][:, :], AF.Exp), "act")
                        ev = p.cnt["act"]
                        p.do("vector", lambda E, b=b, ps=pset: E.tensor_scalar(
                            phi_p[ps][:, :], pb[b][:, :], 0.0, None, ALU.max),
                            "dve")
                        pv = p.cnt["dve"]
                        bank_free[b] = ("dve", pv)
                        p.wait("gpsimd", "act", ev)
                        p.wait("gpsimd", "dve", pv)
                        if first_kv_evac:
                            p.waitv("gpsimd", ktvt_free[par])
                            first_kv_evac = False
                        plv = p.do("gpsimd",
                                   lambda E, par=par, j=j, eh=eh, ps=pset:
                                   E.tensor_tensor(
                                       kt[par][:, j,
                                               eh * TILE:(eh + 1) * TILE],
                                       phi_e[ps][:, :], phi_p[ps][:, :],
                                       ALU.add), "pool")
                        phi_free[pset] = ("pool", plv)
                    else:
                        p.wait("scalar", "pe", pev)
                        if first_kv_evac:
                            p.waitv("scalar", ktvt_free[par])
                            first_kv_evac = False
                        av = p.do("scalar", lambda E, b=b, par=par, j=j, eh=eh:
                                  E.activation(
                                      vt[par][:, j, eh * 8:(eh + 1) * 8, 0:64],
                                      pb[b][:, :], AF.Copy), "act")
                        bank_free[b] = ("act", av)
        kv_ready_pool = p.cnt["pool"]
        kv_ready_act = p.cnt["act"]

        p.wait("tensor", "pool", kv_ready_pool)
        p.wait("tensor", "act", kv_ready_act)
        if i == 0:
            p.do("tensor", lambda E: E.matmul(
                pb[6][:, :], zero_rb[0:1, :], bkv_sb[0:1, 0:TILE],
                start=True, stop=False))
            p.do("tensor", lambda E: E.matmul(
                pb[7][:, :], zero_rb[0:1, :], bkv_sb[0:1, 0:TILE],
                start=True, stop=False))
        for j in range(NCH):
            for h in range(NH):
                pr = h // 2
                bank = 6 if pr < 4 else 7
                po = 64 * (h % 2)
                fo = (pr % 4) * 65
                st = False
                sp_ = False
                lastmm = (j == NCH - 1 and h == NH - 1)
                p.do("tensor", lambda E, bank=bank, po=po, fo=fo, par=par,
                     i=i, j=j, h=h, st=st, sp_=sp_: E.matmul(
                    pb[bank][po:po + 64, fo:fo + 65],
                    kt[par][:, j, h * HD:(h + 1) * HD],
                    vt[par][:, j, h, 0:65],
                    start=st, stop=sp_).annotate(f"kv i{i} j{j} h{h}"),
                    "pe" if lastmm else None)
        if i == NT - 1:
            p.do("tensor", lambda E: E.matmul(
                pb[6][:, :], zero_rb[0:1, :], bkv_sb[0:1, 0:TILE],
                start=False, stop=True))
            p.do("tensor", lambda E: E.matmul(
                pb[7][:, :], zero_rb[0:1, :], bkv_sb[0:1, 0:TILE],
                start=False, stop=True), "pe")
        kvv = p.cnt["pe"]
        ktvt_free[par] = ("pe", kvv)
        xn_fm_free[par] = [("pe", kvv), xnf_v[i]]

        p.wait("scalar", "pool", q_pool)
        q_spill_v[i] = (f"spq{par}",
                        p.dma("scalar", q_sp[:, i, :], qt[par][:, :, :],
                              f"spq{par}"))
        qt_free[par] = q_spill_v[i]

        for _ in range(6):
            job = next(prep_iter, None)
            if job is not None:
                gen_prep(job)

    nc.sbuf_base = _baseA1
    _baseA2 = nc.sbuf_base
    kvk1_sb = S("kvk1_sb", [P, 8 * 65], F32)
    kvg = S("kvg", [P, N_CORES, 8 * 65], F32)
    kvacc = [S(f"kvacc{i}", [P, 8 * 65], F32) for i in range(2)]
    pe_a_done = p.cnt["pe"]
    p.wait("scalar", "pe", pe_a_done)
    p.do("scalar", lambda E: E.activation(
        kvk1_sb[:, 0:260], pb[6][:, 0:260], AF.Copy))
    ev = p.do("scalar", lambda E: E.activation(
        kvk1_sb[:, 260:520], pb[7][:, 0:260], AF.Copy), "act")
    bank_free[6] = ("act", ev)
    bank_free[7] = ("act", ev)
    p.wait("gpsimd", "act", ev)
    p.dma("gpsimd", kvk1_part[:, :], kvk1_sb[:, :], "gb")
    adve = p.cnt["dve"]
    p.wait("gpsimd", "dve", adve)
    mxv = p.dma("gpsimd", maxbounce[0:1, :].rearrange("o p -> p o"),
                maxxn_col[:, :], "gb")
    p.wait("sync", "gb", mxv)
    bv = p.dma("sync", brow[0:1, :], maxbounce[0:1, :], "ldc")
    p.wait("vector", "ldc", bv)
    mv = p.do("vector", lambda E: E.reduce_max(
        scal[0:1, 2:3], brow[0:1, :], axis=AX.X), "dve")
    p.wait("gpsimd", "dve", mv)
    p.dma("gpsimd", mx1_part[:, :], scal[0:1, 2:3], "gb")
    p.wait("gpsimd", "gb", p.cnt["gb"])
    p.do("gpsimd", lambda E: E.collective_compute(
        "AllGather", ALU.bypass, replica_groups=ALLC,
        ins=[kvk1_part[:, :]], outs=[kvk1_gth[:, :]]), "cc")
    ccv1 = p.do("gpsimd", lambda E: E.collective_compute(
        "AllReduce", ALU.max, replica_groups=ALLC,
        ins=[mx1_part[:, :]], outs=[mx1_red[:, :]]), "cc")
    p.wait("gpsimd", "spq0", p.cnt["spq0"])
    p.wait("gpsimd", "spq1", p.cnt["spq1"])
    ccq = p.do("gpsimd", lambda E: E.collective_compute(
        "AllGather", ALU.bypass, replica_groups=ALLC,
        ins=[q_sp.rearrange("p i f -> p (i f)")],
        outs=[qx_gth[:, :]]), "cc")
    p.wait("gpsimd", "cc", ccv1)
    p.dma("gpsimd", kvg[:, :, :],
          kvk1_gth.rearrange("(c p) f -> p c f", p=P), "gb")
    gmxv = p.dma("gpsimd", scal[0:1, 2:3], mx1_red[:, :], "gb")
    p.wait("vector", "gb", gmxv)
    p.seq("vector", lambda E: E.tensor_scalar(
        kvacc[0][:, :], kvg[:, 0, :], pmask[:, 0:1], None, ALU.mult))
    for c in range(1, N_CORES):
        a, b_ = kvacc[(c - 1) % 2], kvacc[c % 2]
        p.fence("vector")
        p.seq("vector", lambda E, a=a, b_=b_, c=c: E.scalar_tensor_tensor(
            b_[:, :], kvg[:, c, :], pmask[:, c:c + 1], a[:, :],
            ALU.mult, ALU.add))
    kvfin = kvacc[(N_CORES - 1) % 2]
    kvmy = kvacc[(N_CORES) % 2]
    p.fence("vector")
    for c in range(4):
        p.seq("vector", lambda E, c=c: E.tensor_scalar(
            kvmy[:, c * 65:(c + 1) * 65], kvfin[:, c * 65:(c + 1) * 65],
            sel2[:, 0:1], None, ALU.mult))
        p.fence("vector")
        p.seq("vector", lambda E, c=c: E.scalar_tensor_tensor(
            kvmy[:, c * 65:(c + 1) * 65],
            kvfin[:, (4 + c) * 65:(5 + c) * 65], sel2[:, 1:2],
            kvmy[:, c * 65:(c + 1) * 65], ALU.mult, ALU.add))
        p.fence("vector")
        p.do("vector", lambda E, c=c: E.tensor_copy(
            kv_bf[:, c, :], kvmy[:, c * 65:(c + 1) * 65]))
        p.do("vector", lambda E, c=c: E.tensor_scalar(
            k1m[:, c, :], em2p[:, :],
            kvmy[:, c * 65 + 64: c * 65 + 65], None, ALU.mult))
    p.wait("vector", "dve", sig_dve)
    p.seq("vector", lambda E: E.tensor_scalar(
        scal[0:1, 3:4], scal[0:1, 2:3], scal[0:1, 0:1], 1.0 / QMAX,
        ALU.mult, ALU.mult))
    p.fence("vector")
    kvdve = p.do("vector", lambda E: E.reciprocal(scal[0:1, 4:5],
                                                  scal[0:1, 3:4]), "dve")
    p.wait("tensor", "dve", kvdve)
    p.waitv("tensor", bank_free[0])
    p.do("tensor", lambda E: E.matmul(
        pb[0][:, 0:1], ones_rf[0:1, :], scal[0:1, 4:5], start=True, stop=True))
    bcv = p.do("tensor", lambda E: E.matmul(
        pb[0][:, 1:2], ones_rf[0:1, :], scal[0:1, 3:4], start=True,
        stop=True), "pe")
    p.wait("scalar", "pe", bcv)
    p.do("scalar", lambda E: E.activation(rx_col[:, :], pb[0][:, 0:1],
                                          AF.Copy))
    bco = p.do("scalar", lambda E: E.activation(sx_col[:, :], pb[0][:, 1:2],
                                                AF.Copy), "act")
    bank_free[0] = ("act", bco)
    p.wait("vector", "act", bco)
    p.fence("vector")
    bar1_kv_dve = p.cnt["dve"]
    for job in prep_iter:
        gen_prep(job)
    prep_gs_gu = p.cnt["gs"]

    p.fence("vector")
    p.do("vector", lambda E: E.tensor_scalar(
        cg_cols[:, :], sg_cols[:, :], sx_col[:, :], None, ALU.mult))
    p.do("vector", lambda E: E.tensor_scalar(
        cu_cols[:, :], su_cols[:, :], sx_col[:, :], None, ALU.mult), "dve")
    bar1_dve = p.cnt["dve"]
    if DEBUG:
        dsc2 = nc.dram_tensor("dbg_scal2", [1, 16], F32,
                              kind="ExternalOutput").ap()
        p.wait("gpsimd", "dve", bar1_dve)
        p.dma("gpsimd", dsc2, scal[0:1, :], "gb")

    nc.sbuf_base = _baseA2
    _baseB = nc.sbuf_base
    qg_bf = [S(f"qg_bf{c}", [P, IFF], BF16) for c in range(DC)]
    S("qg_pad", [P, 4096], mybir.dt.uint8)
    qu_bf = [S(f"qu_bf{c}", [P, IFF], BF16) for c in range(DC)]
    canary = S("canary", [P, 1024], F32)
    p.wait("sync", "gs", prep_gs_gu)
    p.wait("sync", "pe", pe_a_done)
    p.wait("sync", "dve", bar1_kv_dve)
    for s_ in ("sptm0", "sptm1", "spf0", "spf1", "spq0", "spq1"):
        p.wait("sync", s_, p.cnt[s_])
    for wsp, wbf in ((qg_sp, qg_bf), (qu_sp, qu_bf)):
        for c in range(DC):
            p.dma("sync", wbf[c][:, :], wsp[:, c * P:(c + 1) * P], "tqg",
                  transpose=True)
    qgu_ld = p.cnt["tqg"]
    if DEBUG:
        dbg_qgb = nc.dram_tensor("dbg_qgb", [DC * P, IFF], BF16,
                                 kind="ExternalOutput").ap()
        dbg_qub = nc.dram_tensor("dbg_qub", [DC * P, IFF], BF16,
                                 kind="ExternalOutput").ap()
        p.wait("gpsimd", "tqg", qgu_ld)
        for c in range(DC):
            p.dma("gpsimd", dbg_qgb[c * P:(c + 1) * P, :], qg_bf[c][:, :],
                  "gb")
            p.dma("gpsimd", dbg_qub[c * P:(c + 1) * P, :], qu_bf[c][:, :],
                  "gb")
        qgb_dump = p.cnt["gb"]
        p.wait("tensor", "gb", qgb_dump)
    p.wait("vector", "dve", bar1_dve)
    p.do("vector", lambda E: E.memset(canary[:, :], 1234.5), "dve")
    canary_set = p.cnt["dve"]

    hb = [S("hb0", [P, IC, TILE], BF16)]
    xnb = [S(f"xnb{i}", [P, DC * TILE], BF16) for i in range(2)]
    ixb = [S(f"ixb{j}", [P, DC, TILE], BF16) for j in range(2)]
    qf_scr = [S(f"qf_scr{j}", [P, TILE], F32) for j in range(2)]
    qi_scr = [S(f"qi_scr{j}", [P, TILE], I32) for j in range(2)]
    sg_scr = S("sg_scr", [P, TILE], F32)
    su_scr = S("su_scr", [P, TILE], F32)
    hf_scr = S("hf_scr", [P, TILE], F32)

    xnb_free = [None, None]
    hb_free = [None]
    hsp_v = [None] * NT
    scr_free = None
    pe_up_last = None
    qfB_free = [None, None]
    qiB_free = [None, None]
    ixb_guard = [None, None]
    if DEBUG:
        dbg_ix = nc.dram_tensor("dbg_ix", [NT, P, DC * TILE], BF16,
                                kind="ExternalOutput").ap()
        dbg_xnb = nc.dram_tensor("dbg_xnb", [NT, P, DC * TILE], BF16,
                                 kind="ExternalOutput").ap()
        dbg_ix_done = None

    for i in range(NT):
        par = i % 2
        p.waitv("sync", xnb_free[par])
        p.waitv("sync", xnf_v[i])
        xv = p.dma("sync", xnb[par][:, :], xnf_sp[i, :, :], f"lb{par}")
        if DEBUG:
            p.wait("gpsimd", f"lb{par}", xv)
            xnbd = p.dma("gpsimd", dbg_xnb[i, :, :], xnb[par][:, :], "gb")
        p.wait("vector", f"lb{par}", xv)
        if DEBUG:
            p.wait("vector", "gb", xnbd)
        for c in range(DC):
            s = c % 2
            p.waitv("vector", qfB_free[s])
            mv = p.do("vector", lambda E, par=par, c=c, s=s: E.tensor_scalar(
                qf_scr[s][:, :], xnb[par][:, c * TILE:(c + 1) * TILE],
                rx_col[:, :], None, ALU.mult), "dve")
            p.wait("scalar", "dve", mv)
            p.waitv("scalar", qiB_free[s])
            av = p.do("scalar", lambda E, s=s: E.activation(
                qi_scr[s][:, :], qf_scr[s][:, :], AF.Copy), "act")
            qfB_free[s] = ("act", av)
            p.wait("gpsimd", "act", av)
            if c == 0:
                p.waitv("gpsimd", ixb_guard[par])
            cv = p.do("gpsimd", lambda E, c=c, s=s, par=par: E.tensor_scalar(
                ixb[par][:, c, :], qi_scr[s][:, :], 127.0, -128.0, ALU.min,
                ALU.max), "pool")
            qiB_free[s] = ("pool", cv)
        ix_pool = p.cnt["pool"]
        xnb_free[par] = ("dve", p.cnt["dve"])
        if DEBUG:
            dbg_ix_done = p.dma("gpsimd", dbg_ix[i, :, :],
                                ixb[par][:, :, :], "gb")
        p.wait("tensor", "pool", ix_pool)
        if DEBUG:
            p.wait("tensor", "gb", dbg_ix_done)
        if i == 0:
            p.wait("tensor", "tqg", qgu_ld)
        p.waitv("vector", hb_free[0])
        for r in range(IC):
            bg = (2 * r) % 8
            bu = (2 * r + 1) % 8
            p.waitv("tensor", bank_free[bg])
            for cc in range(DC):
                p.do("tensor", lambda E, bg=bg, r=r, cc=cc, par=par: E.matmul(
                    pb[bg][:, :], qg_bf[cc][:, r * P:(r + 1) * P],
                    ixb[par][:, cc, :], start=(cc == 0), stop=(cc == DC - 1)),
                    None)
            p.waitv("tensor", bank_free[bu])
            for cc in range(DC):
                p.do("tensor", lambda E, bu=bu, r=r, cc=cc, par=par: E.matmul(
                    pb[bu][:, :], qu_bf[cc][:, r * P:(r + 1) * P],
                    ixb[par][:, cc, :], start=(cc == 0), stop=(cc == DC - 1)),
                    "pe" if cc == DC - 1 else None)
            uv = p.cnt["pe"]
            if r == IC - 1:
                pe_up_last = uv
                ixb_guard[par] = ("pe", uv)
            p.wait("scalar", "pe", uv)
            if scr_free is not None:
                p.waitv("scalar", scr_free)
            av = p.do("scalar", lambda E, bg=bg, r=r: E.activation(
                sg_scr[:, :], pb[bg][:, :], AF.Silu,
                scale=cg_cols[:, r:r + 1]), "act")
            bank_free[bg] = ("act", av)
            p.wait("vector", "act", av)
            p.fence("vector")
            uev = p.do("vector", lambda E, bu=bu, r=r: E.tensor_scalar(
                su_scr[:, :], pb[bu][:, :], cu_cols[:, r:r + 1], None,
                ALU.mult), "dve")
            bank_free[bu] = ("dve", uev)
            p.fence("vector")
            p.seq("vector", lambda E: E.tensor_mul(
                hf_scr[:, :], sg_scr[:, :], su_scr[:, :]))
            p.fence("vector")
            p.seq("vector", lambda E: E.reduce_max(
                red_tmp[:, 0:1], hf_scr[:, :], axis=AX.X,
                apply_absolute_value=True))
            p.fence("vector")
            p.seq("vector", lambda E: E.tensor_max(
                maxh_col[:, :], maxh_col[:, :], red_tmp[:, 0:1]))
            hcv = p.do("vector", lambda E, par=par, r=r: E.tensor_copy(
                hb[0][:, r, :], hf_scr[:, :]), "dve")
            scr_free = ("dve", hcv)
        h_dve = p.cnt["dve"]
        p.wait("scalar", "dve", h_dve)
        hsp_v[i] = (f"sph{par}",
                    p.dma("scalar", h_sp[i, :, :], hb[0][:, :, :],
                          f"sph{par}"))
        hb_free[0] = hsp_v[i]

        job = next(prep_d_iter, None)
        if job is not None:
            gen_prep(job)

    for job in prep_d_iter:
        gen_prep(job)
    prep_gs = p.cnt["gs"]

    if DEBUG:
        dbg_canB = nc.dram_tensor("dbg_canB", [P, 1024], F32,
                                  kind="ExternalOutput").ap()
        p.wait("gpsimd", "sph0", p.cnt["sph0"])
        p.wait("gpsimd", "sph1", p.cnt["sph1"])
        p.dma("gpsimd", dbg_canB, canary[:, :], "gb")
    bdve = p.cnt["dve"]
    p.wait("gpsimd", "dve", bdve)
    mxv2 = p.dma("gpsimd", maxbounce[0:1, :].rearrange("o p -> p o"),
                 maxh_col[:, :], "gb")
    p.wait("sync", "gb", mxv2)
    bv2 = p.dma("sync", brow[0:1, :], maxbounce[0:1, :], "ldc")
    p.wait("vector", "ldc", bv2)
    mv2 = p.do("vector", lambda E: E.reduce_max(
        scal[0:1, 5:6], brow[0:1, :], axis=AX.X), "dve")
    p.wait("gpsimd", "dve", mv2)
    p.dma("gpsimd", mx2_part[:, :], scal[0:1, 5:6], "gb")
    p.wait("gpsimd", "gb", p.cnt["gb"])
    ccv2 = p.do("gpsimd", lambda E: E.collective_compute(
        "AllReduce", ALU.max, replica_groups=ALLC,
        ins=[mx2_part[:, :]], outs=[mx2_red[:, :]]), "cc")
    p.wait("gpsimd", "cc", ccv2)
    gmx2 = p.dma("gpsimd", scal[0:1, 5:6], mx2_red[:, :], "gb")
    p.wait("vector", "gb", gmx2)
    p.seq("vector", lambda E: E.tensor_scalar(
        scal[0:1, 6:7], scal[0:1, 5:6], scal[0:1, 1:2], 1.0 / QMAX,
        ALU.mult, ALU.mult))
    p.fence("vector")
    sdv = p.do("vector", lambda E: E.reciprocal(scal[0:1, 7:8],
                                                scal[0:1, 6:7]), "dve")
    p.wait("tensor", "dve", sdv)
    p.waitv("tensor", bank_free[0])
    p.do("tensor", lambda E: E.matmul(
        pb[0][:, 0:1], ones_rf[0:1, :], scal[0:1, 7:8], start=True,
        stop=True))
    bcv2 = p.do("tensor", lambda E: E.matmul(
        pb[0][:, 1:2], ones_rf[0:1, :], scal[0:1, 6:7], start=True,
        stop=True), "pe")
    p.wait("scalar", "pe", bcv2)
    p.do("scalar", lambda E: E.activation(rd_col[:, :], pb[0][:, 0:1],
                                          AF.Copy))
    bco2 = p.do("scalar", lambda E: E.activation(sd_col[:, :], pb[0][:, 1:2],
                                                 AF.Copy), "act")
    bank_free[0] = ("act", bco2)
    p.wait("vector", "act", bco2)
    p.fence("vector")
    p.do("vector", lambda E: E.tensor_scalar(
        cd_cols[:, :], sd_cols[:, :], sd_col[:, :], None, ALU.mult), "dve")
    bar2_dve = p.cnt["dve"]

    nc.sbuf_base = _baseB
    qd_bf = [S(f"qd_bf{c}", [P, D], BF16) for c in range(IC)]
    S("qd_pad", [P, 4096], mybir.dt.uint8)
    _baseC1 = nc.sbuf_base
    pe_b_done = p.cnt["pe"]
    p.wait("sync", "pe", pe_b_done)
    p.wait("sync", "gs", prep_gs)
    p.wait("sync", "sph0", p.cnt["sph0"])
    p.wait("sync", "sph1", p.cnt["sph1"])
    p.wait("sync", "lb0", p.cnt["lb0"])
    p.wait("sync", "lb1", p.cnt["lb1"])
    for c in range(IC):
        p.dma("sync", qd_bf[c][:, :], qd_sp[:, c * P:(c + 1) * P], "tqd",
              transpose=True)
    qd_ld = p.cnt["tqd"]

    qca = [S("qca0", [P, 2 * NT, TILE], BF16)]
    qtil = [S(f"qtil{i}", [P, 2 * NT, TILE], BF16) for i in range(2)]
    xnc = [S(f"xnc{i}", [P, DC * TILE], BF16) for i in range(2)]
    rb_f = S("rb_f", [2, TILE], F32)
    rb_f2 = S("rb_f2", [2, TILE], F32)
    rb_b = S("rb_b", [2, TILE], BF16)
    rbx = [S(f"rbx{i}", [P, TILE], BF16) for i in range(2)]
    attn_b = S("attn_b", [P, DC, TILE], BF16)
    z_b = S("z_b", [P, DC, TILE], BF16)
    zsq = S("zsq", [P, TILE], BF16)
    srow = S("srow", [1, TILE], F32)
    sqrow = S("sqrow", [1, TILE], F32)
    murow = S("murow", [1, TILE], F32)
    varrow = S("varrow", [1, TILE], F32)
    rstdrow = S("rstdrow", [1, TILE], F32)
    mu_bb = S("mu_bb", [1, TILE], BF16)
    rstd_bb = S("rstd_bb", [1, TILE], BF16)
    mub = S("mub", [P, TILE], BF16)
    rstdb = S("rstdb", [P, TILE], BF16)
    ln2_t = S("ln2_t", [P, TILE], F32)
    ln2_u = S("ln2_u", [P, TILE], F32)
    sb_t = [S(f"sb_t{i}", [P, DC, TILE], BF16) for i in range(2)]
    wproj_bf = S("wproj_bf", [P, DC, D], BF16)
    p.wait("gpsimd", "pe", pe_b_done)
    p.wait("gpsimd", "gs", prep_gs)
    p.wait("gpsimd", "sph0", p.cnt["sph0"])
    p.wait("gpsimd", "sph1", p.cnt["sph1"])
    p.wait("gpsimd", "lb0", p.cnt["lb0"])
    p.wait("gpsimd", "lb1", p.cnt["lb1"])
    for _c in range(DC):
        p.dma("gpsimd", wproj_bf[:, _c, :], wproj_t[_c * P:(_c + 1) * P, :],
              "wp")
    wp_ld = p.cnt["wp"]
    p.wait("tensor", "wp", wp_ld)

    if DEBUG:
        dbg_z = nc.dram_tensor("dbg_z", [NT, P, DC * TILE], BF16,
                               kind="ExternalOutput").ap()
        dbg_z_done = None
    xnc_free = [None, None]
    sbt_free = [None, None]
    rbx_free = [None, None]
    qca_free = [None, None]
    qtil_free = [None, None]
    ssp_v = [None] * NT
    prev_sdve = None

    regA = nc.alloc_register(mybir.EngineType.SP, "rq_ev")
    regB = nc.alloc_register(mybir.EngineType.SP, "rq_od")
    regGC = [nc.alloc_register(mybir.EngineType.SP, f"rq_gc{i}")
             for i in range(4)]
    holder = {}

    def _dynsetup(E):
        pid = E.partition_id()
        E.reg_mov(regA, pid)
        E.reg_alu(regA, regA, 6, ALU.bitwise_and)
        E.reg_alu(regA, regA, P, ALU.mult)
        E.reg_mov(regB, pid)
        E.reg_alu(regB, regB, 1, ALU.bitwise_or)
        E.reg_alu(regB, regB, P, ALU.mult)
        holder["ev"] = E.snap(regA, min_val=0, max_val=(N_CORES - 2) * P)
        holder["od"] = E.snap(regB, min_val=0, max_val=(N_CORES - 1) * P)
        for cc_ in range(4):
            rg = regGC[cc_]
            E.reg_mov(rg, pid)
            E.reg_alu(rg, rg, 1, ALU.bitwise_and)
            E.reg_alu(rg, rg, 4, ALU.mult)
            if cc_ > 0:
                E.reg_alu(rg, rg, cc_, ALU.add)
            holder[f"gc{cc_}"] = E.snap(rg, min_val=0, max_val=DC - 1)

    p.do("sync", _dynsetup)
    p.wait("sync", "cc", ccq)

    for c in range(4):
        parc = c % 2
        qci = 0
        p.waitv("sync", qca_free[qci])
        qx4 = qx_gth.rearrange("r (i g t) -> r i g t", i=NT, g=DC)
        for ki, key in enumerate(("ev", "od")):
            qcav = p.do(
                "sync",
                lambda E, ki=ki, key=key, qci=qci, c=c: E.dma_start(
                    out=qca[qci][:, ki * NT:(ki + 1) * NT, :],
                    in_=qx4[bass.ds(holder[key], P), 0:NT,
                            bass.ds(holder[f"gc{c}"], 1), :]),
                f"lq{parc}", 16)
        p.wait("vector", f"lq{parc}", qcav)
        p.wait("tensor", f"lq{parc}", qcav)
        if c == 0:
            p.wait("tensor", "dve", bar1_dve)
        p.waitv("tensor", qtil_free[parc])
        for g in range(2 * NT):
            b0 = g % 2
            p.waitv("tensor", bank_free[b0])
            qk1v = p.do("tensor", lambda E, b=b0, c=c, parc=parc, g=g:
                        E.matmul(pb[b][0:2, :], k1m[:, c, :],
                                 qca[qci][:, g, :], start=True, stop=True),
                        "pe")
            p.wait("vector", "pe", qk1v)
            p.fence("vector")
            p.seq("vector", lambda E, b=b0: E.tensor_scalar(
                rb_f[:, :], pb[b][0:2, :], 1e-6, None, ALU.max))
            p.fence("vector")
            p.seq("vector", lambda E: E.reciprocal(rb_f2[:, :], rb_f[:, :]))
            p.fence("vector")
            rbv = p.do("vector", lambda E: E.tensor_copy(rb_b[:, :],
                                                         rb_f2[:, :]), "dve")
            bank_free[b0] = ("dve", rbv)
            b1 = 4 + (g % 2)
            p.waitv("tensor", bank_free[b1])
            p.wait("tensor", "dve", rbv)
            rbm = p.do("tensor", lambda E, b=b1: E.matmul(
                pb[b][:, :], em2h[:, :], rb_b[:, :], start=True, stop=True),
                "pe")
            rset = g % 2
            p.wait("scalar", "pe", rbm)
            p.waitv("scalar", rbx_free[rset])
            av = p.do("scalar", lambda E, b=b1, rs=rset: E.activation(
                rbx[rs][:, :], pb[b][:, :], AF.Copy), "act")
            bank_free[b1] = ("act", av)
            p.wait("vector", "act", av)
            qtv = p.do("vector", lambda E, parc=parc, g=g, rs=rset:
                       E.tensor_tensor(qtil[parc][:, g, :],
                                       qca[qci][:, g, :], rbx[rs][:, :],
                                       ALU.mult), "dve")
            rbx_free[rset] = ("dve", qtv)
        qt_dve = p.cnt["dve"]
        qca_free[qci] = ("dve", qt_dve)
        p.wait("tensor", "dve", qt_dve)

        for hh in range(2):
            bh = 2 * c + hh
            par = bh % 2
            p.waitv("sync", xnc_free[par])
            p.waitv("sync", xnf_v[bh])
            xv = p.dma("sync", xnc[par][:, :], xnf_sp[bh, :, :], f"lxn{par}")
            for cc in range(DC):
                bq_ = 2 + (cc % 2)
                p.waitv("tensor", bank_free[bq_])
                for u in range(2):
                    jj = 2 * cc + u
                    p.do("tensor", lambda E, b=bq_, c=c, hh=hh, u=u, jj=jj,
                         parc=parc: E.matmul(
                        pb[b][u * 64:(u + 1) * 64, :],
                        kv_bf[hh * 64:(hh + 1) * 64, c, 0:64],
                        qtil[parc][hh * 64:(hh + 1) * 64, :, jj::16],
                        start=True, stop=True),
                        "pe" if u == 1 else None)
                qov = p.cnt["pe"]
                p.wait("vector", "pe", qov)
                p.fence("vector")
                dv = p.do("vector", lambda E, b=bq_, cc=cc: E.tensor_copy(
                    attn_b[:, cc, :], pb[b][:, :]), "dve")
                bank_free[bq_] = ("dve", dv)
            attn_dve = p.cnt["dve"]
            p.wait("tensor", "dve", attn_dve)
            for cc2 in range(DC):
                bz = 6 + (cc2 % 2)
                p.waitv("tensor", bank_free[bz])
                for cc in range(DC):
                    p.do("tensor", lambda E, b=bz, c2=cc2, cc=cc: E.matmul(
                        pb[b][:, :], wproj_bf[:, cc, c2 * P:(c2 + 1) * P],
                        attn_b[:, cc, :], start=(cc == 0),
                        stop=(cc == DC - 1)),
                        "pe" if cc == DC - 1 else None)
                zv = p.cnt["pe"]
                p.wait("vector", "pe", zv)
                if cc2 == 0 and prev_sdve is not None:
                    p.wait("vector", "dve", prev_sdve)
                if DEBUG and cc2 == 0 and dbg_z_done is not None:
                    p.wait("vector", "gb", dbg_z_done)
                av = p.do("vector", lambda E, b=bz, c2=cc2: E.tensor_scalar(
                    z_b[:, c2, :], pb[b][:, :], bp_sb[:, c2:c2 + 1], None,
                    ALU.add), "dve")
                bank_free[bz] = ("dve", av)
            z_act = p.cnt["dve"]
            bsu, bsq = (bh % 2), 2 + (bh % 2)
            p.wait("tensor", "dve", z_act)
            p.waitv("tensor", bank_free[bsu])
            for cc in range(DC):
                p.do("tensor", lambda E, b=bsu, cc=cc: E.matmul(
                    pb[b][0:1, :], ones_cb[:, :], z_b[:, cc, :],
                    start=(cc == 0), stop=(cc == DC - 1)),
                    "pe" if cc == DC - 1 else None)
            p.waitv("tensor", bank_free[bsq])
            sqmv_prev_pe = None
            p.wait("scalar", "dve", z_act)
            for cc in range(DC):
                if sqmv_prev_pe is not None:
                    p.wait("scalar", "pe", sqmv_prev_pe)
                sqv = p.do("scalar", lambda E, cc=cc: E.activation(
                    zsq[:, :], z_b[:, cc, :], AF.Square), "act")
                p.wait("tensor", "act", sqv)
                sqmv_prev_pe = p.do("tensor", lambda E, b=bsq, cc=cc:
                                    E.matmul(pb[b][0:1, :], ones_cb[:, :],
                                             zsq[:, :], start=(cc == 0),
                                             stop=(cc == DC - 1)), "pe")
            sqmv = p.cnt["pe"]
            p.wait("vector", "pe", sqmv)
            p.seq("vector", lambda E, b=bsu: E.tensor_scalar(
                murow[0:1, :], pb[b][0:1, :], 1.0 / D, None, ALU.mult))
            p.seq("vector", lambda E, b=bsq: E.tensor_scalar(
                varrow[0:1, :], pb[b][0:1, :], 1.0 / D, LN_EPS, ALU.mult,
                ALU.add))
            p.fence("vector")
            p.seq("vector", lambda E: E.tensor_tensor(
                srow[0:1, :], murow[0:1, :], murow[0:1, :], ALU.mult))
            p.fence("vector")
            muv = p.do("vector", lambda E: E.tensor_sub(
                varrow[0:1, :], varrow[0:1, :], srow[0:1, :]), "dve")
            bank_free[bsu] = ("dve", muv)
            bank_free[bsq] = ("dve", muv)
            p.wait("scalar", "dve", muv)
            p.seq("scalar", lambda E: E.activation(
                sqrow[0:1, :], varrow[0:1, :], AF.Ln))
            p.fence("scalar")
            rsv = p.do("scalar", lambda E: E.activation(
                rstdrow[0:1, :], sqrow[0:1, :], AF.Exp, scale=-0.5), "act")
            p.wait("vector", "act", rsv)
            p.fence("vector")
            p.do("vector", lambda E: E.tensor_copy(mu_bb[0:1, :],
                                                   murow[0:1, :]))
            mbv = p.do("vector", lambda E: E.tensor_copy(
                rstd_bb[0:1, :], rstdrow[0:1, :]), "dve")
            p.wait("tensor", "dve", mbv)
            p.waitv("tensor", bank_free[bsu])
            p.waitv("tensor", bank_free[bsq])
            p.do("tensor", lambda E, b=bsu: E.matmul(
                pb[b][:, :], ones_rb[0:1, :], mu_bb[0:1, :], start=True,
                stop=True))
            bb2 = p.do("tensor", lambda E, b=bsq: E.matmul(
                pb[b][:, :], ones_rb[0:1, :], rstd_bb[0:1, :], start=True,
                stop=True), "pe")
            p.wait("scalar", "pe", bb2)
            p.do("scalar", lambda E, b=bsu: E.activation(
                mub[:, :], pb[b][:, :], AF.Copy))
            bb3 = p.do("scalar", lambda E, b=bsq: E.activation(
                rstdb[:, :], pb[b][:, :], AF.Copy), "act")
            bank_free[bsu] = ("act", bb3)
            bank_free[bsq] = ("act", bb3)
            p.wait("vector", "act", bb3)
            p.wait("vector", f"lxn{par}", xv)
            p.waitv("vector", sbt_free[par])
            p.fence("vector")
            for cc in range(DC):
                p.seq("vector", lambda E, cc=cc: E.scalar_tensor_tensor(
                    ln2_t[:, :], z_b[:, cc, :], 1.0, mub[:, :], ALU.mult,
                    ALU.subtract))
                p.fence("vector")
                p.seq("vector", lambda E: E.tensor_mul(
                    ln2_t[:, :], ln2_t[:, :], rstdb[:, :]))
                p.fence("vector")
                p.seq("vector", lambda E, cc=cc: E.tensor_scalar(
                    ln2_u[:, :], ln2_t[:, :], gam_sb[:, cc:cc + 1],
                    bet_sb[:, cc:cc + 1], ALU.mult, ALU.add))
                p.fence("vector")
                p.seq("vector", lambda E, cc=cc, par=par: E.tensor_tensor(
                    sb_t[par][:, cc, :], ln2_u[:, :],
                    xnc[par][:, cc * TILE:(cc + 1) * TILE], ALU.add))
            sdve = p.cnt["dve"]
            if DEBUG:
                p.wait("gpsimd", "dve", sdve)
                dbg_z_done = p.dma("gpsimd", dbg_z[bh, :, :], z_b[:, :, :],
                                   "gb")
            prev_sdve = sdve
            xnc_free[par] = ("dve", sdve)
            p.wait("scalar", "dve", sdve)
            ssp_v[bh] = (f"sps{par}",
                         p.dma("scalar", s_sp[bh, :, :], sb_t[par][:, :, :],
                               f"sps{par}"))
            sbt_free[par] = ssp_v[bh]
        qtil_free[parc] = ("pe", p.cnt["pe"])

    nc.sbuf_base = _baseC1
    hc = [S(f"hc{i}", [P, IC * TILE], BF16) for i in range(2)]
    qf2_scr = [S("qf2_scr0", [P, TILE], F32)]
    qi2_scr = [S("qi2_scr0", [P, TILE], I32)]
    ihb = [S(f"ihb{j}", [P, IC, TILE], BF16) for j in range(2)]
    sc_t = [S(f"sc_t{i}", [P, DC * TILE], BF16) for i in range(2)]
    g_scr = S("g_scr", [P, TILE], BF16)
    p_scr = [S(f"p_scr{i}", [P, TILE], F32) for i in range(DC)]
    xc2 = [S(f"xc2{i}", [P, D], F32) for i in range(2)]
    yst = [S(f"yst{i}", [P, D], F32) for i in range(2)]

    hc_free = [None, None]
    sct_free = [None, None]
    pscr_free = [None] * DC
    xc2_free = [None] * 2
    yst_free = [None, None]
    gscr_free = None
    qfC_free = [None, None]
    qiC_free = [None, None]
    ihb_guard = [None, None]
    pe_down_last = None

    for i in range(NT):
        par = i % 2
        t0 = i * TILE
        if i == 0:
            p.wait("sync", "sps0", p.cnt["sps0"])
            p.wait("sync", "sps1", p.cnt["sps1"])
        p.waitv("sync", hc_free[par])
        p.waitv("sync", hsp_v[i])
        hv = p.dma("sync", hc[par][:, :], h_sp[i, :, :], f"lh{par}")
        p.waitv("sync", sct_free[par])
        p.waitv("sync", ssp_v[i])
        sv_ = p.dma("sync", sc_t[par][:, :], s_sp[i, :, :], f"ls{par}")
        p.wait("vector", f"lh{par}", hv)
        if i == 0:
            p.wait("vector", "dve", bar2_dve)
        for r in range(IC):
            s = 0
            p.waitv("vector", qfC_free[s])
            mv = p.do("vector", lambda E, par=par, r=r, s=s: E.tensor_scalar(
                qf2_scr[s][:, :], hc[par][:, r * TILE:(r + 1) * TILE],
                rd_col[:, :], None, ALU.mult), "dve")
            p.wait("scalar", "dve", mv)
            p.waitv("scalar", qiC_free[s])
            av = p.do("scalar", lambda E, s=s: E.activation(
                qi2_scr[s][:, :], qf2_scr[s][:, :], AF.Copy), "act")
            qfC_free[s] = ("act", av)
            p.wait("gpsimd", "act", av)
            if r == 0:
                p.waitv("gpsimd", ihb_guard[par])
            cv = p.do("gpsimd", lambda E, r=r, s=s, par=par: E.tensor_scalar(
                ihb[par][:, r, :], qi2_scr[s][:, :], 127.0, -128.0, ALU.min,
                ALU.max), "pool")
            qiC_free[s] = ("pool", cv)
        ih_pool = p.cnt["pool"]
        hc_free[par] = ("dve", p.cnt["dve"])
        p.wait("tensor", "pool", ih_pool)
        if i == 0:
            p.wait("tensor", "tqd", qd_ld)
        for c in range(DC):
            bz = 4 + (c % 4)
            p.waitv("tensor", bank_free[bz])
            for rr in range(IC):
                p.do("tensor", lambda E, b=bz, c=c, rr=rr, par=par: E.matmul(
                    pb[b][:, :], qd_bf[rr][:, c * P:(c + 1) * P],
                    ihb[par][:, rr, :], start=(rr == 0), stop=(rr == IC - 1)),
                    "pe" if rr == IC - 1 else None)
            dvn = p.cnt["pe"]
            if c == DC - 1:
                pe_down_last = dvn
                ihb_guard[par] = ("pe", dvn)
            p.wait("scalar", "pe", dvn)
            if gscr_free is not None:
                p.waitv("scalar", gscr_free)
            av = p.do("scalar", lambda E, b=bz, c=c: E.activation(
                g_scr[:, :], pb[b][:, :], AF.Silu,
                scale=cd_cols[:, c:c + 1]), "act")
            bank_free[bz] = ("act", av)
            p.wait("vector", "act", av)
            p.wait("vector", f"ls{par}", sv_)
            p.waitv("vector", pscr_free[c])
            pv = p.do("vector", lambda E, par=par, c=c: E.tensor_tensor(
                p_scr[c][:, :], g_scr[:, :],
                sc_t[par][:, c * TILE:(c + 1) * TILE], ALU.mult), "dve")
            gscr_free = ("dve", pv)
            if c == DC - 1:
                sct_free[par] = ("dve", pv)
        pall = p.cnt["dve"]
        p.wait("tensor", "dve", pall)
        for j in range(NCH):
            slot = j % 2
            p.waitv("sync", xc2_free[slot])
            xvv = p.dma("sync", xc2[slot][:, :],
                        x_in[t0 + j * P: t0 + (j + 1) * P, :], f"lx{slot}")
            yb = j % 2
            yv = None
            for c in range(DC):
                b = c % 4
                p.waitv("tensor", bank_free[b])
                tv = p.do("tensor", lambda E, b=b, c=c, j=j: E.transpose(
                    pb[b][:, 0:P], p_scr[c][:, j * P:(j + 1) * P],
                    ident[:, :]), "pe")
                p.wait("vector", "pe", tv)
                if c == 0:
                    p.waitv("vector", yst_free[yb])
                    p.wait("vector", f"lx{slot}", xvv)
                yv = p.do("vector", lambda E, b=b, c=c, yb=yb, slot=slot:
                          E.scalar_tensor_tensor(
                              yst[yb][:, c * P:(c + 1) * P], pb[b][:, 0:P],
                              1.0, xc2[slot][:, c * P:(c + 1) * P],
                              ALU.mult, ALU.add), "dve")
                bank_free[b] = ("dve", yv)
            xc2_free[slot] = ("dve", yv)
            p.wait("scalar", "dve", yv)
            ov = p.dma("scalar", y_out[t0 + j * P: t0 + (j + 1) * P, :],
                       yst[yb][:, :], f"spy{yb}")
            yst_free[yb] = (f"spy{yb}", ov)
        tile_pe = p.cnt["pe"]
        for c in range(DC):
            pscr_free[c] = ("pe", tile_pe)

    p.wait("scalar", "spy0", p.cnt["spy0"])
    p.wait("scalar", "spy1", p.cnt["spy1"])
    p.wait("gpsimd", "spy0", p.cnt["spy0"])
    p.wait("gpsimd", "spy1", p.cnt["spy1"])

    if DEBUG:
        dbg_can = nc.dram_tensor("dbg_canary", [P, 1024], F32,
                                 kind="ExternalOutput").ap()
        p.dma("gpsimd", dbg_can, canary[:, :], "gb")
        dbg = [
            ("qgsp", qg_sp, [IFF, D], BF16),
            ("qusp", qu_sp, [IFF, D], BF16),
            ("qdsp", qd_sp, [D, IFF], BF16),
            ("xntm", xn_tm_sp, [TOK, D], BF16),
            ("xnf", xnf_sp, [NT, P, DC * TILE], BF16),
            ("q", q_sp, [P, NT, DC * TILE], BF16),
            ("kvp", kvk1_part, [P, 8 * 65], F32),
            ("h", h_sp, [NT, P, IC * TILE], BF16),
            ("s", s_sp, [NT, P, DC * TILE], BF16),
        ]
        for name, src_, shape, dt in dbg:
            dd = nc.dram_tensor("dbg_" + name, shape, dt,
                                kind="ExternalOutput").ap()
            p.dma("gpsimd", dd, src_, "gb")
        dsc = nc.dram_tensor("dbg_scal", [1, 16], F32,
                             kind="ExternalOutput").ap()
        p.dma("gpsimd", dsc, scal[0:1, :], "gb")
        p.wait("gpsimd", "gb", p.cnt["gb"])

    p.emit()
    return nc


def _consts():
    ones_rb = np.ones((1, P), dtype=ml_dtypes.bfloat16)
    ones_cb = np.ones((P, 1), dtype=ml_dtypes.bfloat16)
    ones_rf = np.ones((1, P), dtype=np.float32)
    em2p = np.zeros((P, 2), dtype=ml_dtypes.bfloat16)
    em2h = np.zeros((2, P), dtype=ml_dtypes.bfloat16)
    for pp in range(P):
        em2p[pp, pp // 64] = 1.0
        em2h[pp // 64, pp] = 1.0
    ident = np.eye(P, dtype=np.float32)
    return ones_rb, ones_cb, ones_rf, em2p, em2h, ident


def kernel(L0_attn_H, L0_attn_W_qkv, L0_attn_b_qkv, L0_attn_W_proj,
           L0_attn_b_proj, L0_attn_gamma, L0_attn_beta, L0_ffn_clip_x,
           L0_ffn_clip_down, L0_ffn_w_gate, L0_ffn_w_up, L0_ffn_w_down):
    x = np.ascontiguousarray(np.asarray(L0_attn_H)).reshape(4 * 8192, D)
    wqkv_t = np.ascontiguousarray(np.asarray(L0_attn_W_qkv).T)
    wproj_t = np.ascontiguousarray(np.asarray(L0_attn_W_proj).T)
    wg = np.ascontiguousarray(np.asarray(L0_ffn_w_gate))
    wu = np.ascontiguousarray(np.asarray(L0_ffn_w_up))
    wd = np.ascontiguousarray(np.asarray(L0_ffn_w_down))
    bqkv = np.asarray(L0_attn_b_qkv)
    bq_cols = np.ascontiguousarray(bqkv[0:D].reshape(DC, P).T)
    bkv_row = np.ascontiguousarray(bqkv[D:3 * D].reshape(1, 2 * D))
    bp_cols = np.ascontiguousarray(np.asarray(L0_attn_b_proj).reshape(DC, P).T)
    gam_cols = np.ascontiguousarray(np.asarray(L0_attn_gamma).reshape(DC, P).T)
    bet_cols = np.ascontiguousarray(np.asarray(L0_attn_beta).reshape(DC, P).T)
    clip = np.zeros((1, 2), dtype=np.float32)
    clip[0, 0] = np.asarray(L0_ffn_clip_x).reshape(-1)[0]
    clip[0, 1] = np.asarray(L0_ffn_clip_down).reshape(-1)[0]
    ones_rb, ones_cb, ones_rf, em2p, em2h, ident = _consts()

    nc = build()
    in_maps = []
    for c in range(N_CORES):
        b, half = c // 2, c % 2
        r0 = b * 8192 + half * TOK
        pmask = np.zeros((P, N_CORES), dtype=np.float32)
        pmask[:, 2 * b] = 1.0
        pmask[:, 2 * b + 1] = 1.0
        sel2 = np.zeros((P, 2), dtype=np.float32)
        sel2[:, half] = 1.0
        in_maps.append({
            "x": np.ascontiguousarray(x[r0:r0 + TOK, :]),
            "wqkv_t": wqkv_t, "wproj_t": wproj_t,
            "wg_nat": wg, "wu_nat": wu, "wd_nat": wd,
            "bq_in": bq_cols, "bkv_in": bkv_row, "bp_in": bp_cols,
            "gam_in": gam_cols, "bet_in": bet_cols, "clip_in": clip,
            "ones_rb_in": ones_rb, "ones_cb_in": ones_cb,
            "ones_rf_in": ones_rf, "em2p_in": em2p, "em2h_in": em2h,
            "sel2_in": sel2, "pmask_in": pmask, "ident_in": ident,
        })
    res = run_bass_kernel_spmd(nc, in_maps, core_ids=list(range(N_CORES)))
    out = np.concatenate([res.results[c]["y"] for c in range(N_CORES)],
                         axis=0)
    return out.reshape(4, 8192, D)


# version 17
# speedup vs baseline: 1.0961x; 1.0901x over previous
import numpy as np
import ml_dtypes
import concourse.bass as bass
import concourse.mybir as mybir
from concourse.bass_utils import run_bass_kernel_spmd

F32 = mybir.dt.float32
BF16 = mybir.dt.bfloat16
I32 = mybir.dt.int32
AF = mybir.ActivationFunctionType
ALU = mybir.AluOpType
AX = mybir.AxisListType

P = 128
D = 1024
DC = D // P
E3 = 3072
NH = 16
HD = 64
IFF = 2816
IC = IFF // P
TILE = 512
NT = 8
TOK = TILE * NT
NCH = TILE // P
QMAX = 127.0
LN_EPS = 1e-5
N_CORES = 8
ALLC = [list(range(N_CORES))]

ENGS = ("sync", "scalar", "vector", "tensor", "gpsimd")
DEBUG = False


class Prog:

    def __init__(self, nc, sems):
        self.nc = nc
        self.ops = {e: [] for e in ENGS}
        self.cnt = {s: 0 for s in sems}
        self.h = {s: nc.alloc_semaphore(s) for s in sems}

    def do(self, eng, fn, sem=None, n=1):
        if sem is None:
            self.ops[eng].append(fn)
            return None
        self.cnt[sem] += n
        h = self.h[sem]
        self.ops[eng].append(lambda E, fn=fn, h=h, n=n: fn(E).then_inc(h, n))
        return self.cnt[sem]

    def dma(self, eng, out, in_, sem, **kw):
        return self.do(
            eng,
            lambda E, o=out, i=in_, kw=kw: E.dma_start(out=o, in_=i, **kw),
            sem, 16)

    def wait(self, eng, sem, val):
        if val is None or val <= 0:
            return
        h = self.h[sem]
        self.ops[eng].append(lambda E, h=h, v=val: E.wait_ge(h, v))

    ENG_SEM = {"vector": "dve", "scalar": "act", "gpsimd": "pool",
               "tensor": "pe"}

    def fence(self, eng):
        sem = self.ENG_SEM[eng]
        self.wait(eng, sem, self.cnt[sem])

    def seq(self, eng, fn):
        return self.do(eng, fn, self.ENG_SEM[eng])

    def waitv(self, eng, sv):
        if sv is None:
            return
        if isinstance(sv, list):
            for e in sv:
                self.waitv(eng, e)
            return
        self.wait(eng, sv[0], sv[1])

    def emit(self):
        with self.nc.Block() as block:
            for eng in ENGS:
                ops = self.ops[eng]

                def runner(E, ops=ops):
                    for f in ops:
                        f(E)

                getattr(block, eng)(runner)


def build():
    nc = bass.Bass("TRN2", target_bir_lowering=False, detect_race_conditions=False)

    x_in = nc.dram_tensor("x", [TOK, D], F32, kind="ExternalInput").ap()
    wqkv_t = nc.dram_tensor("wqkv_t", [D, E3], F32, kind="ExternalInput").ap()
    wproj_t = nc.dram_tensor("wproj_t", [D, D], F32, kind="ExternalInput").ap()
    wg_nat = nc.dram_tensor("wg_nat", [IFF, D], F32, kind="ExternalInput").ap()
    wu_nat = nc.dram_tensor("wu_nat", [IFF, D], F32, kind="ExternalInput").ap()
    wd_nat = nc.dram_tensor("wd_nat", [D, IFF], F32, kind="ExternalInput").ap()
    bq_in = nc.dram_tensor("bq_in", [P, DC], F32, kind="ExternalInput").ap()
    bkv_in = nc.dram_tensor("bkv_in", [1, 2 * D], F32, kind="ExternalInput").ap()
    bp_in = nc.dram_tensor("bp_in", [P, DC], F32, kind="ExternalInput").ap()
    gam_in = nc.dram_tensor("gam_in", [P, DC], F32, kind="ExternalInput").ap()
    bet_in = nc.dram_tensor("bet_in", [P, DC], F32, kind="ExternalInput").ap()
    clip_in = nc.dram_tensor("clip_in", [1, 2], F32, kind="ExternalInput").ap()
    ones_rb_in = nc.dram_tensor("ones_rb_in", [1, P], BF16, kind="ExternalInput").ap()
    ones_cb_in = nc.dram_tensor("ones_cb_in", [P, 1], BF16, kind="ExternalInput").ap()
    ones_rf_in = nc.dram_tensor("ones_rf_in", [1, P], F32, kind="ExternalInput").ap()
    pmask_in = nc.dram_tensor("pmask_in", [P, N_CORES], F32, kind="ExternalInput").ap()
    em2p_in = nc.dram_tensor("em2p_in", [P, 2], BF16, kind="ExternalInput").ap()
    em2h_in = nc.dram_tensor("em2h_in", [P, 2 * NT * P], BF16, kind="ExternalInput").ap()
    sel2_in = nc.dram_tensor("sel2_in", [P, 2], F32, kind="ExternalInput").ap()

    y_out = nc.dram_tensor("y", [TOK, D], F32, kind="ExternalOutput").ap()

    xn_tm_sp = nc.dram_tensor("xn_tm_sp", [TOK, D], BF16).ap()
    xnf_sp = nc.dram_tensor("xnf_sp", [NT, P, DC * TILE], BF16).ap()
    q_sp = nc.dram_tensor("q_sp", [P, NT, DC * TILE], BF16).ap()
    h_sp = nc.dram_tensor("h_sp", [NT, P, IC * TILE], BF16).ap()
    s_sp = nc.dram_tensor("s_sp", [NT, P, DC * TILE], BF16).ap()
    p_sp = nc.dram_tensor("p_sp", [D, TOK], BF16).ap()
    qg_sp = nc.dram_tensor("qg_sp", [IFF, D], BF16).ap()
    qu_sp = nc.dram_tensor("qu_sp", [IFF, D], BF16).ap()
    qd_sp = nc.dram_tensor("qd_sp", [D, IFF], BF16).ap()
    maxbounce = nc.dram_tensor("maxbounce", [1, P], F32).ap()
    kvk1_part = nc.dram_tensor("kvk1_part", [P, 8 * 65], F32).ap()
    kvk1_gth = nc.dram_tensor("kvk1_gth", [N_CORES * P, 8 * 65], F32,
                              addr_space="Shared").ap()
    mx1_part = nc.dram_tensor("mx1_part", [1, 1], F32).ap()
    mx1_red = nc.dram_tensor("mx1_red", [1, 1], F32, addr_space="Shared").ap()
    mx2_part = nc.dram_tensor("mx2_part", [1, 1], F32).ap()
    mx2_red = nc.dram_tensor("mx2_red", [1, 1], F32, addr_space="Shared").ap()
    qx_gth = nc.dram_tensor("qx_gth", [N_CORES * P, NT * DC * TILE],
                            BF16, addr_space="Shared").ap()

    sems = (["ldc", "wq", "wp", "gs", "gb", "tqg", "tqd"]
            + [f"gl{i}" for i in range(2)]
            + ["pe", "dve", "act", "pool", "cc"]
            + [f"lx{i}" for i in range(4)]
            + [f"{nm}{i}" for nm in ("sptm", "trf", "spf", "spq", "lb",
                                     "sph", "lq", "lxn", "sps", "lh", "ls",
                                     "spp", "tpt", "spy") for i in range(2)])
    p = Prog(nc, sems)

    pb = [nc.alloc_psum_tensor(f"pb{i}", [P, TILE], F32) for i in range(8)]

    import functools
    S = functools.partial(nc.alloc_sbuf_tensor, unsafe_allow_realloc=True)
    kv_bf = S("kv_bf", [P, 4, 65], BF16)
    k1m = S("k1m", [P, 4, 2], BF16)
    em2p = S("em2p", [P, 2], BF16)
    sel2 = S("sel2", [P, 2], F32)
    ones_rb = S("ones_rb", [1, P], BF16)
    ones_cb = S("ones_cb", [P, 1], BF16)
    ones_rf = S("ones_rf", [1, P], F32)
    pmask = S("pmask", [P, N_CORES], F32)
    bq_sb = S("bq_sb", [P, DC], F32)
    bkv_sb = S("bkv_sb", [1, 2 * D], BF16)
    bp_sb = S("bp_sb", [P, DC], F32)
    gam_sb = S("gam_sb", [P, DC], F32)
    bet_sb = S("bet_sb", [P, DC], F32)
    clip_sb = S("clip_sb", [1, 2], F32)
    sg_cols = S("sg_cols", [P, IC], F32)
    su_cols = S("su_cols", [P, IC], F32)
    sd_cols = S("sd_cols", [P, DC], F32)
    cg_cols = S("cg_cols", [P, IC], F32)
    cu_cols = S("cu_cols", [P, IC], F32)
    cd_cols = S("cd_cols", [P, DC], F32)
    rx_col = S("rx_col", [P, 1], F32)
    rd_col = S("rd_col", [P, 1], F32)
    sx_col = S("sx_col", [P, 1], F32)
    sd_col = S("sd_col", [P, 1], F32)
    scal = S("scal", [1, 16], F32)
    brow = S("brow", [1, P], F32)
    zero_rb = S("zero_rb", [1, P], BF16)
    maxxn_col = S("maxxn_col", [P, 1], F32)
    maxh_col = S("maxh_col", [P, 1], F32)
    red_tmp = S("red_tmp", [P, 1], F32)

    _baseA1 = nc.sbuf_base
    xch = [S(f"xch{i}", [P, D], F32) for i in range(4)]
    xn_tm = [S(f"xn_tm{i}", [P, NCH, D], BF16) for i in range(2)]
    xn_fm = []
    for _i in range(2):
        xn_fm.append(S(f"xn_fm{_i}", [P, DC, TILE], BF16))
        S(f"xn_fm_pad{_i}", [P, 4096], mybir.dt.uint8)
    qt = [S(f"qt{i}", [P, DC, TILE], BF16) for i in range(2)]
    kt = [S(f"kt{i}", [P, NCH, D], BF16) for i in range(2)]
    vt = [S(f"vt{i}", [P, NCH, NH, 65], BF16) for i in range(2)]
    wqkv_bf = S("wqkv_bf", [P, DC, E3], BF16)
    ln_sum = S("ln_sum", [P, NCH], F32)
    ln_ssq = S("ln_ssq", [P, NCH], F32)
    ln_mu = S("ln_mu", [P, NCH], F32)
    ln_var = S("ln_var", [P, NCH], F32)
    ln_rstd = S("ln_rstd", [P, NCH], F32)
    ln_t = S("ln_t", [P, D], F32)
    dummy1 = ln_t
    dummy2 = ln_t
    ln_tmp = S("ln_tmp", [P, NCH], F32)
    phi_n = [S(f"phi_n{i}", [P, TILE], BF16) for i in range(3)]
    phi_e = [S(f"phi_e{i}", [P, TILE], BF16) for i in range(3)]
    phi_p = [S(f"phi_p{i}", [P, TILE], BF16) for i in range(3)]
    wst = [S(f"wst{j}", [P, D], F32) for j in range(2)]
    wq_f = S("wq_f", [P, D], F32)
    wq_i = S("wq_i", [P, D], I32)
    wq_b = S("wq_b", [P, D], BF16)
    wmx = S("wmx", [P, 1], F32)
    wmx2 = S("wmx2", [P, 1], F32)
    winv = S("winv", [P, 1], F32)


    for (dst, src) in [
        (bq_sb[:, :], bq_in), (bp_sb[:, :], bp_in),
        (gam_sb[:, :], gam_in), (bet_sb[:, :], bet_in), (clip_sb[:, :], clip_in),
        (ones_rb[:, :], ones_rb_in), (ones_cb[:, :], ones_cb_in),
        (ones_rf[:, :], ones_rf_in),
        (em2p[:, :], em2p_in),
        (sel2[:, :], sel2_in), (pmask[:, :], pmask_in),
    ]:
        p.dma("sync", dst, src, "ldc")
    consts_ld = p.cnt["ldc"]

    p.dma("gpsimd", bkv_sb[:, :], bkv_in, "wq")
    for c in range(DC):
        p.dma("gpsimd", wqkv_bf[:, c, :], wqkv_t[c * P:(c + 1) * P, :], "wq")
    wq_ld = p.cnt["wq"]

    p.wait("vector", "ldc", consts_ld)
    p.do("vector", lambda E: E.memset(zero_rb[:, :], 0.0))
    p.do("vector", lambda E: E.memset(maxxn_col[:, :], 0.0))
    p.do("vector", lambda E: E.memset(maxh_col[:, :], 0.0))
    p.do("vector", lambda E: E.memset(vt[0][:, :, :, 64:65], 1.0))
    p.do("vector", lambda E: E.memset(vt[1][:, :, :, 64:65], 1.0), "dve")
    startup_dve = p.cnt["dve"]

    p.wait("scalar", "ldc", consts_ld)
    p.do("scalar", lambda E: E.activation(scal[0:1, 8:10], clip_sb[0:1, 0:2],
                                          AF.Exp, scale=-1.0), "act")
    sig_act = p.cnt["act"]
    p.wait("vector", "act", sig_act)
    p.seq("vector", lambda E: E.tensor_scalar(scal[0:1, 10:12],
                                              scal[0:1, 8:10],
                                              1.0, None, ALU.add))
    p.fence("vector")
    p.do("vector", lambda E: E.reciprocal(scal[0:1, 0:2], scal[0:1, 10:12]),
         "dve")
    sig_dve = p.cnt["dve"]
    if DEBUG:
        dsc1 = nc.dram_tensor("dbg_scal1", [1, 16], F32,
                              kind="ExternalOutput").ap()
        p.wait("gpsimd", "dve", sig_dve)
        p.dma("gpsimd", dsc1, scal[0:1, :], "gb")

    prep_jobs = ([("g", r) for r in range(IC)] + [("u", r) for r in range(IC)]
                 + [("d", r) for r in range(DC)])
    wst_free = [None, None]
    wqb_free = [None]
    wqf_free = [None]
    prep_slot = [0]

    def _prep_ld(nat, r, c0, c1, eng):
        sl = prep_slot[0] % 2
        prep_slot[0] += 1
        w = c1 - c0
        p.waitv(eng, wst_free[sl])
        sem = f"gl{sl}"
        ldv = p.dma(eng, wst[sl][:, 0:w], nat[r * P:(r + 1) * P, c0:c1],
                    sem)
        return sl, ldv, w, sem

    def gen_prep(job):
        kind, r = job
        nat = {"g": wg_nat, "u": wu_nat, "d": wd_nat}[kind]
        spd = {"g": qg_sp, "u": qu_sp, "d": qd_sp}[kind]
        scol = {"g": sg_cols, "u": su_cols, "d": sd_cols}[kind]
        C = D if kind in ("g", "u") else IFF
        halves = [(0, C)] if C <= D else [(0, D), (D, 2 * D), (2 * D, C)]
        single = len(halves) == 1
        ldq = "sync"
        loads = []
        for hi, (c0, c1) in enumerate(halves):
            sl, ldv, w, lsem = _prep_ld(nat, r, c0, c1, ldq)
            loads.append((sl, ldv, w, lsem))
            p.wait("vector", lsem, ldv)
            mdst = wmx if hi == 0 else wmx2
            mv = p.seq("vector", lambda E, m=mdst, w=w, sl=sl: E.reduce_max(
                m[:, :], wst[sl][:, 0:w], axis=AX.X,
                apply_absolute_value=True))
            if hi > 0:
                p.fence("vector")
                mv = p.seq("vector", lambda E: E.tensor_max(
                    wmx[:, :], wmx[:, :], wmx2[:, :]))
            if not single:
                wst_free[sl] = ("dve", mv)
        p.fence("vector")
        p.seq("vector", lambda E: E.reciprocal(winv[:, :], wmx[:, :]))
        p.do("vector", lambda E, sc=scol, r=r: E.tensor_scalar(
            sc[:, r:r + 1], wmx[:, :], 1.0 / QMAX, None, ALU.mult), "dve")
        inv_v = p.cnt["dve"]
        for hi, (c0, c1) in enumerate(halves):
            if single:
                sl, ldv, w, lsem = loads[hi]
            else:
                sl, ldv, w, lsem = _prep_ld(nat, r, c0, c1, ldq)
                p.wait("vector", lsem, ldv)
            p.waitv("vector", wqf_free[0])
            mdve = p.do("vector", lambda E, w=w, sl=sl: E.tensor_scalar(
                wq_f[:, 0:w], wst[sl][:, 0:w], winv[:, :], QMAX, ALU.mult,
                ALU.mult), "dve")
            wst_free[sl] = ("dve", mdve)
            p.wait("scalar", "dve", mdve)
            p.waitv("scalar", wqb_free[0])
            a1 = p.seq("scalar", lambda E, w=w: E.activation(
                wq_i[:, 0:w], wq_f[:, 0:w], AF.Copy))
            wqf_free[0] = ("act", a1)
            p.fence("scalar")
            a2 = p.do("scalar", lambda E, w=w: E.activation(
                wq_b[:, 0:w], wq_i[:, 0:w], AF.Copy), "act")
            p.wait("gpsimd", "act", a2)
            spv = p.dma("gpsimd", spd[r * P:(r + 1) * P, c0:c1], wq_b[:, 0:w],
                        "gs")
            wqb_free[0] = ("gs", spv)

    xch_free = [None] * 4
    xn_tm_free = [None, None]
    xn_fm_free = [None, None]
    qt_free = [None, None]
    ktvt_free = [None, None]
    bank_free = [None] * 8
    phi_free = [None, None, None]
    q_spill_v = [None] * NT
    xnf_v = [None] * NT
    prev_var_dve = 0

    for i in range(NT):
        par = i % 2
        t0 = i * TILE
        ldxv = []
        for j in range(NCH):
            slot = j
            p.waitv("sync", xch_free[slot])
            v = p.dma("sync", xch[slot][:, :],
                      x_in[t0 + j * P: t0 + (j + 1) * P, :], f"lx{slot}")
            ldxv.append((slot, (f"lx{slot}", v)))
        p.fence("vector")
        p.do("vector", lambda E: E.memset(ln_sum[:, :], 0.0))
        if i > 0:
            p.wait("scalar", "dve", prev_var_dve)
        p.do("scalar", lambda E: E.memzero(ln_ssq[:, :]))
        for j in range(NCH):
            slot, v = ldxv[j]
            p.waitv("vector", v)
            p.do("vector", lambda E, s=slot, j=j: E.tensor_scalar(
                dummy1[:, :], xch[s][:, :], 1.0, 0.0, ALU.mult, ALU.add,
                accum_out=ln_sum[:, j:j + 1]), "dve")
            p.waitv("scalar", v)
            p.do("scalar", lambda E, s=slot, j=j: E.activation(
                dummy2[:, :], xch[s][:, :], AF.Square,
                accum_out=ln_ssq[:, j:j + 1]), "act")
        sums_act = p.cnt["act"]
        p.wait("vector", "act", sums_act)
        p.fence("vector")
        p.seq("vector", lambda E: E.tensor_scalar(
            ln_mu[:, :], ln_sum[:, :], 1.0 / D, None, ALU.mult))
        p.seq("vector", lambda E: E.tensor_scalar(
            ln_var[:, :], ln_ssq[:, :], 1.0 / D, LN_EPS, ALU.mult, ALU.add))
        p.fence("vector")
        p.seq("vector", lambda E: E.tensor_tensor(
            ln_tmp[:, :], ln_mu[:, :], ln_mu[:, :], ALU.mult))
        p.fence("vector")
        p.do("vector", lambda E: E.tensor_sub(
            ln_var[:, :], ln_var[:, :], ln_tmp[:, :]), "dve")
        var_dve = p.cnt["dve"]
        prev_var_dve = var_dve
        p.wait("scalar", "dve", var_dve)
        p.seq("scalar", lambda E: E.activation(
            ln_tmp[:, :], ln_var[:, :], AF.Ln))
        p.fence("scalar")
        p.do("scalar", lambda E: E.activation(
            ln_rstd[:, :], ln_tmp[:, :], AF.Exp, scale=-0.5), "act")
        rstd_act = p.cnt["act"]
        p.waitv("vector", xn_tm_free[par])
        p.wait("vector", "act", rstd_act)
        p.wait("vector", "dve", var_dve)
        for j in range(NCH):
            slot, v = ldxv[j]
            p.fence("vector")
            p.seq("vector", lambda E, s=slot, j=j: E.tensor_scalar(
                ln_t[:, :], xch[s][:, :], ln_mu[:, j:j + 1],
                ln_rstd[:, j:j + 1], ALU.subtract, ALU.mult))
            p.fence("vector")
            p.seq("vector", lambda E: E.reduce_max(
                ln_tmp[:, 0:1], ln_t[:, :], axis=AX.X,
                apply_absolute_value=True))
            p.fence("vector")
            p.seq("vector", lambda E: E.tensor_max(
                maxxn_col[:, :], maxxn_col[:, :], ln_tmp[:, 0:1]))
            mdve = p.do("vector", lambda E, par=par, j=j: E.tensor_copy(
                xn_tm[par][:, j, :], ln_t[:, :]), "dve")
            xch_free[slot] = ("dve", mdve)
        xn_dve = p.cnt["dve"]

        p.wait("scalar", "dve", xn_dve)
        spv = p.dma("scalar",
                    xn_tm_sp[t0:t0 + TILE, :].rearrange("(j p) d -> p j d",
                                                        p=P),
                    xn_tm[par][:, :, :], f"sptm{par}")
        xn_tm_free[par] = (f"sptm{par}", spv)
        p.wait("sync", f"sptm{par}", spv)
        p.waitv("sync", xn_fm_free[par])
        for c in range(DC):
            trv = p.dma("sync", xn_fm[par][:, c, :],
                        xn_tm_sp[t0:t0 + TILE, c * P:(c + 1) * P],
                        f"trf{par}", transpose=True)
        p.wait("scalar", f"trf{par}", trv)
        xnf_v[i] = (f"spf{par}",
                    p.dma("scalar", xnf_sp[i, :, :], xn_fm[par][:, :, :],
                          f"spf{par}"))

        p.wait("tensor", f"trf{par}", trv)
        if i == 0:
            p.wait("tensor", "wq", wq_ld)
            p.wait("tensor", "dve", startup_dve)
        bank_i = [0]

        def next_bank():
            b = bank_i[0] % 6
            bank_i[0] += 1
            return b

        for c in range(DC):
            b = next_bank()
            pset = b % 3
            p.waitv("tensor", bank_free[b])
            for cc in range(DC):
                p.do("tensor", lambda E, b=b, c=c, cc=cc, par=par: E.matmul(
                    pb[b][:, :], wqkv_bf[:, cc, c * P:(c + 1) * P],
                    xn_fm[par][:, cc, :], start=(cc == 0),
                    stop=(cc == DC - 1)),
                    "pe" if cc == DC - 1 else None)
            pev = p.cnt["pe"]
            p.wait("vector", "pe", pev)
            p.waitv("vector", phi_free[pset])
            p.do("vector", lambda E, b=b, c=c, ps=pset: E.tensor_scalar(
                phi_n[ps][:, :], pb[b][:, :], bq_sb[:, c:c + 1], 0.0,
                ALU.add, ALU.min), "dve")
            nv = p.cnt["dve"]
            p.wait("scalar", "dve", nv)
            p.do("scalar", lambda E, ps=pset: E.activation(
                phi_e[ps][:, :], phi_n[ps][:, :], AF.Exp), "act")
            ev = p.cnt["act"]
            p.do("vector", lambda E, b=b, c=c, ps=pset: E.tensor_scalar(
                phi_p[ps][:, :], pb[b][:, :], bq_sb[:, c:c + 1], 0.0,
                ALU.add, ALU.max), "dve")
            pv = p.cnt["dve"]
            bank_free[b] = ("dve", pv)
            p.wait("gpsimd", "act", ev)
            p.wait("gpsimd", "dve", pv)
            if c == 0:
                p.waitv("gpsimd", qt_free[par])
            plv = p.do("gpsimd", lambda E, par=par, c=c, ps=pset:
                       E.tensor_tensor(qt[par][:, c, :], phi_e[ps][:, :],
                                       phi_p[ps][:, :], ALU.add), "pool")
            phi_free[pset] = ("pool", plv)
        q_pool = p.cnt["pool"]

        first_kv_evac = True
        for j in range(NCH):
            for kindv, base_e, dstt in (("k", D, kt), ("v", 2 * D, vt)):
                for eh in range(2):
                    b = next_bank()
                    pset = b % 3
                    p.waitv("tensor", bank_free[b])
                    p.do("tensor", lambda E, b=b, base_e=base_e, eh=eh:
                         E.matmul(pb[b][:, :], ones_rb[0:1, :],
                                  bkv_sb[0:1, base_e - D + eh * TILE:
                                         base_e - D + (eh + 1) * TILE],
                                  start=True, stop=False))
                    for cc in range(DC):
                        p.do("tensor", lambda E, b=b, cc=cc, j=j, par=par,
                             base_e=base_e, eh=eh: E.matmul(
                            pb[b][:, :],
                            xn_fm[par][:, cc, j * P:(j + 1) * P],
                            wqkv_bf[:, cc, base_e + eh * TILE:
                                    base_e + (eh + 1) * TILE],
                            start=False, stop=(cc == DC - 1)),
                            "pe" if cc == DC - 1 else None)
                    pev = p.cnt["pe"]
                    if kindv == "k":
                        p.wait("vector", "pe", pev)
                        p.waitv("vector", phi_free[pset])
                        p.do("vector", lambda E, b=b, ps=pset: E.tensor_scalar(
                            phi_n[ps][:, :], pb[b][:, :], 0.0, None, ALU.min),
                            "dve")
                        nv = p.cnt["dve"]
                        p.wait("scalar", "dve", nv)
                        p.do("scalar", lambda E, ps=pset: E.activation(
                            phi_e[ps][:, :], phi_n[ps][:, :], AF.Exp), "act")
                        ev = p.cnt["act"]
                        p.do("vector", lambda E, b=b, ps=pset: E.tensor_scalar(
                            phi_p[ps][:, :], pb[b][:, :], 0.0, None, ALU.max),
                            "dve")
                        pv = p.cnt["dve"]
                        bank_free[b] = ("dve", pv)
                        p.wait("gpsimd", "act", ev)
                        p.wait("gpsimd", "dve", pv)
                        if first_kv_evac:
                            p.waitv("gpsimd", ktvt_free[par])
                            first_kv_evac = False
                        plv = p.do("gpsimd",
                                   lambda E, par=par, j=j, eh=eh, ps=pset:
                                   E.tensor_tensor(
                                       kt[par][:, j,
                                               eh * TILE:(eh + 1) * TILE],
                                       phi_e[ps][:, :], phi_p[ps][:, :],
                                       ALU.add), "pool")
                        phi_free[pset] = ("pool", plv)
                    else:
                        p.wait("scalar", "pe", pev)
                        if first_kv_evac:
                            p.waitv("scalar", ktvt_free[par])
                            first_kv_evac = False
                        av = p.do("scalar", lambda E, b=b, par=par, j=j, eh=eh:
                                  E.activation(
                                      vt[par][:, j, eh * 8:(eh + 1) * 8, 0:64],
                                      pb[b][:, :], AF.Copy), "act")
                        bank_free[b] = ("act", av)
        kv_ready_pool = p.cnt["pool"]
        kv_ready_act = p.cnt["act"]

        p.wait("tensor", "pool", kv_ready_pool)
        p.wait("tensor", "act", kv_ready_act)
        if i == 0:
            p.do("tensor", lambda E: E.matmul(
                pb[6][:, :], zero_rb[0:1, :], bkv_sb[0:1, 0:TILE],
                start=True, stop=False))
            p.do("tensor", lambda E: E.matmul(
                pb[7][:, :], zero_rb[0:1, :], bkv_sb[0:1, 0:TILE],
                start=True, stop=False))
        for j in range(NCH):
            for h in range(NH):
                pr = h // 2
                bank = 6 if pr < 4 else 7
                po = 64 * (h % 2)
                fo = (pr % 4) * 65
                st = False
                sp_ = False
                lastmm = (j == NCH - 1 and h == NH - 1)
                p.do("tensor", lambda E, bank=bank, po=po, fo=fo, par=par,
                     i=i, j=j, h=h, st=st, sp_=sp_: E.matmul(
                    pb[bank][po:po + 64, fo:fo + 65],
                    kt[par][:, j, h * HD:(h + 1) * HD],
                    vt[par][:, j, h, 0:65],
                    start=st, stop=sp_).annotate(f"kv i{i} j{j} h{h}"),
                    "pe" if lastmm else None)
        if i == NT - 1:
            p.do("tensor", lambda E: E.matmul(
                pb[6][:, :], zero_rb[0:1, :], bkv_sb[0:1, 0:TILE],
                start=False, stop=True))
            p.do("tensor", lambda E: E.matmul(
                pb[7][:, :], zero_rb[0:1, :], bkv_sb[0:1, 0:TILE],
                start=False, stop=True), "pe")
        kvv = p.cnt["pe"]
        ktvt_free[par] = ("pe", kvv)
        xn_fm_free[par] = [("pe", kvv), xnf_v[i]]

        p.wait("scalar", "pool", q_pool)
        q_spill_v[i] = (f"spq{par}",
                        p.dma("scalar", q_sp[:, i, :], qt[par][:, :, :],
                              f"spq{par}"))
        qt_free[par] = q_spill_v[i]


    nc.sbuf_base = _baseA1
    _baseA2 = nc.sbuf_base
    kvk1_sb = S("kvk1_sb", [P, 8 * 65], F32)
    kvg = S("kvg", [P, N_CORES, 8 * 65], F32)
    kvacc = [S(f"kvacc{i}", [P, 8 * 65], F32) for i in range(2)]
    pe_a_done = p.cnt["pe"]
    p.wait("scalar", "pe", pe_a_done)
    p.do("scalar", lambda E: E.activation(
        kvk1_sb[:, 0:260], pb[6][:, 0:260], AF.Copy))
    ev = p.do("scalar", lambda E: E.activation(
        kvk1_sb[:, 260:520], pb[7][:, 0:260], AF.Copy), "act")
    bank_free[6] = ("act", ev)
    bank_free[7] = ("act", ev)
    p.wait("gpsimd", "act", ev)
    p.dma("gpsimd", kvk1_part[:, :], kvk1_sb[:, :], "gb")
    adve = p.cnt["dve"]
    p.wait("gpsimd", "dve", adve)
    mxv = p.dma("gpsimd", maxbounce[0:1, :].rearrange("o p -> p o"),
                maxxn_col[:, :], "gb")
    p.wait("sync", "gb", mxv)
    bv = p.dma("sync", brow[0:1, :], maxbounce[0:1, :], "ldc")
    p.wait("vector", "ldc", bv)
    mv = p.do("vector", lambda E: E.reduce_max(
        scal[0:1, 2:3], brow[0:1, :], axis=AX.X), "dve")
    p.wait("gpsimd", "dve", mv)
    p.dma("gpsimd", mx1_part[:, :], scal[0:1, 2:3], "gb")
    p.wait("gpsimd", "gb", p.cnt["gb"])
    p.do("gpsimd", lambda E: E.collective_compute(
        "AllGather", ALU.bypass, replica_groups=ALLC,
        ins=[kvk1_part[:, :]], outs=[kvk1_gth[:, :]]), "cc")
    ccv1 = p.do("gpsimd", lambda E: E.collective_compute(
        "AllReduce", ALU.max, replica_groups=ALLC,
        ins=[mx1_part[:, :]], outs=[mx1_red[:, :]]), "cc")
    p.wait("gpsimd", "spq0", p.cnt["spq0"])
    p.wait("gpsimd", "spq1", p.cnt["spq1"])
    ccq = p.do("gpsimd", lambda E: E.collective_compute(
        "AllGather", ALU.bypass, replica_groups=ALLC,
        ins=[q_sp.rearrange("p i f -> p (i f)")],
        outs=[qx_gth[:, :]]), "cc")
    p.wait("gpsimd", "cc", ccv1)
    p.dma("gpsimd", kvg[:, :, :],
          kvk1_gth.rearrange("(c p) f -> p c f", p=P), "gb")
    gmxv = p.dma("gpsimd", scal[0:1, 2:3], mx1_red[:, :], "gb")
    p.wait("vector", "gb", gmxv)
    p.seq("vector", lambda E: E.tensor_scalar(
        kvacc[0][:, :], kvg[:, 0, :], pmask[:, 0:1], None, ALU.mult))
    for c in range(1, N_CORES):
        a, b_ = kvacc[(c - 1) % 2], kvacc[c % 2]
        p.fence("vector")
        p.seq("vector", lambda E, a=a, b_=b_, c=c: E.scalar_tensor_tensor(
            b_[:, :], kvg[:, c, :], pmask[:, c:c + 1], a[:, :],
            ALU.mult, ALU.add))
    kvfin = kvacc[(N_CORES - 1) % 2]
    kvmy = kvacc[(N_CORES) % 2]
    p.fence("vector")
    for c in range(4):
        p.seq("vector", lambda E, c=c: E.tensor_scalar(
            kvmy[:, c * 65:(c + 1) * 65], kvfin[:, c * 65:(c + 1) * 65],
            sel2[:, 0:1], None, ALU.mult))
        p.fence("vector")
        p.seq("vector", lambda E, c=c: E.scalar_tensor_tensor(
            kvmy[:, c * 65:(c + 1) * 65],
            kvfin[:, (4 + c) * 65:(5 + c) * 65], sel2[:, 1:2],
            kvmy[:, c * 65:(c + 1) * 65], ALU.mult, ALU.add))
        p.fence("vector")
        p.do("vector", lambda E, c=c: E.tensor_copy(
            kv_bf[:, c, :], kvmy[:, c * 65:(c + 1) * 65]))
        p.do("vector", lambda E, c=c: E.tensor_scalar(
            k1m[:, c, :], em2p[:, :],
            kvmy[:, c * 65 + 64: c * 65 + 65], None, ALU.mult))
    p.wait("vector", "dve", sig_dve)
    p.seq("vector", lambda E: E.tensor_scalar(
        scal[0:1, 3:4], scal[0:1, 2:3], scal[0:1, 0:1], 1.0 / QMAX,
        ALU.mult, ALU.mult))
    p.fence("vector")
    kvdve = p.do("vector", lambda E: E.reciprocal(scal[0:1, 4:5],
                                                  scal[0:1, 3:4]), "dve")
    p.wait("tensor", "dve", kvdve)
    p.waitv("tensor", bank_free[0])
    p.do("tensor", lambda E: E.matmul(
        pb[0][:, 0:1], ones_rf[0:1, :], scal[0:1, 4:5], start=True, stop=True))
    bcv = p.do("tensor", lambda E: E.matmul(
        pb[0][:, 1:2], ones_rf[0:1, :], scal[0:1, 3:4], start=True,
        stop=True), "pe")
    p.wait("scalar", "pe", bcv)
    p.do("scalar", lambda E: E.activation(rx_col[:, :], pb[0][:, 0:1],
                                          AF.Copy))
    bco = p.do("scalar", lambda E: E.activation(sx_col[:, :], pb[0][:, 1:2],
                                                AF.Copy), "act")
    bank_free[0] = ("act", bco)
    p.wait("vector", "act", bco)
    p.fence("vector")
    bar1_kv_dve = p.cnt["dve"]
    for job in prep_jobs[:2 * IC]:
        gen_prep(job)
    prep_gs_gu = p.cnt["gs"]

    p.fence("vector")
    p.do("vector", lambda E: E.tensor_scalar(
        cg_cols[:, :], sg_cols[:, :], sx_col[:, :], None, ALU.mult))
    p.do("vector", lambda E: E.tensor_scalar(
        cu_cols[:, :], su_cols[:, :], sx_col[:, :], None, ALU.mult), "dve")
    bar1_dve = p.cnt["dve"]
    if DEBUG:
        dsc2 = nc.dram_tensor("dbg_scal2", [1, 16], F32,
                              kind="ExternalOutput").ap()
        p.wait("gpsimd", "dve", bar1_dve)
        p.dma("gpsimd", dsc2, scal[0:1, :], "gb")

    nc.sbuf_base = _baseA2
    _baseB = nc.sbuf_base
    qg_bf = [S(f"qg_bf{c}", [P, IFF], BF16) for c in range(DC)]
    S("qg_pad", [P, 4096], mybir.dt.uint8)
    qu_bf = [S(f"qu_bf{c}", [P, IFF], BF16) for c in range(DC)]
    canary = S("canary", [P, 1024], F32)
    p.wait("sync", "gs", prep_gs_gu)
    p.wait("sync", "pe", pe_a_done)
    p.wait("sync", "dve", bar1_kv_dve)
    for s_ in ("sptm0", "sptm1", "spf0", "spf1", "spq0", "spq1"):
        p.wait("sync", s_, p.cnt[s_])
    for wsp, wbf in ((qg_sp, qg_bf), (qu_sp, qu_bf)):
        for c in range(DC):
            p.dma("sync", wbf[c][:, :], wsp[:, c * P:(c + 1) * P], "tqg",
                  transpose=True)
    qgu_ld = p.cnt["tqg"]
    if DEBUG:
        dbg_qgb = nc.dram_tensor("dbg_qgb", [DC * P, IFF], BF16,
                                 kind="ExternalOutput").ap()
        dbg_qub = nc.dram_tensor("dbg_qub", [DC * P, IFF], BF16,
                                 kind="ExternalOutput").ap()
        p.wait("gpsimd", "tqg", qgu_ld)
        for c in range(DC):
            p.dma("gpsimd", dbg_qgb[c * P:(c + 1) * P, :], qg_bf[c][:, :],
                  "gb")
            p.dma("gpsimd", dbg_qub[c * P:(c + 1) * P, :], qu_bf[c][:, :],
                  "gb")
        qgb_dump = p.cnt["gb"]
        p.wait("tensor", "gb", qgb_dump)
    p.wait("vector", "dve", bar1_dve)
    p.do("vector", lambda E: E.memset(canary[:, :], 1234.5), "dve")
    canary_set = p.cnt["dve"]
    for job in prep_jobs[2 * IC:]:
        gen_prep(job)
    prep_gs = p.cnt["gs"]

    hb = [S("hb0", [P, IC, TILE], BF16)]
    xnb = [S(f"xnb{i}", [P, DC * TILE], BF16) for i in range(2)]
    ixb = [S(f"ixb{j}", [P, DC, TILE], BF16) for j in range(2)]
    qf_scr = [S(f"qf_scr{j}", [P, TILE], F32) for j in range(2)]
    qi_scr = [S(f"qi_scr{j}", [P, TILE], I32) for j in range(2)]
    sg_scr = S("sg_scr", [P, TILE], F32)
    su_scr = S("su_scr", [P, TILE], F32)
    hf_scr = S("hf_scr", [P, TILE], F32)

    xnb_free = [None, None]
    hb_free = [None]
    hsp_v = [None] * NT
    scr_free = None
    pe_up_last = None
    qfB_free = [None, None]
    qiB_free = [None, None]
    ixb_guard = [None, None]
    if DEBUG:
        dbg_ix = nc.dram_tensor("dbg_ix", [NT, P, DC * TILE], BF16,
                                kind="ExternalOutput").ap()
        dbg_xnb = nc.dram_tensor("dbg_xnb", [NT, P, DC * TILE], BF16,
                                 kind="ExternalOutput").ap()
        dbg_ix_done = None

    for i in range(NT):
        par = i % 2
        p.waitv("sync", xnb_free[par])
        p.waitv("sync", xnf_v[i])
        xv = p.dma("sync", xnb[par][:, :], xnf_sp[i, :, :], f"lb{par}")
        if DEBUG:
            p.wait("gpsimd", f"lb{par}", xv)
            xnbd = p.dma("gpsimd", dbg_xnb[i, :, :], xnb[par][:, :], "gb")
        p.wait("vector", f"lb{par}", xv)
        if DEBUG:
            p.wait("vector", "gb", xnbd)
        for c in range(DC):
            s = c % 2
            p.waitv("vector", qfB_free[s])
            mv = p.do("vector", lambda E, par=par, c=c, s=s: E.tensor_scalar(
                qf_scr[s][:, :], xnb[par][:, c * TILE:(c + 1) * TILE],
                rx_col[:, :], None, ALU.mult), "dve")
            p.wait("scalar", "dve", mv)
            p.waitv("scalar", qiB_free[s])
            av = p.do("scalar", lambda E, s=s: E.activation(
                qi_scr[s][:, :], qf_scr[s][:, :], AF.Copy), "act")
            qfB_free[s] = ("act", av)
            p.wait("gpsimd", "act", av)
            if c == 0:
                p.waitv("gpsimd", ixb_guard[par])
            cv = p.do("gpsimd", lambda E, c=c, s=s, par=par: E.tensor_scalar(
                ixb[par][:, c, :], qi_scr[s][:, :], 127.0, -128.0, ALU.min,
                ALU.max), "pool")
            qiB_free[s] = ("pool", cv)
        ix_pool = p.cnt["pool"]
        xnb_free[par] = ("dve", p.cnt["dve"])
        if DEBUG:
            dbg_ix_done = p.dma("gpsimd", dbg_ix[i, :, :],
                                ixb[par][:, :, :], "gb")
        p.wait("tensor", "pool", ix_pool)
        if DEBUG:
            p.wait("tensor", "gb", dbg_ix_done)
        if i == 0:
            p.wait("tensor", "tqg", qgu_ld)
        p.waitv("vector", hb_free[0])
        for r in range(IC):
            bg = (2 * r) % 8
            bu = (2 * r + 1) % 8
            p.waitv("tensor", bank_free[bg])
            for cc in range(DC):
                p.do("tensor", lambda E, bg=bg, r=r, cc=cc, par=par: E.matmul(
                    pb[bg][:, :], qg_bf[cc][:, r * P:(r + 1) * P],
                    ixb[par][:, cc, :], start=(cc == 0), stop=(cc == DC - 1)),
                    None)
            p.waitv("tensor", bank_free[bu])
            for cc in range(DC):
                p.do("tensor", lambda E, bu=bu, r=r, cc=cc, par=par: E.matmul(
                    pb[bu][:, :], qu_bf[cc][:, r * P:(r + 1) * P],
                    ixb[par][:, cc, :], start=(cc == 0), stop=(cc == DC - 1)),
                    "pe" if cc == DC - 1 else None)
            uv = p.cnt["pe"]
            if r == IC - 1:
                pe_up_last = uv
                ixb_guard[par] = ("pe", uv)
            p.wait("scalar", "pe", uv)
            if scr_free is not None:
                p.waitv("scalar", scr_free)
            av = p.do("scalar", lambda E, bg=bg, r=r: E.activation(
                sg_scr[:, :], pb[bg][:, :], AF.Silu,
                scale=cg_cols[:, r:r + 1]), "act")
            bank_free[bg] = ("act", av)
            p.wait("vector", "act", av)
            p.fence("vector")
            uev = p.do("vector", lambda E, bu=bu, r=r: E.tensor_scalar(
                su_scr[:, :], pb[bu][:, :], cu_cols[:, r:r + 1], None,
                ALU.mult), "dve")
            bank_free[bu] = ("dve", uev)
            p.fence("vector")
            p.seq("vector", lambda E: E.tensor_mul(
                hf_scr[:, :], sg_scr[:, :], su_scr[:, :]))
            p.fence("vector")
            p.seq("vector", lambda E: E.reduce_max(
                red_tmp[:, 0:1], hf_scr[:, :], axis=AX.X,
                apply_absolute_value=True))
            p.fence("vector")
            p.seq("vector", lambda E: E.tensor_max(
                maxh_col[:, :], maxh_col[:, :], red_tmp[:, 0:1]))
            hcv = p.do("vector", lambda E, par=par, r=r: E.tensor_copy(
                hb[0][:, r, :], hf_scr[:, :]), "dve")
            scr_free = ("dve", hcv)
        h_dve = p.cnt["dve"]
        p.wait("scalar", "dve", h_dve)
        hsp_v[i] = (f"sph{par}",
                    p.dma("scalar", h_sp[i, :, :], hb[0][:, :, :],
                          f"sph{par}"))
        hb_free[0] = hsp_v[i]

    if DEBUG:
        dbg_canB = nc.dram_tensor("dbg_canB", [P, 1024], F32,
                                  kind="ExternalOutput").ap()
        p.wait("gpsimd", "sph0", p.cnt["sph0"])
        p.wait("gpsimd", "sph1", p.cnt["sph1"])
        p.dma("gpsimd", dbg_canB, canary[:, :], "gb")
    bdve = p.cnt["dve"]
    p.wait("gpsimd", "dve", bdve)
    mxv2 = p.dma("gpsimd", maxbounce[0:1, :].rearrange("o p -> p o"),
                 maxh_col[:, :], "gb")
    p.wait("sync", "gb", mxv2)
    bv2 = p.dma("sync", brow[0:1, :], maxbounce[0:1, :], "ldc")
    p.wait("vector", "ldc", bv2)
    mv2 = p.do("vector", lambda E: E.reduce_max(
        scal[0:1, 5:6], brow[0:1, :], axis=AX.X), "dve")
    p.wait("gpsimd", "dve", mv2)
    p.dma("gpsimd", mx2_part[:, :], scal[0:1, 5:6], "gb")
    p.wait("gpsimd", "gb", p.cnt["gb"])
    ccv2 = p.do("gpsimd", lambda E: E.collective_compute(
        "AllReduce", ALU.max, replica_groups=ALLC,
        ins=[mx2_part[:, :]], outs=[mx2_red[:, :]]), "cc")
    p.wait("gpsimd", "cc", ccv2)
    gmx2 = p.dma("gpsimd", scal[0:1, 5:6], mx2_red[:, :], "gb")
    p.wait("vector", "gb", gmx2)
    p.seq("vector", lambda E: E.tensor_scalar(
        scal[0:1, 6:7], scal[0:1, 5:6], scal[0:1, 1:2], 1.0 / QMAX,
        ALU.mult, ALU.mult))
    p.fence("vector")
    sdv = p.do("vector", lambda E: E.reciprocal(scal[0:1, 7:8],
                                                scal[0:1, 6:7]), "dve")
    p.wait("tensor", "dve", sdv)
    p.waitv("tensor", bank_free[0])
    p.do("tensor", lambda E: E.matmul(
        pb[0][:, 0:1], ones_rf[0:1, :], scal[0:1, 7:8], start=True,
        stop=True))
    bcv2 = p.do("tensor", lambda E: E.matmul(
        pb[0][:, 1:2], ones_rf[0:1, :], scal[0:1, 6:7], start=True,
        stop=True), "pe")
    p.wait("scalar", "pe", bcv2)
    p.do("scalar", lambda E: E.activation(rd_col[:, :], pb[0][:, 0:1],
                                          AF.Copy))
    bco2 = p.do("scalar", lambda E: E.activation(sd_col[:, :], pb[0][:, 1:2],
                                                 AF.Copy), "act")
    bank_free[0] = ("act", bco2)
    p.wait("vector", "act", bco2)
    p.fence("vector")
    p.do("vector", lambda E: E.tensor_scalar(
        cd_cols[:, :], sd_cols[:, :], sd_col[:, :], None, ALU.mult), "dve")
    bar2_dve = p.cnt["dve"]

    nc.sbuf_base = _baseB
    qd_bf = [S(f"qd_bf{c}", [P, D], BF16) for c in range(IC)]
    S("qd_pad", [P, 4096], mybir.dt.uint8)
    _baseC1 = nc.sbuf_base
    pe_b_done = p.cnt["pe"]
    p.wait("sync", "pe", pe_b_done)
    p.wait("sync", "gs", prep_gs)
    p.wait("sync", "sph0", p.cnt["sph0"])
    p.wait("sync", "sph1", p.cnt["sph1"])
    p.wait("sync", "lb0", p.cnt["lb0"])
    p.wait("sync", "lb1", p.cnt["lb1"])
    for c in range(IC):
        p.dma("sync", qd_bf[c][:, :], qd_sp[:, c * P:(c + 1) * P], "tqd",
              transpose=True)
    qd_ld = p.cnt["tqd"]

    qca = [S("qca0", [P, 2 * NT, TILE], BF16)]
    qtil = [S(f"qtil{i}", [P, 2 * NT, TILE], BF16) for i in range(2)]
    xnc = [S(f"xnc{i}", [P, DC * TILE], BF16) for i in range(2)]
    rb_f = S("rb_f", [P, TILE], F32)
    rb_f2 = S("rb_f2", [P, TILE], F32)
    rb_b = S("rb_b", [P, 6 * TILE], BF16)
    rbx = [S(f"rbx{i}", [P, TILE], BF16) for i in range(2)]
    attn_b = S("attn_b", [P, DC, TILE], BF16)
    z_b = S("z_b", [P, DC, TILE], BF16)
    zsq = S("zsq", [P, TILE], BF16)
    srow = S("srow", [1, TILE], F32)
    sqrow = S("sqrow", [1, TILE], F32)
    murow = S("murow", [1, TILE], F32)
    varrow = S("varrow", [1, TILE], F32)
    rstdrow = S("rstdrow", [1, TILE], F32)
    mu_bb = S("mu_bb", [1, TILE], BF16)
    rstd_bb = S("rstd_bb", [1, TILE], BF16)
    mub = S("mub", [P, TILE], BF16)
    rstdb = S("rstdb", [P, TILE], BF16)
    ln2_t = S("ln2_t", [P, TILE], F32)
    ln2_u = S("ln2_u", [P, TILE], F32)
    sb_t = [S(f"sb_t{i}", [P, DC, TILE], BF16) for i in range(2)]
    wproj_bf = S("wproj_bf", [P, DC, D], BF16)
    em2h = S("em2h", [P, 2 * NT * P], BF16)
    emv = p.dma("sync", em2h[:, :], em2h_in, "tqd")
    p.wait("gpsimd", "pe", pe_b_done)
    p.wait("gpsimd", "gs", prep_gs)
    p.wait("gpsimd", "sph0", p.cnt["sph0"])
    p.wait("gpsimd", "sph1", p.cnt["sph1"])
    p.wait("gpsimd", "lb0", p.cnt["lb0"])
    p.wait("gpsimd", "lb1", p.cnt["lb1"])
    for _c in range(DC):
        p.dma("gpsimd", wproj_bf[:, _c, :], wproj_t[_c * P:(_c + 1) * P, :],
              "wp")
    wp_ld = p.cnt["wp"]
    p.wait("tensor", "wp", wp_ld)
    p.wait("tensor", "tqd", emv)

    if DEBUG:
        dbg_z = nc.dram_tensor("dbg_z", [NT, P, DC * TILE], BF16,
                               kind="ExternalOutput").ap()
        dbg_z_done = None
    xnc_free = [None, None]
    sbt_free = [None, None]
    rbx_free = [None, None]
    qca_free = [None, None]
    qtil_free = [None, None]
    ssp_v = [None] * NT
    prev_sdve = None

    regA = nc.alloc_register(mybir.EngineType.SP, "rq_ev")
    regB = nc.alloc_register(mybir.EngineType.SP, "rq_od")
    regGC = [nc.alloc_register(mybir.EngineType.SP, f"rq_gc{i}")
             for i in range(4)]
    holder = {}

    def _dynsetup(E):
        pid = E.partition_id()
        E.reg_mov(regA, pid)
        E.reg_alu(regA, regA, 6, ALU.bitwise_and)
        E.reg_alu(regA, regA, P, ALU.mult)
        E.reg_mov(regB, pid)
        E.reg_alu(regB, regB, 1, ALU.bitwise_or)
        E.reg_alu(regB, regB, P, ALU.mult)
        holder["ev"] = E.snap(regA, min_val=0, max_val=(N_CORES - 2) * P)
        holder["od"] = E.snap(regB, min_val=0, max_val=(N_CORES - 1) * P)
        for cc_ in range(4):
            rg = regGC[cc_]
            E.reg_mov(rg, pid)
            E.reg_alu(rg, rg, 1, ALU.bitwise_and)
            E.reg_alu(rg, rg, 4, ALU.mult)
            if cc_ > 0:
                E.reg_alu(rg, rg, cc_, ALU.add)
            holder[f"gc{cc_}"] = E.snap(rg, min_val=0, max_val=DC - 1)

    p.do("sync", _dynsetup)
    p.wait("sync", "cc", ccq)

    for c in range(4):
        parc = c % 2
        qci = 0
        p.waitv("sync", qca_free[qci])
        qx4 = qx_gth.rearrange("r (i g t) -> r i g t", i=NT, g=DC)
        for ki, key in enumerate(("ev", "od")):
            qcav = p.do(
                "sync",
                lambda E, ki=ki, key=key, qci=qci, c=c: E.dma_start(
                    out=qca[qci][:, ki * NT:(ki + 1) * NT, :],
                    in_=qx4[bass.ds(holder[key], P), 0:NT,
                            bass.ds(holder[f"gc{c}"], 1), :]),
                f"lq{parc}", 16)
        p.wait("vector", f"lq{parc}", qcav)
        p.wait("tensor", f"lq{parc}", qcav)
        if c == 0:
            p.wait("tensor", "dve", bar1_dve)
        p.waitv("tensor", qtil_free[parc])
        for b in range(6):
            p.waitv("tensor", bank_free[b])
            p.do("tensor", lambda E, b=b: E.matmul(
                pb[b][:, :], zero_rb[0:1, :], bkv_sb[0:1, 0:TILE],
                start=True, stop=False))
            for k in range(3):
                g = 3 * b + k
                if g >= 2 * NT:
                    break
                p.do("tensor", lambda E, b=b, c=c, g=g, k=k: E.matmul(
                    pb[b][32 * k:32 * k + 2, :], k1m[:, c, :],
                    qca[qci][:, g, :], start=False, stop=False))
            qk1v = p.do("tensor", lambda E, b=b: E.matmul(
                pb[b][:, :], zero_rb[0:1, :], bkv_sb[0:1, 0:TILE],
                start=False, stop=True), "pe")
            p.wait("vector", "pe", qk1v)
            p.fence("vector")
            p.seq("vector", lambda E, b=b: E.tensor_scalar(
                rb_f[:, :], pb[b][:, :], 1e-6, None, ALU.max))
            p.fence("vector")
            p.seq("vector", lambda E: E.reciprocal(rb_f2[:, :], rb_f[:, :]))
            p.fence("vector")
            rbv = p.do("vector", lambda E, b=b: E.tensor_copy(
                rb_b[:, b * TILE:(b + 1) * TILE], rb_f2[:, :]), "dve")
            bank_free[b] = ("dve", rbv)
        p.wait("tensor", "dve", p.cnt["dve"])
        for g in range(2 * NT):
            b1 = 4 + (g % 2)
            p.waitv("tensor", bank_free[b1])
            rbm = p.do("tensor", lambda E, b=b1, g=g: E.matmul(
                pb[b][:, :], em2h[:, g * P:(g + 1) * P],
                rb_b[:, (g // 3) * TILE:(g // 3 + 1) * TILE],
                start=True, stop=True), "pe")
            rset = g % 2
            p.wait("scalar", "pe", rbm)
            p.waitv("scalar", rbx_free[rset])
            av = p.do("scalar", lambda E, b=b1, rs=rset: E.activation(
                rbx[rs][:, :], pb[b][:, :], AF.Copy), "act")
            bank_free[b1] = ("act", av)
            p.wait("vector", "act", av)
            qtv = p.do("vector", lambda E, parc=parc, g=g, rs=rset:
                       E.tensor_tensor(qtil[parc][:, g, :],
                                       qca[qci][:, g, :], rbx[rs][:, :],
                                       ALU.mult), "dve")
            rbx_free[rset] = ("dve", qtv)
        qt_dve = p.cnt["dve"]
        qca_free[qci] = ("dve", qt_dve)
        p.wait("tensor", "dve", qt_dve)

        for hh in range(2):
            bh = 2 * c + hh
            par = bh % 2
            p.waitv("sync", xnc_free[par])
            p.waitv("sync", xnf_v[bh])
            xv = p.dma("sync", xnc[par][:, :], xnf_sp[bh, :, :], f"lxn{par}")
            for cc in range(DC):
                bq_ = 2 + (cc % 2)
                p.waitv("tensor", bank_free[bq_])
                for u in range(2):
                    jj = 2 * cc + u
                    p.do("tensor", lambda E, b=bq_, c=c, hh=hh, u=u, jj=jj,
                         parc=parc: E.matmul(
                        pb[b][u * 64:(u + 1) * 64, :],
                        kv_bf[hh * 64:(hh + 1) * 64, c, 0:64],
                        qtil[parc][hh * 64:(hh + 1) * 64, :, jj::16],
                        start=True, stop=True),
                        "pe" if u == 1 else None)
                qov = p.cnt["pe"]
                p.wait("vector", "pe", qov)
                p.fence("vector")
                dv = p.do("vector", lambda E, b=bq_, cc=cc: E.tensor_copy(
                    attn_b[:, cc, :], pb[b][:, :]), "dve")
                bank_free[bq_] = ("dve", dv)
            attn_dve = p.cnt["dve"]
            p.wait("tensor", "dve", attn_dve)
            for cc2 in range(DC):
                bz = 6 + (cc2 % 2)
                p.waitv("tensor", bank_free[bz])
                for cc in range(DC):
                    p.do("tensor", lambda E, b=bz, c2=cc2, cc=cc: E.matmul(
                        pb[b][:, :], wproj_bf[:, cc, c2 * P:(c2 + 1) * P],
                        attn_b[:, cc, :], start=(cc == 0),
                        stop=(cc == DC - 1)),
                        "pe" if cc == DC - 1 else None)
                zv = p.cnt["pe"]
                p.wait("vector", "pe", zv)
                if cc2 == 0 and prev_sdve is not None:
                    p.wait("vector", "dve", prev_sdve)
                if DEBUG and cc2 == 0 and dbg_z_done is not None:
                    p.wait("vector", "gb", dbg_z_done)
                av = p.do("vector", lambda E, b=bz, c2=cc2: E.tensor_scalar(
                    z_b[:, c2, :], pb[b][:, :], bp_sb[:, c2:c2 + 1], None,
                    ALU.add), "dve")
                bank_free[bz] = ("dve", av)
            z_act = p.cnt["dve"]
            bsu, bsq = (bh % 2), 2 + (bh % 2)
            p.wait("tensor", "dve", z_act)
            p.waitv("tensor", bank_free[bsu])
            for cc in range(DC):
                p.do("tensor", lambda E, b=bsu, cc=cc: E.matmul(
                    pb[b][0:1, :], ones_cb[:, :], z_b[:, cc, :],
                    start=(cc == 0), stop=(cc == DC - 1)),
                    "pe" if cc == DC - 1 else None)
            p.waitv("tensor", bank_free[bsq])
            sqmv_prev_pe = None
            p.wait("scalar", "dve", z_act)
            for cc in range(DC):
                if sqmv_prev_pe is not None:
                    p.wait("scalar", "pe", sqmv_prev_pe)
                sqv = p.do("scalar", lambda E, cc=cc: E.activation(
                    zsq[:, :], z_b[:, cc, :], AF.Square), "act")
                p.wait("tensor", "act", sqv)
                sqmv_prev_pe = p.do("tensor", lambda E, b=bsq, cc=cc:
                                    E.matmul(pb[b][0:1, :], ones_cb[:, :],
                                             zsq[:, :], start=(cc == 0),
                                             stop=(cc == DC - 1)), "pe")
            sqmv = p.cnt["pe"]
            p.wait("vector", "pe", sqmv)
            p.seq("vector", lambda E, b=bsu: E.tensor_scalar(
                murow[0:1, :], pb[b][0:1, :], 1.0 / D, None, ALU.mult))
            p.seq("vector", lambda E, b=bsq: E.tensor_scalar(
                varrow[0:1, :], pb[b][0:1, :], 1.0 / D, LN_EPS, ALU.mult,
                ALU.add))
            p.fence("vector")
            p.seq("vector", lambda E: E.tensor_tensor(
                srow[0:1, :], murow[0:1, :], murow[0:1, :], ALU.mult))
            p.fence("vector")
            muv = p.do("vector", lambda E: E.tensor_sub(
                varrow[0:1, :], varrow[0:1, :], srow[0:1, :]), "dve")
            bank_free[bsu] = ("dve", muv)
            bank_free[bsq] = ("dve", muv)
            p.wait("scalar", "dve", muv)
            p.seq("scalar", lambda E: E.activation(
                sqrow[0:1, :], varrow[0:1, :], AF.Ln))
            p.fence("scalar")
            rsv = p.do("scalar", lambda E: E.activation(
                rstdrow[0:1, :], sqrow[0:1, :], AF.Exp, scale=-0.5), "act")
            p.wait("vector", "act", rsv)
            p.fence("vector")
            p.do("vector", lambda E: E.tensor_copy(mu_bb[0:1, :],
                                                   murow[0:1, :]))
            mbv = p.do("vector", lambda E: E.tensor_copy(
                rstd_bb[0:1, :], rstdrow[0:1, :]), "dve")
            p.wait("tensor", "dve", mbv)
            p.waitv("tensor", bank_free[bsu])
            p.waitv("tensor", bank_free[bsq])
            p.do("tensor", lambda E, b=bsu: E.matmul(
                pb[b][:, :], ones_rb[0:1, :], mu_bb[0:1, :], start=True,
                stop=True))
            bb2 = p.do("tensor", lambda E, b=bsq: E.matmul(
                pb[b][:, :], ones_rb[0:1, :], rstd_bb[0:1, :], start=True,
                stop=True), "pe")
            p.wait("scalar", "pe", bb2)
            p.do("scalar", lambda E, b=bsu: E.activation(
                mub[:, :], pb[b][:, :], AF.Copy))
            bb3 = p.do("scalar", lambda E, b=bsq: E.activation(
                rstdb[:, :], pb[b][:, :], AF.Copy), "act")
            bank_free[bsu] = ("act", bb3)
            bank_free[bsq] = ("act", bb3)
            p.wait("vector", "act", bb3)
            p.wait("vector", f"lxn{par}", xv)
            p.waitv("vector", sbt_free[par])
            p.fence("vector")
            for cc in range(DC):
                p.seq("vector", lambda E, cc=cc: E.scalar_tensor_tensor(
                    ln2_t[:, :], z_b[:, cc, :], 1.0, mub[:, :], ALU.mult,
                    ALU.subtract))
                p.fence("vector")
                p.seq("vector", lambda E: E.tensor_mul(
                    ln2_t[:, :], ln2_t[:, :], rstdb[:, :]))
                p.fence("vector")
                p.seq("vector", lambda E, cc=cc: E.tensor_scalar(
                    ln2_u[:, :], ln2_t[:, :], gam_sb[:, cc:cc + 1],
                    bet_sb[:, cc:cc + 1], ALU.mult, ALU.add))
                p.fence("vector")
                p.seq("vector", lambda E, cc=cc, par=par: E.tensor_tensor(
                    sb_t[par][:, cc, :], ln2_u[:, :],
                    xnc[par][:, cc * TILE:(cc + 1) * TILE], ALU.add))
            sdve = p.cnt["dve"]
            if DEBUG:
                p.wait("gpsimd", "dve", sdve)
                dbg_z_done = p.dma("gpsimd", dbg_z[bh, :, :], z_b[:, :, :],
                                   "gb")
            prev_sdve = sdve
            xnc_free[par] = ("dve", sdve)
            p.wait("scalar", "dve", sdve)
            ssp_v[bh] = (f"sps{par}",
                         p.dma("scalar", s_sp[bh, :, :], sb_t[par][:, :, :],
                               f"sps{par}"))
            sbt_free[par] = ssp_v[bh]
        qtil_free[parc] = ("pe", p.cnt["pe"])

    nc.sbuf_base = _baseC1
    hc = [S(f"hc{i}", [P, IC * TILE], BF16) for i in range(2)]
    qf2_scr = [S("qf2_scr0", [P, TILE], F32)]
    qi2_scr = [S("qi2_scr0", [P, TILE], I32)]
    ihb = [S(f"ihb{j}", [P, IC, TILE], BF16) for j in range(2)]
    sc_t = [S(f"sc_t{i}", [P, DC * TILE], BF16) for i in range(2)]
    g_scr = S("g_scr", [P, TILE], BF16)
    p_scr = [S(f"p_scr{i}", [P, TILE], BF16) for i in range(2)]
    ptm = []
    for _i in range(2):
        ptm.append(S(f"ptm{_i}", [P, NCH, 2, TILE], BF16))
        S(f"ptm_pad{_i}", [P, 4096], mybir.dt.uint8)
    xc2 = [S(f"xc2{i}", [P, D], F32) for i in range(2)]
    yst = [S(f"yst{i}", [P, D], F32) for i in range(2)]

    hc_free = [None, None]
    sct_free = [None, None]
    pscr_free = [None, None]
    ptm_free = [None, None]
    xc2_free = [None] * 2
    yst_free = [None, None]
    gscr_free = None
    qfC_free = [None, None]
    qiC_free = [None, None]
    ihb_guard = [None, None]
    spp_rec = [None] * NT

    def c2_assembly(k):
        par = k % 2
        t0 = k * TILE
        p.waitv("sync", ptm_free[par])
        p.wait("sync", "spp0", spp_rec[k][0])
        p.wait("sync", "spp1", spp_rec[k][1])
        ptv = None
        for j in range(NCH):
            for eh in range(2):
                ptv = p.dma("sync", ptm[par][:, j, eh, :],
                            p_sp[eh * TILE:(eh + 1) * TILE,
                                 t0 + j * P: t0 + (j + 1) * P],
                            f"tpt{par}", transpose=True)
        yv = None
        for j in range(NCH):
            slot = j % 2
            p.waitv("sync", xc2_free[slot])
            xvv = p.dma("sync", xc2[slot][:, :],
                        x_in[t0 + j * P: t0 + (j + 1) * P, :], f"lx{slot}")
            yb = j % 2
            p.waitv("vector", yst_free[yb])
            p.wait("vector", f"lx{slot}", xvv)
            p.wait("vector", f"tpt{par}", ptv)
            yv = p.do("vector", lambda E, s=slot, par=par, j=j, yb=yb:
                      E.tensor_tensor(yst[yb][:, :], xc2[s][:, :],
                                      ptm[par][:, j, :, :], ALU.add), "dve")
            xc2_free[slot] = ("dve", yv)
            p.wait("scalar", "dve", yv)
            ov = p.dma("scalar", y_out[t0 + j * P: t0 + (j + 1) * P, :],
                       yst[yb][:, :], f"spy{yb}")
            yst_free[yb] = (f"spy{yb}", ov)
        ptm_free[par] = ("dve", yv)
    pe_down_last = None

    for i in range(NT):
        par = i % 2
        t0 = i * TILE
        if i == 0:
            p.wait("sync", "sps0", p.cnt["sps0"])
            p.wait("sync", "sps1", p.cnt["sps1"])
        p.waitv("sync", hc_free[par])
        p.waitv("sync", hsp_v[i])
        hv = p.dma("sync", hc[par][:, :], h_sp[i, :, :], f"lh{par}")
        p.waitv("sync", sct_free[par])
        p.waitv("sync", ssp_v[i])
        sv_ = p.dma("sync", sc_t[par][:, :], s_sp[i, :, :], f"ls{par}")
        p.wait("vector", f"lh{par}", hv)
        if i == 0:
            p.wait("vector", "dve", bar2_dve)
        for r in range(IC):
            s = 0
            p.waitv("vector", qfC_free[s])
            mv = p.do("vector", lambda E, par=par, r=r, s=s: E.tensor_scalar(
                qf2_scr[s][:, :], hc[par][:, r * TILE:(r + 1) * TILE],
                rd_col[:, :], None, ALU.mult), "dve")
            p.wait("scalar", "dve", mv)
            p.waitv("scalar", qiC_free[s])
            av = p.do("scalar", lambda E, s=s: E.activation(
                qi2_scr[s][:, :], qf2_scr[s][:, :], AF.Copy), "act")
            qfC_free[s] = ("act", av)
            p.wait("gpsimd", "act", av)
            if r == 0:
                p.waitv("gpsimd", ihb_guard[par])
            cv = p.do("gpsimd", lambda E, r=r, s=s, par=par: E.tensor_scalar(
                ihb[par][:, r, :], qi2_scr[s][:, :], 127.0, -128.0, ALU.min,
                ALU.max), "pool")
            qiC_free[s] = ("pool", cv)
        ih_pool = p.cnt["pool"]
        hc_free[par] = ("dve", p.cnt["dve"])
        p.wait("tensor", "pool", ih_pool)
        if i == 0:
            p.wait("tensor", "tqd", qd_ld)
        for c in range(DC):
            bz = 4 + (c % 4)
            p.waitv("tensor", bank_free[bz])
            for rr in range(IC):
                p.do("tensor", lambda E, b=bz, c=c, rr=rr, par=par: E.matmul(
                    pb[b][:, :], qd_bf[rr][:, c * P:(c + 1) * P],
                    ihb[par][:, rr, :], start=(rr == 0), stop=(rr == IC - 1)),
                    "pe" if rr == IC - 1 else None)
            dvn = p.cnt["pe"]
            if c == DC - 1:
                pe_down_last = dvn
                ihb_guard[par] = ("pe", dvn)
            p.wait("scalar", "pe", dvn)
            if gscr_free is not None:
                p.waitv("scalar", gscr_free)
            av = p.do("scalar", lambda E, b=bz, c=c: E.activation(
                g_scr[:, :], pb[b][:, :], AF.Silu,
                scale=cd_cols[:, c:c + 1]), "act")
            bank_free[bz] = ("act", av)
            ps = c % 2
            p.wait("vector", "act", av)
            p.wait("vector", f"ls{par}", sv_)
            p.waitv("vector", pscr_free[ps])
            pv = p.do("vector", lambda E, par=par, c=c, ps=ps: E.tensor_tensor(
                p_scr[ps][:, :], g_scr[:, :],
                sc_t[par][:, c * TILE:(c + 1) * TILE], ALU.mult), "dve")
            gscr_free = ("dve", pv)
            if c == DC - 1:
                sct_free[par] = ("dve", pv)
            p.wait("scalar", "dve", pv)
            pspv = p.dma("scalar", p_sp[c * P:(c + 1) * P, t0:t0 + TILE],
                         p_scr[ps][:, :], f"spp{ps}")
            pscr_free[ps] = (f"spp{ps}", pspv)
        spp_rec[i] = (p.cnt["spp0"], p.cnt["spp1"])
        if i > 0:
            c2_assembly(i - 1)
    c2_assembly(NT - 1)

    p.wait("scalar", "spy0", p.cnt["spy0"])
    p.wait("scalar", "spy1", p.cnt["spy1"])
    p.wait("gpsimd", "spy0", p.cnt["spy0"])
    p.wait("gpsimd", "spy1", p.cnt["spy1"])

    if DEBUG:
        dbg_can = nc.dram_tensor("dbg_canary", [P, 1024], F32,
                                 kind="ExternalOutput").ap()
        p.dma("gpsimd", dbg_can, canary[:, :], "gb")
        dbg = [
            ("qgsp", qg_sp, [IFF, D], BF16),
            ("qusp", qu_sp, [IFF, D], BF16),
            ("qdsp", qd_sp, [D, IFF], BF16),
            ("xntm", xn_tm_sp, [TOK, D], BF16),
            ("xnf", xnf_sp, [NT, P, DC * TILE], BF16),
            ("q", q_sp, [P, NT, DC * TILE], BF16),
            ("kvp", kvk1_part, [P, 8 * 65], F32),
            ("h", h_sp, [NT, P, IC * TILE], BF16),
            ("s", s_sp, [NT, P, DC * TILE], BF16),
        ]
        for name, src_, shape, dt in dbg:
            dd = nc.dram_tensor("dbg_" + name, shape, dt,
                                kind="ExternalOutput").ap()
            p.dma("gpsimd", dd, src_, "gb")
        dsc = nc.dram_tensor("dbg_scal", [1, 16], F32,
                             kind="ExternalOutput").ap()
        p.dma("gpsimd", dsc, scal[0:1, :], "gb")
        p.wait("gpsimd", "gb", p.cnt["gb"])

    p.emit()
    return nc


def _consts():
    ones_rb = np.ones((1, P), dtype=ml_dtypes.bfloat16)
    ones_cb = np.ones((P, 1), dtype=ml_dtypes.bfloat16)
    ones_rf = np.ones((1, P), dtype=np.float32)
    em2p = np.zeros((P, 2), dtype=ml_dtypes.bfloat16)
    em2h = np.zeros((P, 16 * P), dtype=ml_dtypes.bfloat16)
    for pp in range(P):
        em2p[pp, pp // 64] = 1.0
    for g in range(16):
        for pp in range(P):
            em2h[32 * (g % 3) + pp // 64, g * P + pp] = 1.0
    return ones_rb, ones_cb, ones_rf, em2p, em2h


def kernel(L0_attn_H, L0_attn_W_qkv, L0_attn_b_qkv, L0_attn_W_proj,
           L0_attn_b_proj, L0_attn_gamma, L0_attn_beta, L0_ffn_clip_x,
           L0_ffn_clip_down, L0_ffn_w_gate, L0_ffn_w_up, L0_ffn_w_down):
    x = np.ascontiguousarray(np.asarray(L0_attn_H)).reshape(4 * 8192, D)
    wqkv_t = np.ascontiguousarray(np.asarray(L0_attn_W_qkv).T)
    wproj_t = np.ascontiguousarray(np.asarray(L0_attn_W_proj).T)
    wg = np.ascontiguousarray(np.asarray(L0_ffn_w_gate))
    wu = np.ascontiguousarray(np.asarray(L0_ffn_w_up))
    wd = np.ascontiguousarray(np.asarray(L0_ffn_w_down))
    bqkv = np.asarray(L0_attn_b_qkv)
    bq_cols = np.ascontiguousarray(bqkv[0:D].reshape(DC, P).T)
    bkv_row = np.ascontiguousarray(bqkv[D:3 * D].reshape(1, 2 * D))
    bp_cols = np.ascontiguousarray(np.asarray(L0_attn_b_proj).reshape(DC, P).T)
    gam_cols = np.ascontiguousarray(np.asarray(L0_attn_gamma).reshape(DC, P).T)
    bet_cols = np.ascontiguousarray(np.asarray(L0_attn_beta).reshape(DC, P).T)
    clip = np.zeros((1, 2), dtype=np.float32)
    clip[0, 0] = np.asarray(L0_ffn_clip_x).reshape(-1)[0]
    clip[0, 1] = np.asarray(L0_ffn_clip_down).reshape(-1)[0]
    ones_rb, ones_cb, ones_rf, em2p, em2h = _consts()

    nc = build()
    in_maps = []
    for c in range(N_CORES):
        b, half = c // 2, c % 2
        r0 = b * 8192 + half * TOK
        pmask = np.zeros((P, N_CORES), dtype=np.float32)
        pmask[:, 2 * b] = 1.0
        pmask[:, 2 * b + 1] = 1.0
        sel2 = np.zeros((P, 2), dtype=np.float32)
        sel2[:, half] = 1.0
        in_maps.append({
            "x": np.ascontiguousarray(x[r0:r0 + TOK, :]),
            "wqkv_t": wqkv_t, "wproj_t": wproj_t,
            "wg_nat": wg, "wu_nat": wu, "wd_nat": wd,
            "bq_in": bq_cols, "bkv_in": bkv_row, "bp_in": bp_cols,
            "gam_in": gam_cols, "bet_in": bet_cols, "clip_in": clip,
            "ones_rb_in": ones_rb, "ones_cb_in": ones_cb,
            "ones_rf_in": ones_rf, "em2p_in": em2p, "em2h_in": em2h,
            "sel2_in": sel2, "pmask_in": pmask,
        })
    res = run_bass_kernel_spmd(nc, in_maps, core_ids=list(range(N_CORES)))
    out = np.concatenate([res.results[c]["y"] for c in range(N_CORES)],
                         axis=0)
    return out.reshape(4, 8192, D)


# version 19
# speedup vs baseline: 1.0967x; 1.0005x over previous
import numpy as np
import ml_dtypes
import concourse.bass as bass
import concourse.mybir as mybir
from concourse.bass_utils import run_bass_kernel_spmd

F32 = mybir.dt.float32
BF16 = mybir.dt.bfloat16
I32 = mybir.dt.int32
AF = mybir.ActivationFunctionType
ALU = mybir.AluOpType
AX = mybir.AxisListType

P = 128
D = 1024
DC = D // P
E3 = 3072
NH = 16
HD = 64
IFF = 2816
IC = IFF // P
TILE = 512
NT = 8
TOK = TILE * NT
NCH = TILE // P
QMAX = 127.0
LN_EPS = 1e-5
N_CORES = 8
ALLC = [list(range(N_CORES))]

ENGS = ("sync", "scalar", "vector", "tensor", "gpsimd")
DEBUG = False


class Prog:

    def __init__(self, nc, sems):
        self.nc = nc
        self.ops = {e: [] for e in ENGS}
        self.cnt = {s: 0 for s in sems}
        self.h = {s: nc.alloc_semaphore(s) for s in sems}

    def do(self, eng, fn, sem=None, n=1):
        if sem is None:
            self.ops[eng].append(fn)
            return None
        self.cnt[sem] += n
        h = self.h[sem]
        self.ops[eng].append(lambda E, fn=fn, h=h, n=n: fn(E).then_inc(h, n))
        return self.cnt[sem]

    def dma(self, eng, out, in_, sem, **kw):
        return self.do(
            eng,
            lambda E, o=out, i=in_, kw=kw: E.dma_start(out=o, in_=i, **kw),
            sem, 16)

    def wait(self, eng, sem, val):
        if val is None or val <= 0:
            return
        h = self.h[sem]
        self.ops[eng].append(lambda E, h=h, v=val: E.wait_ge(h, v))

    ENG_SEM = {"vector": "dve", "scalar": "act", "gpsimd": "pool",
               "tensor": "pe"}

    def fence(self, eng):
        sem = self.ENG_SEM[eng]
        self.wait(eng, sem, self.cnt[sem])

    def seq(self, eng, fn):
        return self.do(eng, fn, self.ENG_SEM[eng])

    def waitv(self, eng, sv):
        if sv is None:
            return
        if isinstance(sv, list):
            for e in sv:
                self.waitv(eng, e)
            return
        self.wait(eng, sv[0], sv[1])

    def emit(self):
        with self.nc.Block() as block:
            for eng in ENGS:
                ops = self.ops[eng]

                def runner(E, ops=ops):
                    for f in ops:
                        f(E)

                getattr(block, eng)(runner)


def build():
    nc = bass.Bass("TRN2", target_bir_lowering=False, detect_race_conditions=False)

    x_in = nc.dram_tensor("x", [TOK, D], F32, kind="ExternalInput").ap()
    wqkv_t = nc.dram_tensor("wqkv_t", [D, E3], F32, kind="ExternalInput").ap()
    wproj_t = nc.dram_tensor("wproj_t", [D, D], F32, kind="ExternalInput").ap()
    wg_nat = nc.dram_tensor("wg_nat", [IFF, D], F32, kind="ExternalInput").ap()
    wu_nat = nc.dram_tensor("wu_nat", [IFF, D], F32, kind="ExternalInput").ap()
    wd_nat = nc.dram_tensor("wd_nat", [D, IFF], F32, kind="ExternalInput").ap()
    bq_in = nc.dram_tensor("bq_in", [P, DC], F32, kind="ExternalInput").ap()
    bkv_in = nc.dram_tensor("bkv_in", [1, 2 * D], F32, kind="ExternalInput").ap()
    bp_in = nc.dram_tensor("bp_in", [P, DC], F32, kind="ExternalInput").ap()
    gam_in = nc.dram_tensor("gam_in", [P, DC], F32, kind="ExternalInput").ap()
    bet_in = nc.dram_tensor("bet_in", [P, DC], F32, kind="ExternalInput").ap()
    clip_in = nc.dram_tensor("clip_in", [1, 2], F32, kind="ExternalInput").ap()
    ones_rb_in = nc.dram_tensor("ones_rb_in", [1, P], BF16, kind="ExternalInput").ap()
    ones_cb_in = nc.dram_tensor("ones_cb_in", [P, 1], BF16, kind="ExternalInput").ap()
    ones_rf_in = nc.dram_tensor("ones_rf_in", [1, P], F32, kind="ExternalInput").ap()
    pmask_in = nc.dram_tensor("pmask_in", [P, N_CORES], F32, kind="ExternalInput").ap()
    em2p_in = nc.dram_tensor("em2p_in", [P, 2], BF16, kind="ExternalInput").ap()
    em2h_in = nc.dram_tensor("em2h_in", [P, 2 * NT * P], BF16, kind="ExternalInput").ap()
    sel2_in = nc.dram_tensor("sel2_in", [P, 2], F32, kind="ExternalInput").ap()

    y_out = nc.dram_tensor("y", [TOK, D], F32, kind="ExternalOutput").ap()

    xn_tm_sp = nc.dram_tensor("xn_tm_sp", [TOK, D], BF16).ap()
    xnf_sp = nc.dram_tensor("xnf_sp", [NT, P, DC * TILE], BF16).ap()
    q_sp = nc.dram_tensor("q_sp", [P, NT, DC * TILE], BF16).ap()
    h_sp = nc.dram_tensor("h_sp", [NT, P, IC * TILE], BF16).ap()
    s_sp = nc.dram_tensor("s_sp", [NT, P, DC * TILE], BF16).ap()
    p_sp = nc.dram_tensor("p_sp", [D, TOK], BF16).ap()
    qg_sp = nc.dram_tensor("qg_sp", [IFF, D], BF16).ap()
    qu_sp = nc.dram_tensor("qu_sp", [IFF, D], BF16).ap()
    qd_sp = nc.dram_tensor("qd_sp", [D, IFF], BF16).ap()
    maxbounce = nc.dram_tensor("maxbounce", [1, P], F32).ap()
    kvk1_part = nc.dram_tensor("kvk1_part", [P, 8 * 65], F32).ap()
    kvk1_gth = nc.dram_tensor("kvk1_gth", [N_CORES * P, 8 * 65], F32,
                              addr_space="Shared").ap()
    mx1_part = nc.dram_tensor("mx1_part", [1, 1], F32).ap()
    mx1_red = nc.dram_tensor("mx1_red", [1, 1], F32, addr_space="Shared").ap()
    mx2_part = nc.dram_tensor("mx2_part", [1, 1], F32).ap()
    mx2_red = nc.dram_tensor("mx2_red", [1, 1], F32, addr_space="Shared").ap()
    qx_gth = nc.dram_tensor("qx_gth", [N_CORES * P, NT * DC * TILE],
                            BF16, addr_space="Shared").ap()

    sems = (["ldc", "wq", "wp", "gs", "gb", "tqg", "tqd"]
            + [f"gl{i}" for i in range(2)]
            + ["pe", "dve", "act", "pool", "cc"]
            + [f"lx{i}" for i in range(4)]
            + [f"{nm}{i}" for nm in ("sptm", "trf", "spf", "spq", "lb",
                                     "sph", "lq", "lxn", "sps", "lh", "ls",
                                     "spp", "tpt", "spy") for i in range(2)])
    p = Prog(nc, sems)

    pb = [nc.alloc_psum_tensor(f"pb{i}", [P, TILE], F32) for i in range(8)]

    import functools
    S = functools.partial(nc.alloc_sbuf_tensor, unsafe_allow_realloc=True)
    kv_bf = S("kv_bf", [P, 4, 65], BF16)
    k1m = S("k1m", [P, 4, 2], BF16)
    em2p = S("em2p", [P, 2], BF16)
    sel2 = S("sel2", [P, 2], F32)
    ones_rb = S("ones_rb", [1, P], BF16)
    ones_cb = S("ones_cb", [P, 1], BF16)
    ones_rf = S("ones_rf", [1, P], F32)
    pmask = S("pmask", [P, N_CORES], F32)
    bq_sb = S("bq_sb", [P, DC], F32)
    bkv_sb = S("bkv_sb", [1, 2 * D], BF16)
    bp_sb = S("bp_sb", [P, DC], F32)
    gam_sb = S("gam_sb", [P, DC], F32)
    bet_sb = S("bet_sb", [P, DC], F32)
    clip_sb = S("clip_sb", [1, 2], F32)
    sg_cols = S("sg_cols", [P, IC], F32)
    su_cols = S("su_cols", [P, IC], F32)
    sd_cols = S("sd_cols", [P, DC], F32)
    cg_cols = S("cg_cols", [P, IC], F32)
    cu_cols = S("cu_cols", [P, IC], F32)
    cd_cols = S("cd_cols", [P, DC], F32)
    rx_col = S("rx_col", [P, 1], F32)
    rd_col = S("rd_col", [P, 1], F32)
    sx_col = S("sx_col", [P, 1], F32)
    sd_col = S("sd_col", [P, 1], F32)
    scal = S("scal", [1, 16], F32)
    brow = S("brow", [1, P], F32)
    zero_rb = S("zero_rb", [1, P], BF16)
    maxxn_col = S("maxxn_col", [P, 1], F32)
    maxh_col = S("maxh_col", [P, 1], F32)
    red_tmp = S("red_tmp", [P, 1], F32)

    _baseA1 = nc.sbuf_base
    xch = [S(f"xch{i}", [P, D], F32) for i in range(4)]
    xn_tm = [S(f"xn_tm{i}", [P, NCH, D], BF16) for i in range(2)]
    xn_fm = []
    for _i in range(2):
        xn_fm.append(S(f"xn_fm{_i}", [P, DC, TILE], BF16))
        S(f"xn_fm_pad{_i}", [P, 4096], mybir.dt.uint8)
    qt = [S(f"qt{i}", [P, DC, TILE], BF16) for i in range(2)]
    kt = [S(f"kt{i}", [P, NCH, D], BF16) for i in range(2)]
    vt = [S(f"vt{i}", [P, NCH, NH, 65], BF16) for i in range(2)]
    wqkv_bf = S("wqkv_bf", [P, DC, E3], BF16)
    ln_sum = S("ln_sum", [P, NCH], F32)
    ln_ssq = S("ln_ssq", [P, NCH], F32)
    ln_mu = S("ln_mu", [P, NCH], F32)
    ln_var = S("ln_var", [P, NCH], F32)
    ln_rstd = S("ln_rstd", [P, NCH], F32)
    ln_t = S("ln_t", [P, D], F32)
    dummy1 = ln_t
    dummy2 = ln_t
    ln_tmp = S("ln_tmp", [P, NCH], F32)
    phi_n = [S(f"phi_n{i}", [P, TILE], BF16) for i in range(3)]
    phi_e = [S(f"phi_e{i}", [P, TILE], BF16) for i in range(3)]
    phi_p = [S(f"phi_p{i}", [P, TILE], BF16) for i in range(3)]
    wst = [S(f"wst{j}", [P, D], F32) for j in range(2)]
    wq_f = S("wq_f", [P, D], F32)
    wq_i = S("wq_i", [P, D], I32)
    wq_b = S("wq_b", [P, D], BF16)
    wmx = S("wmx", [P, 1], F32)
    wmx2 = S("wmx2", [P, 1], F32)
    winv = S("winv", [P, 1], F32)


    for (dst, src) in [
        (bq_sb[:, :], bq_in), (bp_sb[:, :], bp_in),
        (gam_sb[:, :], gam_in), (bet_sb[:, :], bet_in), (clip_sb[:, :], clip_in),
        (ones_rb[:, :], ones_rb_in), (ones_cb[:, :], ones_cb_in),
        (ones_rf[:, :], ones_rf_in),
        (em2p[:, :], em2p_in),
        (sel2[:, :], sel2_in), (pmask[:, :], pmask_in),
    ]:
        p.dma("sync", dst, src, "ldc")
    consts_ld = p.cnt["ldc"]

    p.dma("gpsimd", bkv_sb[:, :], bkv_in, "wq")
    for c in range(DC):
        p.dma("gpsimd", wqkv_bf[:, c, :], wqkv_t[c * P:(c + 1) * P, :], "wq")
    wq_ld = p.cnt["wq"]

    p.wait("vector", "ldc", consts_ld)
    p.do("vector", lambda E: E.memset(zero_rb[:, :], 0.0))
    p.do("vector", lambda E: E.memset(maxxn_col[:, :], 0.0))
    p.do("vector", lambda E: E.memset(maxh_col[:, :], 0.0))
    p.do("vector", lambda E: E.memset(vt[0][:, :, :, 64:65], 1.0))
    p.do("vector", lambda E: E.memset(vt[1][:, :, :, 64:65], 1.0), "dve")
    startup_dve = p.cnt["dve"]

    p.wait("scalar", "ldc", consts_ld)
    p.do("scalar", lambda E: E.activation(scal[0:1, 8:10], clip_sb[0:1, 0:2],
                                          AF.Exp, scale=-1.0), "act")
    sig_act = p.cnt["act"]
    p.wait("vector", "act", sig_act)
    p.seq("vector", lambda E: E.tensor_scalar(scal[0:1, 10:12],
                                              scal[0:1, 8:10],
                                              1.0, None, ALU.add))
    p.fence("vector")
    p.do("vector", lambda E: E.reciprocal(scal[0:1, 0:2], scal[0:1, 10:12]),
         "dve")
    sig_dve = p.cnt["dve"]
    if DEBUG:
        dsc1 = nc.dram_tensor("dbg_scal1", [1, 16], F32,
                              kind="ExternalOutput").ap()
        p.wait("gpsimd", "dve", sig_dve)
        p.dma("gpsimd", dsc1, scal[0:1, :], "gb")

    prep_jobs = ([("g", r) for r in range(IC)] + [("u", r) for r in range(IC)]
                 + [("d", r) for r in range(DC)])
    wst_free = [None, None]
    wqb_free = [None]
    wqf_free = [None]
    prep_slot = [0]

    def _prep_ld(nat, r, c0, c1, eng):
        sl = prep_slot[0] % 2
        prep_slot[0] += 1
        w = c1 - c0
        p.waitv(eng, wst_free[sl])
        sem = f"gl{sl}"
        ldv = p.dma(eng, wst[sl][:, 0:w], nat[r * P:(r + 1) * P, c0:c1],
                    sem)
        return sl, ldv, w, sem

    def gen_prep(job):
        kind, r = job
        nat = {"g": wg_nat, "u": wu_nat, "d": wd_nat}[kind]
        spd = {"g": qg_sp, "u": qu_sp, "d": qd_sp}[kind]
        scol = {"g": sg_cols, "u": su_cols, "d": sd_cols}[kind]
        C = D if kind in ("g", "u") else IFF
        halves = [(0, C)] if C <= D else [(0, D), (D, 2 * D), (2 * D, C)]
        single = len(halves) == 1
        ldq = "sync"
        loads = []
        for hi, (c0, c1) in enumerate(halves):
            sl, ldv, w, lsem = _prep_ld(nat, r, c0, c1, ldq)
            loads.append((sl, ldv, w, lsem))
            p.wait("vector", lsem, ldv)
            mdst = wmx if hi == 0 else wmx2
            mv = p.seq("vector", lambda E, m=mdst, w=w, sl=sl: E.reduce_max(
                m[:, :], wst[sl][:, 0:w], axis=AX.X,
                apply_absolute_value=True))
            if hi > 0:
                p.fence("vector")
                mv = p.seq("vector", lambda E: E.tensor_max(
                    wmx[:, :], wmx[:, :], wmx2[:, :]))
            if not single:
                wst_free[sl] = ("dve", mv)
        p.fence("vector")
        p.seq("vector", lambda E: E.reciprocal(winv[:, :], wmx[:, :]))
        p.do("vector", lambda E, sc=scol, r=r: E.tensor_scalar(
            sc[:, r:r + 1], wmx[:, :], 1.0 / QMAX, None, ALU.mult), "dve")
        inv_v = p.cnt["dve"]
        for hi, (c0, c1) in enumerate(halves):
            if single:
                sl, ldv, w, lsem = loads[hi]
            else:
                sl, ldv, w, lsem = _prep_ld(nat, r, c0, c1, ldq)
                p.wait("vector", lsem, ldv)
            p.waitv("vector", wqf_free[0])
            mdve = p.do("vector", lambda E, w=w, sl=sl: E.tensor_scalar(
                wq_f[:, 0:w], wst[sl][:, 0:w], winv[:, :], QMAX, ALU.mult,
                ALU.mult), "dve")
            wst_free[sl] = ("dve", mdve)
            p.wait("scalar", "dve", mdve)
            p.waitv("scalar", wqb_free[0])
            a1 = p.seq("scalar", lambda E, w=w: E.activation(
                wq_i[:, 0:w], wq_f[:, 0:w], AF.Copy))
            wqf_free[0] = ("act", a1)
            p.fence("scalar")
            a2 = p.do("scalar", lambda E, w=w: E.activation(
                wq_b[:, 0:w], wq_i[:, 0:w], AF.Copy), "act")
            p.wait("gpsimd", "act", a2)
            spv = p.dma("gpsimd", spd[r * P:(r + 1) * P, c0:c1], wq_b[:, 0:w],
                        "gs")
            wqb_free[0] = ("gs", spv)

    xch_free = [None] * 4
    xn_tm_free = [None, None]
    xn_fm_free = [None, None]
    qt_free = [None, None]
    ktvt_free = [None, None]
    bank_free = [None] * 8
    phi_free = [None, None, None]
    q_spill_v = [None] * NT
    xnf_v = [None] * NT
    prev_var_dve = 0

    for i in range(NT):
        par = i % 2
        t0 = i * TILE
        ldxv = []
        for j in range(NCH):
            slot = j
            p.waitv("sync", xch_free[slot])
            v = p.dma("sync", xch[slot][:, :],
                      x_in[t0 + j * P: t0 + (j + 1) * P, :], f"lx{slot}")
            ldxv.append((slot, (f"lx{slot}", v)))
        p.fence("vector")
        p.do("vector", lambda E: E.memset(ln_sum[:, :], 0.0))
        if i > 0:
            p.wait("scalar", "dve", prev_var_dve)
        p.do("scalar", lambda E: E.memzero(ln_ssq[:, :]))
        for j in range(NCH):
            slot, v = ldxv[j]
            p.waitv("vector", v)
            p.do("vector", lambda E, s=slot, j=j: E.tensor_scalar(
                dummy1[:, :], xch[s][:, :], 1.0, 0.0, ALU.mult, ALU.add,
                accum_out=ln_sum[:, j:j + 1]), "dve")
            p.waitv("scalar", v)
            p.do("scalar", lambda E, s=slot, j=j: E.activation(
                dummy2[:, :], xch[s][:, :], AF.Square,
                accum_out=ln_ssq[:, j:j + 1]), "act")
        sums_act = p.cnt["act"]
        p.wait("vector", "act", sums_act)
        p.fence("vector")
        p.seq("vector", lambda E: E.tensor_scalar(
            ln_mu[:, :], ln_sum[:, :], 1.0 / D, None, ALU.mult))
        p.seq("vector", lambda E: E.tensor_scalar(
            ln_var[:, :], ln_ssq[:, :], 1.0 / D, LN_EPS, ALU.mult, ALU.add))
        p.fence("vector")
        p.seq("vector", lambda E: E.tensor_tensor(
            ln_tmp[:, :], ln_mu[:, :], ln_mu[:, :], ALU.mult))
        p.fence("vector")
        p.do("vector", lambda E: E.tensor_sub(
            ln_var[:, :], ln_var[:, :], ln_tmp[:, :]), "dve")
        var_dve = p.cnt["dve"]
        prev_var_dve = var_dve
        p.wait("scalar", "dve", var_dve)
        p.seq("scalar", lambda E: E.activation(
            ln_tmp[:, :], ln_var[:, :], AF.Ln))
        p.fence("scalar")
        p.do("scalar", lambda E: E.activation(
            ln_rstd[:, :], ln_tmp[:, :], AF.Exp, scale=-0.5), "act")
        rstd_act = p.cnt["act"]
        p.waitv("vector", xn_tm_free[par])
        p.wait("vector", "act", rstd_act)
        p.wait("vector", "dve", var_dve)
        for j in range(NCH):
            slot, v = ldxv[j]
            p.fence("vector")
            p.seq("vector", lambda E, s=slot, j=j: E.tensor_scalar(
                ln_t[:, :], xch[s][:, :], ln_mu[:, j:j + 1],
                ln_rstd[:, j:j + 1], ALU.subtract, ALU.mult))
            p.fence("vector")
            p.seq("vector", lambda E: E.reduce_max(
                ln_tmp[:, 0:1], ln_t[:, :], axis=AX.X,
                apply_absolute_value=True))
            p.fence("vector")
            p.seq("vector", lambda E: E.tensor_max(
                maxxn_col[:, :], maxxn_col[:, :], ln_tmp[:, 0:1]))
            mdve = p.do("vector", lambda E, par=par, j=j: E.tensor_copy(
                xn_tm[par][:, j, :], ln_t[:, :]), "dve")
            xch_free[slot] = ("dve", mdve)
        xn_dve = p.cnt["dve"]

        p.wait("scalar", "dve", xn_dve)
        spv = p.dma("scalar",
                    xn_tm_sp[t0:t0 + TILE, :].rearrange("(j p) d -> p j d",
                                                        p=P),
                    xn_tm[par][:, :, :], f"sptm{par}")
        xn_tm_free[par] = (f"sptm{par}", spv)
        p.wait("sync", f"sptm{par}", spv)
        p.waitv("sync", xn_fm_free[par])
        for c in range(DC):
            trv = p.dma("sync", xn_fm[par][:, c, :],
                        xn_tm_sp[t0:t0 + TILE, c * P:(c + 1) * P],
                        f"trf{par}", transpose=True)
        p.wait("scalar", f"trf{par}", trv)
        xnf_v[i] = (f"spf{par}",
                    p.dma("scalar", xnf_sp[i, :, :], xn_fm[par][:, :, :],
                          f"spf{par}"))

        p.wait("tensor", f"trf{par}", trv)
        if i == 0:
            p.wait("tensor", "wq", wq_ld)
            p.wait("tensor", "dve", startup_dve)
        bank_i = [0]

        def next_bank():
            b = bank_i[0] % 6
            bank_i[0] += 1
            return b

        for c in range(DC):
            b = next_bank()
            pset = b % 3
            p.waitv("tensor", bank_free[b])
            for cc in range(DC):
                p.do("tensor", lambda E, b=b, c=c, cc=cc, par=par: E.matmul(
                    pb[b][:, :], wqkv_bf[:, cc, c * P:(c + 1) * P],
                    xn_fm[par][:, cc, :], start=(cc == 0),
                    stop=(cc == DC - 1)),
                    "pe" if cc == DC - 1 else None)
            pev = p.cnt["pe"]
            p.wait("vector", "pe", pev)
            p.waitv("vector", phi_free[pset])
            p.do("vector", lambda E, b=b, c=c, ps=pset: E.tensor_scalar(
                phi_n[ps][:, :], pb[b][:, :], bq_sb[:, c:c + 1], 0.0,
                ALU.add, ALU.min), "dve")
            nv = p.cnt["dve"]
            p.wait("scalar", "dve", nv)
            p.do("scalar", lambda E, ps=pset: E.activation(
                phi_e[ps][:, :], phi_n[ps][:, :], AF.Exp), "act")
            ev = p.cnt["act"]
            p.do("vector", lambda E, b=b, c=c, ps=pset: E.tensor_scalar(
                phi_p[ps][:, :], pb[b][:, :], bq_sb[:, c:c + 1], 0.0,
                ALU.add, ALU.max), "dve")
            pv = p.cnt["dve"]
            bank_free[b] = ("dve", pv)
            p.wait("gpsimd", "act", ev)
            p.wait("gpsimd", "dve", pv)
            if c == 0:
                p.waitv("gpsimd", qt_free[par])
            plv = p.do("gpsimd", lambda E, par=par, c=c, ps=pset:
                       E.tensor_tensor(qt[par][:, c, :], phi_e[ps][:, :],
                                       phi_p[ps][:, :], ALU.add), "pool")
            phi_free[pset] = ("pool", plv)
        q_pool = p.cnt["pool"]

        first_kv_evac = True
        for j in range(NCH):
            for kindv, base_e, dstt in (("k", D, kt), ("v", 2 * D, vt)):
                for eh in range(2):
                    b = next_bank()
                    pset = b % 3
                    p.waitv("tensor", bank_free[b])
                    p.do("tensor", lambda E, b=b, base_e=base_e, eh=eh:
                         E.matmul(pb[b][:, :], ones_rb[0:1, :],
                                  bkv_sb[0:1, base_e - D + eh * TILE:
                                         base_e - D + (eh + 1) * TILE],
                                  start=True, stop=False))
                    for cc in range(DC):
                        p.do("tensor", lambda E, b=b, cc=cc, j=j, par=par,
                             base_e=base_e, eh=eh: E.matmul(
                            pb[b][:, :],
                            xn_fm[par][:, cc, j * P:(j + 1) * P],
                            wqkv_bf[:, cc, base_e + eh * TILE:
                                    base_e + (eh + 1) * TILE],
                            start=False, stop=(cc == DC - 1)),
                            "pe" if cc == DC - 1 else None)
                    pev = p.cnt["pe"]
                    if kindv == "k":
                        p.wait("vector", "pe", pev)
                        p.waitv("vector", phi_free[pset])
                        p.do("vector", lambda E, b=b, ps=pset: E.tensor_scalar(
                            phi_n[ps][:, :], pb[b][:, :], 0.0, None, ALU.min),
                            "dve")
                        nv = p.cnt["dve"]
                        p.wait("scalar", "dve", nv)
                        p.do("scalar", lambda E, ps=pset: E.activation(
                            phi_e[ps][:, :], phi_n[ps][:, :], AF.Exp), "act")
                        ev = p.cnt["act"]
                        p.do("vector", lambda E, b=b, ps=pset: E.tensor_scalar(
                            phi_p[ps][:, :], pb[b][:, :], 0.0, None, ALU.max),
                            "dve")
                        pv = p.cnt["dve"]
                        bank_free[b] = ("dve", pv)
                        p.wait("gpsimd", "act", ev)
                        p.wait("gpsimd", "dve", pv)
                        if first_kv_evac:
                            p.waitv("gpsimd", ktvt_free[par])
                            first_kv_evac = False
                        plv = p.do("gpsimd",
                                   lambda E, par=par, j=j, eh=eh, ps=pset:
                                   E.tensor_tensor(
                                       kt[par][:, j,
                                               eh * TILE:(eh + 1) * TILE],
                                       phi_e[ps][:, :], phi_p[ps][:, :],
                                       ALU.add), "pool")
                        phi_free[pset] = ("pool", plv)
                    else:
                        p.wait("scalar", "pe", pev)
                        if first_kv_evac:
                            p.waitv("scalar", ktvt_free[par])
                            first_kv_evac = False
                        av = p.do("scalar", lambda E, b=b, par=par, j=j, eh=eh:
                                  E.activation(
                                      vt[par][:, j, eh * 8:(eh + 1) * 8, 0:64],
                                      pb[b][:, :], AF.Copy), "act")
                        bank_free[b] = ("act", av)
        kv_ready_pool = p.cnt["pool"]
        kv_ready_act = p.cnt["act"]

        p.wait("tensor", "pool", kv_ready_pool)
        p.wait("tensor", "act", kv_ready_act)
        if i == 0:
            p.do("tensor", lambda E: E.matmul(
                pb[6][:, :], zero_rb[0:1, :], bkv_sb[0:1, 0:TILE],
                start=True, stop=False))
            p.do("tensor", lambda E: E.matmul(
                pb[7][:, :], zero_rb[0:1, :], bkv_sb[0:1, 0:TILE],
                start=True, stop=False))
        for j in range(NCH):
            for h in range(NH):
                pr = h // 2
                bank = 6 if pr < 4 else 7
                po = 64 * (h % 2)
                fo = (pr % 4) * 65
                st = False
                sp_ = False
                lastmm = (j == NCH - 1 and h == NH - 1)
                p.do("tensor", lambda E, bank=bank, po=po, fo=fo, par=par,
                     i=i, j=j, h=h, st=st, sp_=sp_: E.matmul(
                    pb[bank][po:po + 64, fo:fo + 65],
                    kt[par][:, j, h * HD:(h + 1) * HD],
                    vt[par][:, j, h, 0:65],
                    start=st, stop=sp_).annotate(f"kv i{i} j{j} h{h}"),
                    "pe" if lastmm else None)
        if i == NT - 1:
            p.do("tensor", lambda E: E.matmul(
                pb[6][:, :], zero_rb[0:1, :], bkv_sb[0:1, 0:TILE],
                start=False, stop=True))
            p.do("tensor", lambda E: E.matmul(
                pb[7][:, :], zero_rb[0:1, :], bkv_sb[0:1, 0:TILE],
                start=False, stop=True), "pe")
        kvv = p.cnt["pe"]
        ktvt_free[par] = ("pe", kvv)
        xn_fm_free[par] = [("pe", kvv), xnf_v[i]]

        p.wait("scalar", "pool", q_pool)
        q_spill_v[i] = (f"spq{par}",
                        p.dma("scalar", q_sp[:, i, :], qt[par][:, :, :],
                              f"spq{par}"))
        qt_free[par] = q_spill_v[i]


    nc.sbuf_base = _baseA1
    _baseA2 = nc.sbuf_base
    kvk1_sb = S("kvk1_sb", [P, 8 * 65], F32)
    kvg = S("kvg", [P, N_CORES, 8 * 65], F32)
    kvacc = [S(f"kvacc{i}", [P, 8 * 65], F32) for i in range(2)]
    pe_a_done = p.cnt["pe"]
    p.wait("scalar", "pe", pe_a_done)
    p.do("scalar", lambda E: E.activation(
        kvk1_sb[:, 0:260], pb[6][:, 0:260], AF.Copy))
    ev = p.do("scalar", lambda E: E.activation(
        kvk1_sb[:, 260:520], pb[7][:, 0:260], AF.Copy), "act")
    bank_free[6] = ("act", ev)
    bank_free[7] = ("act", ev)
    p.wait("gpsimd", "act", ev)
    p.dma("gpsimd", kvk1_part[:, :], kvk1_sb[:, :], "gb")
    adve = p.cnt["dve"]
    p.wait("gpsimd", "dve", adve)
    mxv = p.dma("gpsimd", maxbounce[0:1, :].rearrange("o p -> p o"),
                maxxn_col[:, :], "gb")
    p.wait("sync", "gb", mxv)
    bv = p.dma("sync", brow[0:1, :], maxbounce[0:1, :], "ldc")
    p.wait("vector", "ldc", bv)
    mv = p.do("vector", lambda E: E.reduce_max(
        scal[0:1, 2:3], brow[0:1, :], axis=AX.X), "dve")
    p.wait("gpsimd", "dve", mv)
    p.dma("gpsimd", mx1_part[:, :], scal[0:1, 2:3], "gb")
    p.wait("gpsimd", "gb", p.cnt["gb"])
    p.do("gpsimd", lambda E: E.collective_compute(
        "AllGather", ALU.bypass, replica_groups=ALLC,
        ins=[kvk1_part[:, :]], outs=[kvk1_gth[:, :]]), "cc")
    ccv1 = p.do("gpsimd", lambda E: E.collective_compute(
        "AllReduce", ALU.max, replica_groups=ALLC,
        ins=[mx1_part[:, :]], outs=[mx1_red[:, :]]), "cc")
    p.wait("gpsimd", "spq0", p.cnt["spq0"])
    p.wait("gpsimd", "spq1", p.cnt["spq1"])
    ccq = p.do("gpsimd", lambda E: E.collective_compute(
        "AllGather", ALU.bypass, replica_groups=ALLC,
        ins=[q_sp.rearrange("p i f -> p (i f)")],
        outs=[qx_gth[:, :]]), "cc")
    p.wait("gpsimd", "cc", ccv1)
    p.dma("gpsimd", kvg[:, :, :],
          kvk1_gth.rearrange("(c p) f -> p c f", p=P), "gb")
    gmxv = p.dma("gpsimd", scal[0:1, 2:3], mx1_red[:, :], "gb")
    p.wait("vector", "gb", gmxv)
    p.seq("vector", lambda E: E.tensor_scalar(
        kvacc[0][:, :], kvg[:, 0, :], pmask[:, 0:1], None, ALU.mult))
    for c in range(1, N_CORES):
        a, b_ = kvacc[(c - 1) % 2], kvacc[c % 2]
        p.fence("vector")
        p.seq("vector", lambda E, a=a, b_=b_, c=c: E.scalar_tensor_tensor(
            b_[:, :], kvg[:, c, :], pmask[:, c:c + 1], a[:, :],
            ALU.mult, ALU.add))
    kvfin = kvacc[(N_CORES - 1) % 2]
    kvmy = kvacc[(N_CORES) % 2]
    p.fence("vector")
    for c in range(4):
        p.seq("vector", lambda E, c=c: E.tensor_scalar(
            kvmy[:, c * 65:(c + 1) * 65], kvfin[:, c * 65:(c + 1) * 65],
            sel2[:, 0:1], None, ALU.mult))
        p.fence("vector")
        p.seq("vector", lambda E, c=c: E.scalar_tensor_tensor(
            kvmy[:, c * 65:(c + 1) * 65],
            kvfin[:, (4 + c) * 65:(5 + c) * 65], sel2[:, 1:2],
            kvmy[:, c * 65:(c + 1) * 65], ALU.mult, ALU.add))
        p.fence("vector")
        p.do("vector", lambda E, c=c: E.tensor_copy(
            kv_bf[:, c, :], kvmy[:, c * 65:(c + 1) * 65]))
        p.do("vector", lambda E, c=c: E.tensor_scalar(
            k1m[:, c, :], em2p[:, :],
            kvmy[:, c * 65 + 64: c * 65 + 65], None, ALU.mult))
    p.wait("vector", "dve", sig_dve)
    p.seq("vector", lambda E: E.tensor_scalar(
        scal[0:1, 3:4], scal[0:1, 2:3], scal[0:1, 0:1], 1.0 / QMAX,
        ALU.mult, ALU.mult))
    p.fence("vector")
    kvdve = p.do("vector", lambda E: E.reciprocal(scal[0:1, 4:5],
                                                  scal[0:1, 3:4]), "dve")
    p.wait("tensor", "dve", kvdve)
    p.waitv("tensor", bank_free[0])
    p.do("tensor", lambda E: E.matmul(
        pb[0][:, 0:1], ones_rf[0:1, :], scal[0:1, 4:5], start=True, stop=True))
    bcv = p.do("tensor", lambda E: E.matmul(
        pb[0][:, 1:2], ones_rf[0:1, :], scal[0:1, 3:4], start=True,
        stop=True), "pe")
    p.wait("scalar", "pe", bcv)
    p.do("scalar", lambda E: E.activation(rx_col[:, :], pb[0][:, 0:1],
                                          AF.Copy))
    bco = p.do("scalar", lambda E: E.activation(sx_col[:, :], pb[0][:, 1:2],
                                                AF.Copy), "act")
    bank_free[0] = ("act", bco)
    p.wait("vector", "act", bco)
    p.fence("vector")
    bar1_kv_dve = p.cnt["dve"]
    for job in prep_jobs[:2 * IC]:
        gen_prep(job)
    prep_gs_gu = p.cnt["gs"]

    p.fence("vector")
    p.do("vector", lambda E: E.tensor_scalar(
        cg_cols[:, :], sg_cols[:, :], sx_col[:, :], None, ALU.mult))
    p.do("vector", lambda E: E.tensor_scalar(
        cu_cols[:, :], su_cols[:, :], sx_col[:, :], None, ALU.mult), "dve")
    bar1_dve = p.cnt["dve"]
    if DEBUG:
        dsc2 = nc.dram_tensor("dbg_scal2", [1, 16], F32,
                              kind="ExternalOutput").ap()
        p.wait("gpsimd", "dve", bar1_dve)
        p.dma("gpsimd", dsc2, scal[0:1, :], "gb")

    nc.sbuf_base = _baseA2
    _baseB = nc.sbuf_base
    qg_bf = [S(f"qg_bf{c}", [P, IFF], BF16) for c in range(DC)]
    S("qg_pad", [P, 4096], mybir.dt.uint8)
    qu_bf = [S(f"qu_bf{c}", [P, IFF], BF16) for c in range(DC)]
    canary = S("canary", [P, 1024], F32)
    p.wait("sync", "gs", prep_gs_gu)
    p.wait("sync", "pe", pe_a_done)
    p.wait("sync", "dve", bar1_kv_dve)
    for s_ in ("sptm0", "sptm1", "spf0", "spf1", "spq0", "spq1"):
        p.wait("sync", s_, p.cnt[s_])
    for wsp, wbf in ((qg_sp, qg_bf), (qu_sp, qu_bf)):
        for c in range(DC):
            p.dma("sync", wbf[c][:, :], wsp[:, c * P:(c + 1) * P], "tqg",
                  transpose=True)
    qgu_ld = p.cnt["tqg"]
    if DEBUG:
        dbg_qgb = nc.dram_tensor("dbg_qgb", [DC * P, IFF], BF16,
                                 kind="ExternalOutput").ap()
        dbg_qub = nc.dram_tensor("dbg_qub", [DC * P, IFF], BF16,
                                 kind="ExternalOutput").ap()
        p.wait("gpsimd", "tqg", qgu_ld)
        for c in range(DC):
            p.dma("gpsimd", dbg_qgb[c * P:(c + 1) * P, :], qg_bf[c][:, :],
                  "gb")
            p.dma("gpsimd", dbg_qub[c * P:(c + 1) * P, :], qu_bf[c][:, :],
                  "gb")
        qgb_dump = p.cnt["gb"]
        p.wait("tensor", "gb", qgb_dump)
    p.wait("vector", "dve", bar1_dve)
    p.do("vector", lambda E: E.memset(canary[:, :], 1234.5), "dve")
    canary_set = p.cnt["dve"]
    for job in prep_jobs[2 * IC:]:
        gen_prep(job)
    prep_gs = p.cnt["gs"]

    hb = [S("hb0", [P, IC, TILE], BF16)]
    xnb = [S(f"xnb{i}", [P, DC * TILE], BF16) for i in range(2)]
    ixb = [S(f"ixb{j}", [P, DC, TILE], BF16) for j in range(2)]
    qf_scr = [S(f"qf_scr{j}", [P, TILE], F32) for j in range(2)]
    qi_scr = [S(f"qi_scr{j}", [P, TILE], I32) for j in range(2)]
    sg_scr = S("sg_scr", [P, TILE], F32)
    su_scr = S("su_scr", [P, TILE], F32)
    hf_scr = S("hf_scr", [P, TILE], F32)

    xnb_free = [None, None]
    hb_free = [None]
    hsp_v = [None] * NT
    scr_free = None
    pe_up_last = None
    qfB_free = [None, None]
    qiB_free = [None, None]
    ixb_guard = [None, None]
    if DEBUG:
        dbg_ix = nc.dram_tensor("dbg_ix", [NT, P, DC * TILE], BF16,
                                kind="ExternalOutput").ap()
        dbg_xnb = nc.dram_tensor("dbg_xnb", [NT, P, DC * TILE], BF16,
                                 kind="ExternalOutput").ap()
        dbg_ix_done = None

    for i in range(NT):
        par = i % 2
        p.waitv("sync", xnb_free[par])
        p.waitv("sync", xnf_v[i])
        xv = p.dma("sync", xnb[par][:, :], xnf_sp[i, :, :], f"lb{par}")
        if DEBUG:
            p.wait("gpsimd", f"lb{par}", xv)
            xnbd = p.dma("gpsimd", dbg_xnb[i, :, :], xnb[par][:, :], "gb")
        p.wait("vector", f"lb{par}", xv)
        if DEBUG:
            p.wait("vector", "gb", xnbd)
        for c in range(DC):
            s = c % 2
            p.waitv("vector", qfB_free[s])
            mv = p.do("vector", lambda E, par=par, c=c, s=s: E.tensor_scalar(
                qf_scr[s][:, :], xnb[par][:, c * TILE:(c + 1) * TILE],
                rx_col[:, :], None, ALU.mult), "dve")
            p.wait("scalar", "dve", mv)
            p.waitv("scalar", qiB_free[s])
            av = p.do("scalar", lambda E, s=s: E.activation(
                qi_scr[s][:, :], qf_scr[s][:, :], AF.Copy), "act")
            qfB_free[s] = ("act", av)
            p.wait("gpsimd", "act", av)
            if c == 0:
                p.waitv("gpsimd", ixb_guard[par])
            cv = p.do("gpsimd", lambda E, c=c, s=s, par=par: E.tensor_scalar(
                ixb[par][:, c, :], qi_scr[s][:, :], 127.0, -128.0, ALU.min,
                ALU.max), "pool")
            qiB_free[s] = ("pool", cv)
        ix_pool = p.cnt["pool"]
        xnb_free[par] = ("dve", p.cnt["dve"])
        if DEBUG:
            dbg_ix_done = p.dma("gpsimd", dbg_ix[i, :, :],
                                ixb[par][:, :, :], "gb")
        p.wait("tensor", "pool", ix_pool)
        if DEBUG:
            p.wait("tensor", "gb", dbg_ix_done)
        if i == 0:
            p.wait("tensor", "tqg", qgu_ld)
        p.waitv("vector", hb_free[0])
        for r in range(IC):
            bg = (2 * r) % 8
            bu = (2 * r + 1) % 8
            p.waitv("tensor", bank_free[bg])
            for cc in range(DC):
                p.do("tensor", lambda E, bg=bg, r=r, cc=cc, par=par: E.matmul(
                    pb[bg][:, :], qg_bf[cc][:, r * P:(r + 1) * P],
                    ixb[par][:, cc, :], start=(cc == 0), stop=(cc == DC - 1)),
                    None)
            p.waitv("tensor", bank_free[bu])
            for cc in range(DC):
                p.do("tensor", lambda E, bu=bu, r=r, cc=cc, par=par: E.matmul(
                    pb[bu][:, :], qu_bf[cc][:, r * P:(r + 1) * P],
                    ixb[par][:, cc, :], start=(cc == 0), stop=(cc == DC - 1)),
                    "pe" if cc == DC - 1 else None)
            uv = p.cnt["pe"]
            if r == IC - 1:
                pe_up_last = uv
                ixb_guard[par] = ("pe", uv)
            p.wait("scalar", "pe", uv)
            if scr_free is not None:
                p.waitv("scalar", scr_free)
            av = p.do("scalar", lambda E, bg=bg, r=r: E.activation(
                sg_scr[:, :], pb[bg][:, :], AF.Silu,
                scale=cg_cols[:, r:r + 1]), "act")
            bank_free[bg] = ("act", av)
            p.wait("vector", "act", av)
            p.fence("vector")
            uev = p.do("vector", lambda E, bu=bu, r=r: E.tensor_scalar(
                su_scr[:, :], pb[bu][:, :], cu_cols[:, r:r + 1], None,
                ALU.mult), "dve")
            bank_free[bu] = ("dve", uev)
            p.fence("vector")
            p.seq("vector", lambda E: E.tensor_mul(
                hf_scr[:, :], sg_scr[:, :], su_scr[:, :]))
            p.fence("vector")
            p.seq("vector", lambda E: E.reduce_max(
                red_tmp[:, 0:1], hf_scr[:, :], axis=AX.X,
                apply_absolute_value=True))
            p.fence("vector")
            p.seq("vector", lambda E: E.tensor_max(
                maxh_col[:, :], maxh_col[:, :], red_tmp[:, 0:1]))
            hcv = p.do("vector", lambda E, par=par, r=r: E.tensor_copy(
                hb[0][:, r, :], hf_scr[:, :]), "dve")
            scr_free = ("dve", hcv)
        h_dve = p.cnt["dve"]
        p.wait("scalar", "dve", h_dve)
        hsp_v[i] = (f"sph{par}",
                    p.dma("scalar", h_sp[i, :, :], hb[0][:, :, :],
                          f"sph{par}"))
        hb_free[0] = hsp_v[i]

    if DEBUG:
        dbg_canB = nc.dram_tensor("dbg_canB", [P, 1024], F32,
                                  kind="ExternalOutput").ap()
        p.wait("gpsimd", "sph0", p.cnt["sph0"])
        p.wait("gpsimd", "sph1", p.cnt["sph1"])
        p.dma("gpsimd", dbg_canB, canary[:, :], "gb")
    bdve = p.cnt["dve"]
    p.wait("gpsimd", "dve", bdve)
    mxv2 = p.dma("gpsimd", maxbounce[0:1, :].rearrange("o p -> p o"),
                 maxh_col[:, :], "gb")
    p.wait("sync", "gb", mxv2)
    bv2 = p.dma("sync", brow[0:1, :], maxbounce[0:1, :], "ldc")
    p.wait("vector", "ldc", bv2)
    mv2 = p.do("vector", lambda E: E.reduce_max(
        scal[0:1, 5:6], brow[0:1, :], axis=AX.X), "dve")
    p.wait("gpsimd", "dve", mv2)
    p.dma("gpsimd", mx2_part[:, :], scal[0:1, 5:6], "gb")
    p.wait("gpsimd", "gb", p.cnt["gb"])
    ccv2 = p.do("gpsimd", lambda E: E.collective_compute(
        "AllReduce", ALU.max, replica_groups=ALLC,
        ins=[mx2_part[:, :]], outs=[mx2_red[:, :]]), "cc")
    p.wait("gpsimd", "cc", ccv2)
    gmx2 = p.dma("gpsimd", scal[0:1, 5:6], mx2_red[:, :], "gb")
    p.wait("vector", "gb", gmx2)
    p.seq("vector", lambda E: E.tensor_scalar(
        scal[0:1, 6:7], scal[0:1, 5:6], scal[0:1, 1:2], 1.0 / QMAX,
        ALU.mult, ALU.mult))
    p.fence("vector")
    sdv = p.do("vector", lambda E: E.reciprocal(scal[0:1, 7:8],
                                                scal[0:1, 6:7]), "dve")
    p.wait("tensor", "dve", sdv)
    p.waitv("tensor", bank_free[0])
    p.do("tensor", lambda E: E.matmul(
        pb[0][:, 0:1], ones_rf[0:1, :], scal[0:1, 7:8], start=True,
        stop=True))
    bcv2 = p.do("tensor", lambda E: E.matmul(
        pb[0][:, 1:2], ones_rf[0:1, :], scal[0:1, 6:7], start=True,
        stop=True), "pe")
    p.wait("scalar", "pe", bcv2)
    p.do("scalar", lambda E: E.activation(rd_col[:, :], pb[0][:, 0:1],
                                          AF.Copy))
    bco2 = p.do("scalar", lambda E: E.activation(sd_col[:, :], pb[0][:, 1:2],
                                                 AF.Copy), "act")
    bank_free[0] = ("act", bco2)
    p.wait("vector", "act", bco2)
    p.fence("vector")
    p.do("vector", lambda E: E.tensor_scalar(
        cd_cols[:, :], sd_cols[:, :], sd_col[:, :], None, ALU.mult), "dve")
    bar2_dve = p.cnt["dve"]

    nc.sbuf_base = _baseB
    qd_bf = [S(f"qd_bf{c}", [P, D], BF16) for c in range(IC)]
    S("qd_pad", [P, 4096], mybir.dt.uint8)
    _baseC1 = nc.sbuf_base
    pe_b_done = p.cnt["pe"]
    p.wait("sync", "pe", pe_b_done)
    p.wait("sync", "gs", prep_gs)
    p.wait("sync", "sph0", p.cnt["sph0"])
    p.wait("sync", "sph1", p.cnt["sph1"])
    p.wait("sync", "lb0", p.cnt["lb0"])
    p.wait("sync", "lb1", p.cnt["lb1"])
    for c in range(IC):
        p.dma("sync", qd_bf[c][:, :], qd_sp[:, c * P:(c + 1) * P], "tqd",
              transpose=True)
    qd_ld = p.cnt["tqd"]

    qca = [S("qca0", [P, 2 * NT, TILE], BF16)]
    qtil = [S(f"qtil{i}", [P, 2 * NT, TILE], BF16) for i in range(2)]
    xnc = [S(f"xnc{i}", [P, DC * TILE], BF16) for i in range(2)]
    rb_f = S("rb_f", [P, TILE], F32)
    rb_f2 = S("rb_f2", [P, TILE], F32)
    rb_b = S("rb_b", [P, 6 * TILE], BF16)
    rbx = [S(f"rbx{i}", [P, TILE], BF16) for i in range(2)]
    attn_b = S("attn_b", [P, DC, TILE], BF16)
    z_b = S("z_b", [P, DC, TILE], BF16)
    zsq = S("zsq", [P, TILE], BF16)
    srow = S("srow", [1, TILE], F32)
    sqrow = S("sqrow", [1, TILE], F32)
    murow = S("murow", [1, TILE], F32)
    varrow = S("varrow", [1, TILE], F32)
    rstdrow = S("rstdrow", [1, TILE], F32)
    mu_bb = S("mu_bb", [1, TILE], BF16)
    rstd_bb = S("rstd_bb", [1, TILE], BF16)
    mub = S("mub", [P, TILE], BF16)
    rstdb = S("rstdb", [P, TILE], BF16)
    ln2_t = S("ln2_t", [P, TILE], F32)
    ln2_u = S("ln2_u", [P, TILE], F32)
    sb_t = [S(f"sb_t{i}", [P, DC, TILE], BF16) for i in range(2)]
    wproj_bf = S("wproj_bf", [P, DC, D], BF16)
    em2h = S("em2h", [P, 2 * NT * P], BF16)
    emv = p.dma("sync", em2h[:, :], em2h_in, "tqd")
    p.wait("gpsimd", "pe", pe_b_done)
    p.wait("gpsimd", "gs", prep_gs)
    p.wait("gpsimd", "sph0", p.cnt["sph0"])
    p.wait("gpsimd", "sph1", p.cnt["sph1"])
    p.wait("gpsimd", "lb0", p.cnt["lb0"])
    p.wait("gpsimd", "lb1", p.cnt["lb1"])
    for _c in range(DC):
        p.dma("gpsimd", wproj_bf[:, _c, :], wproj_t[_c * P:(_c + 1) * P, :],
              "wp")
    wp_ld = p.cnt["wp"]
    p.wait("tensor", "wp", wp_ld)
    p.wait("tensor", "tqd", emv)

    if DEBUG:
        dbg_z = nc.dram_tensor("dbg_z", [NT, P, DC * TILE], BF16,
                               kind="ExternalOutput").ap()
        dbg_z_done = None
    xnc_free = [None, None]
    sbt_free = [None, None]
    rbx_free = [None, None]
    qca_free = [None, None]
    qtil_free = [None, None]
    ssp_v = [None] * NT
    prev_sdve = None

    regA = nc.alloc_register(mybir.EngineType.SP, "rq_ev")
    regB = nc.alloc_register(mybir.EngineType.SP, "rq_od")
    regGC = [nc.alloc_register(mybir.EngineType.SP, f"rq_gc{i}")
             for i in range(4)]
    holder = {}

    def _dynsetup(E):
        pid = E.partition_id()
        E.reg_mov(regA, pid)
        E.reg_alu(regA, regA, 6, ALU.bitwise_and)
        E.reg_alu(regA, regA, P, ALU.mult)
        E.reg_mov(regB, pid)
        E.reg_alu(regB, regB, 1, ALU.bitwise_or)
        E.reg_alu(regB, regB, P, ALU.mult)
        holder["ev"] = E.snap(regA, min_val=0, max_val=(N_CORES - 2) * P)
        holder["od"] = E.snap(regB, min_val=0, max_val=(N_CORES - 1) * P)
        for cc_ in range(4):
            rg = regGC[cc_]
            E.reg_mov(rg, pid)
            E.reg_alu(rg, rg, 1, ALU.bitwise_and)
            E.reg_alu(rg, rg, 4, ALU.mult)
            if cc_ > 0:
                E.reg_alu(rg, rg, cc_, ALU.add)
            holder[f"gc{cc_}"] = E.snap(rg, min_val=0, max_val=DC - 1)

    p.do("sync", _dynsetup)
    p.wait("sync", "cc", ccq)

    for c in range(4):
        parc = c % 2
        qci = 0
        p.waitv("sync", qca_free[qci])
        qx4 = qx_gth.rearrange("r (i g t) -> r i g t", i=NT, g=DC)
        for ki, key in enumerate(("ev", "od")):
            qcav = p.do(
                "sync",
                lambda E, ki=ki, key=key, qci=qci, c=c: E.dma_start(
                    out=qca[qci][:, ki * NT:(ki + 1) * NT, :],
                    in_=qx4[bass.ds(holder[key], P), 0:NT,
                            bass.ds(holder[f"gc{c}"], 1), :]),
                f"lq{parc}", 16)
        p.wait("vector", f"lq{parc}", qcav)
        p.wait("tensor", f"lq{parc}", qcav)
        if c == 0:
            p.wait("tensor", "dve", bar1_dve)
        p.waitv("tensor", qtil_free[parc])
        for b in range(6):
            p.waitv("tensor", bank_free[b])
            p.do("tensor", lambda E, b=b: E.matmul(
                pb[b][:, :], zero_rb[0:1, :], bkv_sb[0:1, 0:TILE],
                start=True, stop=False))
            for k in range(3):
                g = 3 * b + k
                if g >= 2 * NT:
                    break
                p.do("tensor", lambda E, b=b, c=c, g=g, k=k: E.matmul(
                    pb[b][32 * k:32 * k + 2, :], k1m[:, c, :],
                    qca[qci][:, g, :], start=False, stop=False))
            qk1v = p.do("tensor", lambda E, b=b: E.matmul(
                pb[b][:, :], zero_rb[0:1, :], bkv_sb[0:1, 0:TILE],
                start=False, stop=True), "pe")
            p.wait("vector", "pe", qk1v)
            p.fence("vector")
            p.seq("vector", lambda E, b=b: E.tensor_scalar(
                rb_f[:, :], pb[b][:, :], 1e-6, None, ALU.max))
            p.fence("vector")
            p.seq("vector", lambda E: E.reciprocal(rb_f2[:, :], rb_f[:, :]))
            p.fence("vector")
            rbv = p.do("vector", lambda E, b=b: E.tensor_copy(
                rb_b[:, b * TILE:(b + 1) * TILE], rb_f2[:, :]), "dve")
            bank_free[b] = ("dve", rbv)
        p.wait("tensor", "dve", p.cnt["dve"])
        for g in range(2 * NT):
            b1 = 4 + (g % 2)
            p.waitv("tensor", bank_free[b1])
            rbm = p.do("tensor", lambda E, b=b1, g=g: E.matmul(
                pb[b][:, :], em2h[:, g * P:(g + 1) * P],
                rb_b[:, (g // 3) * TILE:(g // 3 + 1) * TILE],
                start=True, stop=True), "pe")
            rset = g % 2
            p.wait("scalar", "pe", rbm)
            p.waitv("scalar", rbx_free[rset])
            av = p.do("scalar", lambda E, b=b1, rs=rset: E.activation(
                rbx[rs][:, :], pb[b][:, :], AF.Copy), "act")
            bank_free[b1] = ("act", av)
            p.wait("vector", "act", av)
            qtv = p.do("vector", lambda E, parc=parc, g=g, rs=rset:
                       E.tensor_tensor(qtil[parc][:, g, :],
                                       qca[qci][:, g, :], rbx[rs][:, :],
                                       ALU.mult), "dve")
            rbx_free[rset] = ("dve", qtv)
        qt_dve = p.cnt["dve"]
        qca_free[qci] = ("dve", qt_dve)
        p.wait("tensor", "dve", qt_dve)

        for hh in range(2):
            bh = 2 * c + hh
            par = bh % 2
            p.waitv("sync", xnc_free[par])
            p.waitv("sync", xnf_v[bh])
            xv = p.dma("sync", xnc[par][:, :], xnf_sp[bh, :, :], f"lxn{par}")
            for cc in range(DC):
                bq_ = 2 + (cc % 2)
                p.waitv("tensor", bank_free[bq_])
                for u in range(2):
                    jj = 2 * cc + u
                    p.do("tensor", lambda E, b=bq_, c=c, hh=hh, u=u, jj=jj,
                         parc=parc: E.matmul(
                        pb[b][u * 64:(u + 1) * 64, :],
                        kv_bf[hh * 64:(hh + 1) * 64, c, 0:64],
                        qtil[parc][hh * 64:(hh + 1) * 64, :, jj::16],
                        start=True, stop=True),
                        "pe" if u == 1 else None)
                qov = p.cnt["pe"]
                p.wait("vector", "pe", qov)
                p.fence("vector")
                dv = p.do("vector", lambda E, b=bq_, cc=cc: E.tensor_copy(
                    attn_b[:, cc, :], pb[b][:, :]), "dve")
                bank_free[bq_] = ("dve", dv)
            attn_dve = p.cnt["dve"]
            p.wait("tensor", "dve", attn_dve)
            for cc2 in range(DC):
                bz = 6 + (cc2 % 2)
                p.waitv("tensor", bank_free[bz])
                for cc in range(DC):
                    p.do("tensor", lambda E, b=bz, c2=cc2, cc=cc: E.matmul(
                        pb[b][:, :], wproj_bf[:, cc, c2 * P:(c2 + 1) * P],
                        attn_b[:, cc, :], start=(cc == 0),
                        stop=(cc == DC - 1)),
                        "pe" if cc == DC - 1 else None)
                zv = p.cnt["pe"]
                p.wait("vector", "pe", zv)
                if cc2 == 0 and prev_sdve is not None:
                    p.wait("vector", "dve", prev_sdve)
                if DEBUG and cc2 == 0 and dbg_z_done is not None:
                    p.wait("vector", "gb", dbg_z_done)
                av = p.do("vector", lambda E, b=bz, c2=cc2: E.tensor_scalar(
                    z_b[:, c2, :], pb[b][:, :], bp_sb[:, c2:c2 + 1], None,
                    ALU.add), "dve")
                bank_free[bz] = ("dve", av)
            z_act = p.cnt["dve"]
            bsu, bsq = (bh % 2), 2 + (bh % 2)
            p.wait("tensor", "dve", z_act)
            p.waitv("tensor", bank_free[bsu])
            for cc in range(DC):
                p.do("tensor", lambda E, b=bsu, cc=cc: E.matmul(
                    pb[b][0:1, :], ones_cb[:, :], z_b[:, cc, :],
                    start=(cc == 0), stop=(cc == DC - 1)),
                    "pe" if cc == DC - 1 else None)
            p.waitv("tensor", bank_free[bsq])
            sqmv_prev_pe = None
            p.wait("scalar", "dve", z_act)
            for cc in range(DC):
                if sqmv_prev_pe is not None:
                    p.wait("scalar", "pe", sqmv_prev_pe)
                sqv = p.do("scalar", lambda E, cc=cc: E.activation(
                    zsq[:, :], z_b[:, cc, :], AF.Square), "act")
                p.wait("tensor", "act", sqv)
                sqmv_prev_pe = p.do("tensor", lambda E, b=bsq, cc=cc:
                                    E.matmul(pb[b][0:1, :], ones_cb[:, :],
                                             zsq[:, :], start=(cc == 0),
                                             stop=(cc == DC - 1)), "pe")
            sqmv = p.cnt["pe"]
            p.wait("vector", "pe", sqmv)
            p.seq("vector", lambda E, b=bsu: E.tensor_scalar(
                murow[0:1, :], pb[b][0:1, :], 1.0 / D, None, ALU.mult))
            p.seq("vector", lambda E, b=bsq: E.tensor_scalar(
                varrow[0:1, :], pb[b][0:1, :], 1.0 / D, LN_EPS, ALU.mult,
                ALU.add))
            p.fence("vector")
            p.seq("vector", lambda E: E.tensor_tensor(
                srow[0:1, :], murow[0:1, :], murow[0:1, :], ALU.mult))
            p.fence("vector")
            muv = p.do("vector", lambda E: E.tensor_sub(
                varrow[0:1, :], varrow[0:1, :], srow[0:1, :]), "dve")
            bank_free[bsu] = ("dve", muv)
            bank_free[bsq] = ("dve", muv)
            p.wait("scalar", "dve", muv)
            p.seq("scalar", lambda E: E.activation(
                sqrow[0:1, :], varrow[0:1, :], AF.Ln))
            p.fence("scalar")
            rsv = p.do("scalar", lambda E: E.activation(
                rstdrow[0:1, :], sqrow[0:1, :], AF.Exp, scale=-0.5), "act")
            p.wait("vector", "act", rsv)
            p.fence("vector")
            p.do("vector", lambda E: E.tensor_copy(mu_bb[0:1, :],
                                                   murow[0:1, :]))
            mbv = p.do("vector", lambda E: E.tensor_copy(
                rstd_bb[0:1, :], rstdrow[0:1, :]), "dve")
            p.wait("tensor", "dve", mbv)
            p.waitv("tensor", bank_free[bsu])
            p.waitv("tensor", bank_free[bsq])
            p.do("tensor", lambda E, b=bsu: E.matmul(
                pb[b][:, :], ones_rb[0:1, :], mu_bb[0:1, :], start=True,
                stop=True))
            bb2 = p.do("tensor", lambda E, b=bsq: E.matmul(
                pb[b][:, :], ones_rb[0:1, :], rstd_bb[0:1, :], start=True,
                stop=True), "pe")
            p.wait("scalar", "pe", bb2)
            p.do("scalar", lambda E, b=bsu: E.activation(
                mub[:, :], pb[b][:, :], AF.Copy))
            bb3 = p.do("scalar", lambda E, b=bsq: E.activation(
                rstdb[:, :], pb[b][:, :], AF.Copy), "act")
            bank_free[bsu] = ("act", bb3)
            bank_free[bsq] = ("act", bb3)
            p.wait("vector", "act", bb3)
            p.wait("vector", f"lxn{par}", xv)
            p.waitv("vector", sbt_free[par])
            p.fence("vector")
            p.seq("vector", lambda E: E.scalar_tensor_tensor(
                mub[:, :], mub[:, :], -1.0, rstdb[:, :], ALU.mult, ALU.mult))
            p.fence("vector")
            for cc in range(DC):
                p.seq("vector", lambda E, cc=cc: E.scalar_tensor_tensor(
                    ln2_t[:, :], z_b[:, cc, :], gam_sb[:, cc:cc + 1],
                    rstdb[:, :], ALU.mult, ALU.mult))
                p.fence("vector")
                p.seq("vector", lambda E, cc=cc: E.scalar_tensor_tensor(
                    ln2_u[:, :], mub[:, :], gam_sb[:, cc:cc + 1],
                    ln2_t[:, :], ALU.mult, ALU.add))
                p.fence("vector")
                p.seq("vector", lambda E, cc=cc, par=par:
                      E.scalar_tensor_tensor(
                          sb_t[par][:, cc, :], ln2_u[:, :],
                          bet_sb[:, cc:cc + 1],
                          xnc[par][:, cc * TILE:(cc + 1) * TILE],
                          ALU.add, ALU.add))
                p.fence("vector")
            sdve = p.cnt["dve"]
            if DEBUG:
                p.wait("gpsimd", "dve", sdve)
                dbg_z_done = p.dma("gpsimd", dbg_z[bh, :, :], z_b[:, :, :],
                                   "gb")
            prev_sdve = sdve
            xnc_free[par] = ("dve", sdve)
            p.wait("scalar", "dve", sdve)
            ssp_v[bh] = (f"sps{par}",
                         p.dma("scalar", s_sp[bh, :, :], sb_t[par][:, :, :],
                               f"sps{par}"))
            sbt_free[par] = ssp_v[bh]
        qtil_free[parc] = ("pe", p.cnt["pe"])

    nc.sbuf_base = _baseC1
    hc = [S(f"hc{i}", [P, IC * TILE], BF16) for i in range(2)]
    qf2_scr = [S("qf2_scr0", [P, TILE], F32)]
    qi2_scr = [S("qi2_scr0", [P, TILE], I32)]
    ihb = [S(f"ihb{j}", [P, IC, TILE], BF16) for j in range(2)]
    sc_t = [S(f"sc_t{i}", [P, DC * TILE], BF16) for i in range(2)]
    g_scr = S("g_scr", [P, TILE], BF16)
    p_scr = [S(f"p_scr{i}", [P, TILE], BF16) for i in range(2)]
    ptm = []
    for _i in range(2):
        ptm.append(S(f"ptm{_i}", [P, NCH, 2, TILE], BF16))
        S(f"ptm_pad{_i}", [P, 4096], mybir.dt.uint8)
    xc2 = [S(f"xc2{i}", [P, D], F32) for i in range(2)]
    yst = [S(f"yst{i}", [P, D], F32) for i in range(2)]

    hc_free = [None, None]
    sct_free = [None, None]
    pscr_free = [None, None]
    ptm_free = [None, None]
    xc2_free = [None] * 2
    yst_free = [None, None]
    gscr_free = None
    qfC_free = [None, None]
    qiC_free = [None, None]
    ihb_guard = [None, None]
    spp_rec = [None] * NT

    def c2_assembly(k):
        par = k % 2
        t0 = k * TILE
        p.waitv("sync", ptm_free[par])
        p.wait("sync", "spp0", spp_rec[k][0])
        p.wait("sync", "spp1", spp_rec[k][1])
        ptv = None
        for j in range(NCH):
            for eh in range(2):
                ptv = p.dma("sync", ptm[par][:, j, eh, :],
                            p_sp[eh * TILE:(eh + 1) * TILE,
                                 t0 + j * P: t0 + (j + 1) * P],
                            f"tpt{par}", transpose=True)
        yv = None
        for j in range(NCH):
            slot = j % 2
            p.waitv("sync", xc2_free[slot])
            xvv = p.dma("sync", xc2[slot][:, :],
                        x_in[t0 + j * P: t0 + (j + 1) * P, :], f"lx{slot}")
            yb = j % 2
            p.waitv("vector", yst_free[yb])
            p.wait("vector", f"lx{slot}", xvv)
            p.wait("vector", f"tpt{par}", ptv)
            yv = p.do("vector", lambda E, s=slot, par=par, j=j, yb=yb:
                      E.tensor_tensor(yst[yb][:, :], xc2[s][:, :],
                                      ptm[par][:, j, :, :], ALU.add), "dve")
            xc2_free[slot] = ("dve", yv)
            p.wait("scalar", "dve", yv)
            ov = p.dma("scalar", y_out[t0 + j * P: t0 + (j + 1) * P, :],
                       yst[yb][:, :], f"spy{yb}")
            yst_free[yb] = (f"spy{yb}", ov)
        ptm_free[par] = ("dve", yv)
    pe_down_last = None

    for i in range(NT):
        par = i % 2
        t0 = i * TILE
        if i == 0:
            p.wait("sync", "sps0", p.cnt["sps0"])
            p.wait("sync", "sps1", p.cnt["sps1"])
        p.waitv("sync", hc_free[par])
        p.waitv("sync", hsp_v[i])
        hv = p.dma("sync", hc[par][:, :], h_sp[i, :, :], f"lh{par}")
        p.waitv("sync", sct_free[par])
        p.waitv("sync", ssp_v[i])
        sv_ = p.dma("sync", sc_t[par][:, :], s_sp[i, :, :], f"ls{par}")
        p.wait("vector", f"lh{par}", hv)
        if i == 0:
            p.wait("vector", "dve", bar2_dve)
        for r in range(IC):
            s = 0
            p.waitv("vector", qfC_free[s])
            mv = p.do("vector", lambda E, par=par, r=r, s=s: E.tensor_scalar(
                qf2_scr[s][:, :], hc[par][:, r * TILE:(r + 1) * TILE],
                rd_col[:, :], None, ALU.mult), "dve")
            p.wait("scalar", "dve", mv)
            p.waitv("scalar", qiC_free[s])
            av = p.do("scalar", lambda E, s=s: E.activation(
                qi2_scr[s][:, :], qf2_scr[s][:, :], AF.Copy), "act")
            qfC_free[s] = ("act", av)
            p.wait("gpsimd", "act", av)
            if r == 0:
                p.waitv("gpsimd", ihb_guard[par])
            cv = p.do("gpsimd", lambda E, r=r, s=s, par=par: E.tensor_scalar(
                ihb[par][:, r, :], qi2_scr[s][:, :], 127.0, -128.0, ALU.min,
                ALU.max), "pool")
            qiC_free[s] = ("pool", cv)
        ih_pool = p.cnt["pool"]
        hc_free[par] = ("dve", p.cnt["dve"])
        p.wait("tensor", "pool", ih_pool)
        if i == 0:
            p.wait("tensor", "tqd", qd_ld)
        for c in range(DC):
            bz = 4 + (c % 4)
            p.waitv("tensor", bank_free[bz])
            for rr in range(IC):
                p.do("tensor", lambda E, b=bz, c=c, rr=rr, par=par: E.matmul(
                    pb[b][:, :], qd_bf[rr][:, c * P:(c + 1) * P],
                    ihb[par][:, rr, :], start=(rr == 0), stop=(rr == IC - 1)),
                    "pe" if rr == IC - 1 else None)
            dvn = p.cnt["pe"]
            if c == DC - 1:
                pe_down_last = dvn
                ihb_guard[par] = ("pe", dvn)
            p.wait("scalar", "pe", dvn)
            if gscr_free is not None:
                p.waitv("scalar", gscr_free)
            av = p.do("scalar", lambda E, b=bz, c=c: E.activation(
                g_scr[:, :], pb[b][:, :], AF.Silu,
                scale=cd_cols[:, c:c + 1]), "act")
            bank_free[bz] = ("act", av)
            ps = c % 2
            p.wait("vector", "act", av)
            p.wait("vector", f"ls{par}", sv_)
            p.waitv("vector", pscr_free[ps])
            pv = p.do("vector", lambda E, par=par, c=c, ps=ps: E.tensor_tensor(
                p_scr[ps][:, :], g_scr[:, :],
                sc_t[par][:, c * TILE:(c + 1) * TILE], ALU.mult), "dve")
            gscr_free = ("dve", pv)
            if c == DC - 1:
                sct_free[par] = ("dve", pv)
            p.wait("scalar", "dve", pv)
            pspv = p.dma("scalar", p_sp[c * P:(c + 1) * P, t0:t0 + TILE],
                         p_scr[ps][:, :], f"spp{ps}")
            pscr_free[ps] = (f"spp{ps}", pspv)
        spp_rec[i] = (p.cnt["spp0"], p.cnt["spp1"])
        if i > 0:
            c2_assembly(i - 1)
    c2_assembly(NT - 1)

    p.wait("scalar", "spy0", p.cnt["spy0"])
    p.wait("scalar", "spy1", p.cnt["spy1"])
    p.wait("gpsimd", "spy0", p.cnt["spy0"])
    p.wait("gpsimd", "spy1", p.cnt["spy1"])

    if DEBUG:
        dbg_can = nc.dram_tensor("dbg_canary", [P, 1024], F32,
                                 kind="ExternalOutput").ap()
        p.dma("gpsimd", dbg_can, canary[:, :], "gb")
        dbg = [
            ("qgsp", qg_sp, [IFF, D], BF16),
            ("qusp", qu_sp, [IFF, D], BF16),
            ("qdsp", qd_sp, [D, IFF], BF16),
            ("xntm", xn_tm_sp, [TOK, D], BF16),
            ("xnf", xnf_sp, [NT, P, DC * TILE], BF16),
            ("q", q_sp, [P, NT, DC * TILE], BF16),
            ("kvp", kvk1_part, [P, 8 * 65], F32),
            ("h", h_sp, [NT, P, IC * TILE], BF16),
            ("s", s_sp, [NT, P, DC * TILE], BF16),
        ]
        for name, src_, shape, dt in dbg:
            dd = nc.dram_tensor("dbg_" + name, shape, dt,
                                kind="ExternalOutput").ap()
            p.dma("gpsimd", dd, src_, "gb")
        dsc = nc.dram_tensor("dbg_scal", [1, 16], F32,
                             kind="ExternalOutput").ap()
        p.dma("gpsimd", dsc, scal[0:1, :], "gb")
        p.wait("gpsimd", "gb", p.cnt["gb"])

    p.emit()
    return nc


def _consts():
    ones_rb = np.ones((1, P), dtype=ml_dtypes.bfloat16)
    ones_cb = np.ones((P, 1), dtype=ml_dtypes.bfloat16)
    ones_rf = np.ones((1, P), dtype=np.float32)
    em2p = np.zeros((P, 2), dtype=ml_dtypes.bfloat16)
    em2h = np.zeros((P, 16 * P), dtype=ml_dtypes.bfloat16)
    for pp in range(P):
        em2p[pp, pp // 64] = 1.0
    for g in range(16):
        for pp in range(P):
            em2h[32 * (g % 3) + pp // 64, g * P + pp] = 1.0
    return ones_rb, ones_cb, ones_rf, em2p, em2h


def kernel(L0_attn_H, L0_attn_W_qkv, L0_attn_b_qkv, L0_attn_W_proj,
           L0_attn_b_proj, L0_attn_gamma, L0_attn_beta, L0_ffn_clip_x,
           L0_ffn_clip_down, L0_ffn_w_gate, L0_ffn_w_up, L0_ffn_w_down):
    x = np.ascontiguousarray(np.asarray(L0_attn_H)).reshape(4 * 8192, D)
    wqkv_t = np.ascontiguousarray(np.asarray(L0_attn_W_qkv).T)
    wproj_t = np.ascontiguousarray(np.asarray(L0_attn_W_proj).T)
    wg = np.ascontiguousarray(np.asarray(L0_ffn_w_gate))
    wu = np.ascontiguousarray(np.asarray(L0_ffn_w_up))
    wd = np.ascontiguousarray(np.asarray(L0_ffn_w_down))
    bqkv = np.asarray(L0_attn_b_qkv)
    bq_cols = np.ascontiguousarray(bqkv[0:D].reshape(DC, P).T)
    bkv_row = np.ascontiguousarray(bqkv[D:3 * D].reshape(1, 2 * D))
    bp_cols = np.ascontiguousarray(np.asarray(L0_attn_b_proj).reshape(DC, P).T)
    gam_cols = np.ascontiguousarray(np.asarray(L0_attn_gamma).reshape(DC, P).T)
    bet_cols = np.ascontiguousarray(np.asarray(L0_attn_beta).reshape(DC, P).T)
    clip = np.zeros((1, 2), dtype=np.float32)
    clip[0, 0] = np.asarray(L0_ffn_clip_x).reshape(-1)[0]
    clip[0, 1] = np.asarray(L0_ffn_clip_down).reshape(-1)[0]
    ones_rb, ones_cb, ones_rf, em2p, em2h = _consts()

    nc = build()
    in_maps = []
    for c in range(N_CORES):
        b, half = c // 2, c % 2
        r0 = b * 8192 + half * TOK
        pmask = np.zeros((P, N_CORES), dtype=np.float32)
        pmask[:, 2 * b] = 1.0
        pmask[:, 2 * b + 1] = 1.0
        sel2 = np.zeros((P, 2), dtype=np.float32)
        sel2[:, half] = 1.0
        in_maps.append({
            "x": np.ascontiguousarray(x[r0:r0 + TOK, :]),
            "wqkv_t": wqkv_t, "wproj_t": wproj_t,
            "wg_nat": wg, "wu_nat": wu, "wd_nat": wd,
            "bq_in": bq_cols, "bkv_in": bkv_row, "bp_in": bp_cols,
            "gam_in": gam_cols, "bet_in": bet_cols, "clip_in": clip,
            "ones_rb_in": ones_rb, "ones_cb_in": ones_cb,
            "ones_rf_in": ones_rf, "em2p_in": em2p, "em2h_in": em2h,
            "sel2_in": sel2, "pmask_in": pmask,
        })
    res = run_bass_kernel_spmd(nc, in_maps, core_ids=list(range(N_CORES)))
    out = np.concatenate([res.results[c]["y"] for c in range(N_CORES)],
                         axis=0)
    return out.reshape(4, 8192, D)
